# version 16
# baseline (speedup 1.0000x reference)
import numpy as np
import ml_dtypes
from contextlib import ExitStack

import concourse.bass as bass
import concourse.mybir as mybir
from concourse.bass_utils import run_bass_kernel_spmd
from concourse.library_overlay import lower_extended_insts

F32 = mybir.dt.float32
BF16 = mybir.dt.bfloat16
AF = mybir.ActivationFunctionType
ALU = mybir.AluOpType
AX = mybir.AxisListType
BFNP = ml_dtypes.bfloat16

B = 2
N = 4096
D = 1024
H = 16
DH = 64
WIN = 256
DI = 2730
NL = 6
NCORES = 8

N_LAYERS_BUILD = 6
LAST_PROG = None
DEBUG_TAPS = ()


def ts(i, sz):
    return slice(i * sz, (i + 1) * sz)


def host_consts():
    c = {}
    c["identf"] = np.eye(128, dtype=np.float32)
    c["identb"] = np.eye(128, dtype=BFNP)
    c["ones_col"] = np.ones((128, 1), dtype=BFNP)
    c["ones_row"] = np.ones((1, 128), dtype=BFNP)
    half = np.arange(0, DH, 2, dtype=np.float64)
    inv_freq = 1.0 / 10000.0 ** (half / DH)
    t = np.arange(N, dtype=np.float64)
    freqs = t[:, None] * inv_freq[None, :]
    freqs = np.concatenate([freqs, freqs], -1)
    base_scale = (half + 0.4 * DH) / (1.4 * DH)
    power = (t - N // 2) / (WIN // 2)
    scale = base_scale[None, :] ** power[:, None]
    scale = np.concatenate([scale, scale], -1)
    cos, sin = np.cos(freqs), np.sin(freqs)
    sign = np.concatenate([-np.ones(32), np.ones(32)])[None, :]
    c["tab_cq"] = (8.0 * cos * scale).astype(BFNP)
    c["tab_sq"] = (8.0 * sin * scale * sign).astype(BFNP)
    c["tab_ck"] = (cos / scale).astype(BFNP)
    c["tab_sk"] = (sin / scale * sign).astype(BFNP)
    i = np.arange(128, dtype=np.float32)
    mv = np.zeros((128, 6), dtype=np.float32)
    mv[:, 0] = i
    mv[:, 1] = i + 257
    mv[:, 2] = 0.0
    mv[:, 3] = i + 1
    mv[:, 4] = 0.0
    mv[:, 5] = i + 129
    c["maskv"] = mv
    ind = np.zeros((8, 128, 16), dtype=BFNP)
    for cc in range(8):
        for j in range(2):
            ind[cc, j * 64:(j + 1) * 64, 2 * cc + j] = 1.0
    c["indic"] = ind
    return c


def host_layouts(inputs):
    lay = {}
    b1 = np.asarray(inputs["L0_ffn_b1"], np.float32)
    cols = np.zeros((NL, 128, 44), dtype=np.float32)
    for j in range(22):
        for half in range(2):
            lo = half * 2730 + j * 128
            w = min(128, half * 2730 + 2730 - lo)
            cols[:, :w, half * 22 + j] = b1[:, lo:lo + w]
    lay["b1cols"] = np.ascontiguousarray(cols)
    for name, key in [("b2cols", "L0_ffn_b2"), ("gcols", "L0_ffn_ln_g"),
                      ("bcols", "L0_ffn_ln_b")]:
        v = np.asarray(inputs[key], np.float32)
        lay[name] = np.ascontiguousarray(v.reshape(NL, 8, 128).transpose(0, 2, 1))
    return lay


class Buf:
    __slots__ = ("name", "w", "r", "psum")

    def __init__(self, name="", psum=False):
        self.name = name
        self.w = None
        self.r = []
        self.psum = psum


class Op:
    __slots__ = ("eng", "fns", "deps", "need", "val", "idx", "skey", "dma")

    def __init__(self, eng, fns, deps, idx):
        self.eng = eng
        self.fns = fns
        self.deps = deps
        self.need = False
        self.val = None
        self.idx = idx
        self.dma = False


class Prog:
    ENGS = ("T", "V", "S", "Y", "G")

    NLANES = 8

    def __init__(self):
        self.ops = []
        self._bar = []
        self._bar_seen = set()
        self._lane_rr = {"Y": 0, "G": 0}
        self._lane_ctr = {}
        self._last_stream = {}

    def emit(self, eng, fns, reads=(), writes=(), deps=(), gc=False):
        if not isinstance(fns, (list, tuple)):
            fns = [fns]
        dset = []
        for b in reads:
            if b.w is not None:
                dset.append(b.w)
            if b.psum:
                dset.extend(o for o in b.r if o.eng != eng)
        for b in writes:
            dset.extend(b.r)
            if b.w is not None:
                dset.append(b.w)
        for d in deps:
            if d is not None:
                dset.append(d)
        if eng not in self._bar_seen:
            dset.extend(self._bar)
            self._bar_seen.add(eng)
        op = Op(eng, list(fns), dset, len(self.ops))
        if eng in ("Y", "G") and not gc:
            op.dma = True
            lane = self._lane_rr[eng]
            self._lane_rr[eng] = (lane + 1) % self.NLANES
            op.skey = (eng, lane)
            self._lane_ctr[op.skey] = self._lane_ctr.get(op.skey, 0) + 16
            op.val = self._lane_ctr[op.skey]
            op.need = True
            self._last_stream[op.skey] = op
        else:
            op.skey = None
            self._last_stream[("P" if eng == "G" else eng, 0)] = op
        self.ops.append(op)
        for b in reads:
            b.r.append(op)
        for b in writes:
            b.w = op
            b.r = []
        return op

    def barrier(self):
        self._bar = list(self._last_stream.values())
        self._bar_seen = set()

    def finalize(self, nc, st):
        for op in self.ops:
            for d in op.deps:
                d.need = True
        skey_ctr = dict(self._lane_ctr)
        for op in self.ops:
            if op.skey is not None:
                continue
            if op.need:
                op.skey = ("P" if op.eng == "G" else op.eng, 0)
                skey_ctr[op.skey] = skey_ctr.get(op.skey, 0) + 1
                op.val = skey_ctr[op.skey]
        sems = {k: st.enter_context(
            nc.semaphore(f"sem{k[0]}{k[1]}")) for k in skey_ctr}
        per_eng = {e: [o for o in self.ops if o.eng == e] for e in self.ENGS}
        total = dict(skey_ctr)
        block = st.enter_context(nc.Block())

        def run(eng_name, handle):
            waited = {k: 0 for k in sems}
            for op in per_eng[eng_name]:
                tgt = {}
                for d in op.deps:
                    if d.val is None:
                        continue
                    if d.val > tgt.get(d.skey, 0):
                        tgt[d.skey] = d.val
                if op.dma:
                    prev = op.val - 16
                    if prev > tgt.get(op.skey, 0):
                        tgt[op.skey] = prev
                for k, v in sorted(tgt.items()):
                    if v > waited[k]:
                        handle.wait_ge(sems[k], v)
                        waited[k] = v
                instr = None
                for fn in op.fns:
                    instr = fn(handle)
                if op.need or op.dma:
                    instr.then_inc(sems[op.skey], 16 if op.dma else 1)
            if eng_name == "G":
                for k in sorted(total):
                    if k[0] in ("Y", "G"):
                        handle.wait_ge(sems[k], total[k])

        @block.sync
        def _(h):
            run("Y", h)

        @block.vector
        def _(h):
            run("V", h)

        @block.scalar
        def _(h):
            run("S", h)

        @block.tensor
        def _(h):
            run("T", h)

        @block.gpsimd
        def _(h):
            run("G", h)

        return total


def build_program(n_layers=N_LAYERS_BUILD, taps=DEBUG_TAPS, phases="ACD"):
    import os
    SUBA = int(os.environ.get("SUBA", "4"))
    nc = bass.Bass("TRN2", target_bir_lowering=False)
    consts = host_consts()

    din = {}
    din["x"] = nc.dram_tensor("x", [B, N, D], F32, kind="ExternalInput").ap()
    for nm, sh in [("Wqkv", [NL, D, 3 * D]), ("Wo", [NL, D, D]),
                   ("W1", [NL, D, 5460]), ("W2", [NL, DI, D]),
                   ("qscale", [NL, DH]), ("kscale", [NL, DH]),
                   ("b1cols", [NL, 128, 44]), ("b2cols", [NL, 128, 8]),
                   ("gcols", [NL, 128, 8]), ("bcols", [NL, 128, 8])]:
        din[nm] = nc.dram_tensor(nm, sh, F32, kind="ExternalInput").ap()
    for nm in ["identf", "maskv"]:
        din[nm] = nc.dram_tensor(nm, list(consts[nm].shape), F32,
                                 kind="ExternalInput").ap()
    for nm in ["identb", "ones_col", "ones_row", "tab_cq", "tab_sq",
               "tab_ck", "tab_sk", "indic"]:
        din[nm] = nc.dram_tensor(nm, list(consts[nm].shape), BF16,
                                 kind="ExternalInput").ap()
    d_out = nc.dram_tensor("out", [B, N, D], F32, kind="ExternalOutput").ap()

    def dscratch(name, shape, dt):
        kind = "ExternalOutput" if name in taps else "Internal"
        return nc.dram_tensor(name, shape, dt, kind=kind).ap()

    h_resid = dscratch("resid", [B, D, N], F32)
    h_xnT = dscratch("xnT", [B, D, N], BF16)
    h_s = dscratch("s", [B, D, N], F32)
    h_g = dscratch("g", [B, D, N], F32)

    p = Prog()
    st = ExitStack()
    with st:
        sb = lambda nm, sh, dt: st.enter_context(nc.sbuf_tensor(nm, sh, dt))

        c_idb = sb("c_idb", [128, 128], BF16)
        c_ones_col = sb("c_ones_col", [128, 1], BF16)
        c_ones_row = sb("c_ones_row", [1, 128], BF16)
        c_maskv = sb("c_maskv", [128, 6], F32)
        c_indic = sb("c_indic", [128, 8 * 16], BF16)
        g_negmu = sb("g_negmu", [128, B * 8], F32)

        CONST = Buf("const")
        b_negmu = Buf("negmu")
        NTt = N // 128
        bh_resid = [[Buf(f"h_resid{b}_{t}") for t in range(NTt)]
                    for b in range(B)]
        bh_xnT = [[Buf(f"h_xnT{b}_{t}") for t in range(NTt)] for b in range(B)]
        bh_s = [Buf(f"h_s{b}") for b in range(B)]
        bh_g = [[Buf(f"h_g{b}_{t}") for t in range(NTt)] for b in range(B)]

        pq_t = ([st.enter_context(nc.psum_tensor("pq01", [128, 1024], F32))]
                + [st.enter_context(nc.psum_tensor("pq23", [128, 1024], F32))]
                + [st.enter_context(nc.psum_tensor("pq45", [128, 1024], F32))]
                + [st.enter_context(nc.psum_tensor(f"pq{i}", [128, 512], F32))
                   for i in range(6, 8)])
        pq01 = pq_t[0][:]
        pq23 = pq_t[1][:]
        pq45 = pq_t[2][:]
        pq = ([pq01[:, 0:512], pq01[:, 512:1024]]
              + [pq23[:, 0:512], pq23[:, 512:1024]]
              + [pq45[:, 0:512], pq45[:, 512:1024]]
              + [pq_t[3][:], pq_t[4][:]])
        pqb = [pq[i].bitcast(BF16) for i in range(8)]
        b_pq = [Buf(f"pq{i}", psum=True) for i in range(8)]

        def load_const(dst_ap, src_ap):
            p.emit("Y", lambda h, d=dst_ap, s=src_ap: h.dma_start(out=d, in_=s),
                   writes=[CONST])

        load_const(c_idb[:], din["identb"])
        load_const(c_ones_col[:], din["ones_col"])
        load_const(c_ones_row[:], din["ones_row"])
        load_const(c_maskv[:], din["maskv"])
        load_const(c_indic[:].rearrange("p (c j) -> p c j", c=8),
                   din["indic"].rearrange("c p j -> p c j"))

        stIdf = ExitStack()
        c_idf = stIdf.enter_context(nc.sbuf_tensor("c_idf", [128, 128], F32))
        load_const(c_idf[:], din["identf"])
        stA = ExitStack()
        sbA = lambda nm, sh, dt: stA.enter_context(nc.sbuf_tensor(nm, sh, dt))
        a_x = [sbA(f"a_x{i}", [128, D], F32) for i in range(3)]
        a_stat = [sbA(f"a_stat{i}", [128, 8], F32) for i in range(2)]
        a_junk = sbA("a_junk", [128, D], BF16)
        a_xnf = [sbA(f"a_xnf{i}", [128, D], F32) for i in range(2)]
        a_xnb = [sbA(f"a_xnb{i}", [128, D], BF16) for i in range(2)]
        a_xtf = [sbA(f"a_xtf{i}", [128, D], F32) for i in range(2)]
        a_xtb = [sbA(f"a_xtb{i}", [128, D], BF16) for i in range(2)]
        a_murow = sbA("a_murow", [1, D], F32)
        a_mub = sbA("a_mub", [1, D], BF16)
        b_ax = [Buf(f"a_x{i}") for i in range(3)]
        b_st = [Buf(f"a_stat{i}") for i in range(2)]
        b_junk = Buf("a_junk")
        b_xnf = [Buf(f"a_xnf{i}") for i in range(2)]
        b_xnb = [Buf(f"a_xnb{i}") for i in range(2)]
        b_xtf = [Buf(f"a_xtf{i}") for i in range(2)]
        b_xtb = [Buf(f"a_xtb{i}") for i in range(2)]
        b_mur = Buf("a_murow")

        NT = N // 128
        for b in range(B):
            for t in range(NT):
                u = b * NT + t
                xa, bxa = a_x[u % 3], b_ax[u % 3]
                stt, bst = a_stat[u % 2], b_st[u % 2]
                p.emit("Y", lambda h, xa=xa, b=b, t=t: h.dma_start(
                    out=xa[:], in_=din["x"][b, ts(t, 128), :]),
                    writes=[bxa])
                p.emit("V", lambda h, stt=stt, xa=xa: h.reduce_sum(
                    stt[:, 0:1], xa[:], AX.X, negate=True),
                    reads=[bxa], writes=[bst])
                p.emit("V", lambda h, stt=stt: h.tensor_scalar(
                    stt[:, 1:2], stt[:, 0:1], 1.0 / D, None, ALU.mult),
                    reads=[bst], writes=[bst])
                p.emit("S", lambda h, stt=stt, xa=xa: h.activation(
                    a_junk[:], xa[:], AF.Square, bias=stt[:, 1:2],
                    accum_out=stt[:, 2:3]),
                    reads=[bxa, bst], writes=[b_junk, bst])
                p.emit("V", lambda h, stt=stt: h.tensor_scalar(
                    stt[:, 3:4], stt[:, 2:3], 1.0 / D, 1e-5, ALU.mult, ALU.add),
                    reads=[bst], writes=[bst])
                p.emit("V", lambda h, stt=stt: h.reciprocal(
                    stt[:, 4:5], stt[:, 3:4]), reads=[bst], writes=[bst])
                p.emit("S", lambda h, stt=stt: h.activation(
                    stt[:, 5:6], stt[:, 4:5], AF.Sqrt),
                    reads=[bst], writes=[bst])
                xnf, bxnf = a_xnf[u % 2], b_xnf[u % 2]
                xnb, bxnb = a_xnb[u % 2], b_xnb[u % 2]
                p.emit("V", lambda h, xnf=xnf, xa=xa, stt=stt: h.tensor_scalar(
                    xnf[:], xa[:], stt[:, 1:2], stt[:, 5:6], ALU.add, ALU.mult),
                    reads=[bxa, bst], writes=[bxnf])
                p.emit("V", lambda h, xnb=xnb, xnf=xnf: h.tensor_copy(
                    xnb[:], xnf[:]), reads=[bxnf], writes=[bxnb])
                for half in range(2 if SUBA >= 2 else 0):
                    fns = [(lambda h, xnf=xnf, j=j, cc=half * 4 + j, hf=half:
                            h.matmul(pq[hf][:, ts(j, 128)], xnf[:, ts(cc, 128)],
                                     c_idf[:], is_transpose=True,
                                     start=(j == 0), stop=(j == 3)))
                           for j in range(4)]
                    p.emit("T", fns, reads=[bxnf, CONST], writes=[b_pq[half]])
                xtf, bxtf = a_xtf[u % 2], b_xtf[u % 2]
                xtb, bxtb = a_xtb[u % 2], b_xtb[u % 2]
                if SUBA < 2:
                    p.emit("G", lambda h, xnf=xnf, b=b, t=t: h.dma_start(
                        out=h_resid[b, ts(t, 128), 0:N], in_=xnf[:, 0:N]),
                        reads=[bxnf], writes=[bh_resid[b]])
                    continue
                for half in range(2):
                    p.emit("V", lambda h, xtf=xtf, hf=half: h.tensor_copy(
                        xtf[:].rearrange("p (c n) -> p c n", c=8)
                        [:, 4 * hf:4 * hf + 4, :],
                        pq[hf][:].rearrange("p (c n) -> p c n", c=4)),
                        reads=[b_pq[half]], writes=[bxtf])
                    if SUBA >= 3:
                        p.emit("S", lambda h, xtb=xtb, hf=half: h.copy(
                            xtb[:].rearrange("p (c n) -> p c n", c=8)
                            [:, 4 * hf:4 * hf + 4, :],
                            pq[hf][:].rearrange("p (c n) -> p c n", c=4)),
                            reads=[b_pq[half]], writes=[bxtb])
                if SUBA >= 4:
                    p.emit("T", lambda h, xnb=xnb, t=t: h.matmul(
                        pq[6][0:1, :], c_ones_col[:], xnb[:, 0:512],
                        start=(t == 0), stop=(t == NT - 1)),
                        reads=[bxnb, CONST], writes=[b_pq[6]])
                    p.emit("T", lambda h, xnb=xnb, t=t: h.matmul(
                        pq[7][0:1, :], c_ones_col[:], xnb[:, 512:1024],
                        start=(t == 0), stop=(t == NT - 1)),
                        reads=[bxnb, CONST], writes=[b_pq[7]])
                p.emit("G", lambda h, xtf=xtf, b=b, t=t: h.dma_start(
                    out=h_resid[b].rearrange("(c p) n -> p c n", p=128)
                    [:, :, ts(t, 128)],
                    in_=xtf[:].rearrange("p (c n) -> p c n", c=8)),
                    reads=[bxtf], writes=[bh_resid[b][t]])
                if n_layers == 0:
                    p.emit("G", lambda h, xtf=xtf, b=b, t=t: h.dma_start(
                        out=h_g[b].rearrange("(c p) n -> p c n", p=128)
                        [:, :, ts(t, 128)],
                        in_=xtf[:].rearrange("p (c n) -> p c n", c=8)),
                        reads=[bxtf], writes=[bh_g[b][t]])
                if SUBA >= 3 and SUBA != 5:
                    p.emit("G", lambda h, xtb=xtb, b=b, t=t: h.dma_start(
                        out=h_xnT[b].rearrange("(c p) n -> p c n", p=128)
                        [:, :, ts(t, 128)],
                        in_=xtb[:].rearrange("p (c n) -> p c n", c=8)),
                        reads=[bxtb], writes=[bh_xnT[b][t]])
            if SUBA < 4:
                continue
            p.emit("V", lambda h: h.tensor_scalar(
                a_murow[:, 0:512], pq[6][0:1, :], 1.0 / N, None, ALU.mult),
                reads=[b_pq[6]], writes=[b_mur])
            p.emit("V", lambda h: h.tensor_scalar(
                a_murow[:, 512:1024], pq[7][0:1, :], 1.0 / N, None, ALU.mult),
                reads=[b_pq[7]], writes=[b_mur])
            p.emit("V", lambda h: h.tensor_copy(a_mub[:], a_murow[:]),
                   reads=[b_mur], writes=[b_mur])
            fns = [(lambda h, j=j: h.matmul(
                pqb[0][:, 2 * j:2 * j + 1], a_mub[:, ts(j, 128)],
                c_idb[0:1, 0:1], is_transpose=True, start=(j == 0),
                stop=(j == 7)))
                   for j in range(8)]
            p.emit("T", fns, reads=[b_mur, CONST], writes=[b_pq[0]])
            p.emit("V", lambda h, b=b: h.tensor_scalar(
                g_negmu[:, b * 8:(b + 1) * 8],
                pqb[0][:, 0:16].rearrange("p (a t) -> p a t", t=2)[:, :, 0:1],
                -1.0, None, ALU.mult), reads=[b_pq[0]], writes=[b_negmu])

        stA.close()
        stIdf.close()
        p.barrier()


        NBLK = N // 256
        NT2 = N // 128
        NFT = N // 512
        for li in range(n_layers):
            last_layer = (li == n_layers - 1)
            stP = ExitStack()
            sbP = lambda nm, sh, dt: stP.enter_context(
                nc.sbuf_tensor(f"L{li}a_" + nm, sh, dt))
            w_qkv = sbP("wqkv", [128, 8 * 3072], BF16)
            w_o = sbP("wo", [128, 8 * 1024], BF16)
            stage = [sbP("stage0", [128, 3072], F32)]
            stageb = [stage[0][:].bitcast(BF16)]
            tabs = {nm: sbP(nm, [128, (N // 128) * 64], BF16)
                    for nm in ["tqc", "tqs", "tkc", "tks"]}
            scl = sbP("scl", [1, 64], F32)
            sclb = sbP("sclb", [1, 64], BF16)
            qsb = sbP("qsb", [128, 4 * 64], BF16)
            xr = [sbP(f"xr{i}", [128, 8 * 256], F32) for i in range(2)]
            halo = sbP("halo", [128, 4], F32)
            xs = [sbP(f"xs{i}", [128, 8 * 256], BF16) for i in range(2)]
            qraw = [sbP(f"qraw{i}", [128, 1024], BF16) for i in range(2)]
            kraw = [sbP(f"kraw{i}", [128, 1024], BF16) for i in range(2)]
            kn = [sbP(f"kn{i}", [128, 1024], BF16) for i in range(2)]
            rt1 = sbP("rt1", [128, 1024], BF16)
            rt2 = sbP("rt2", [128, 1024], BF16)
            qrot = [sbP(f"qrot{i}", [128, 1024], BF16) for i in range(2)]
            krot = [sbP(f"krot{i}", [128, 1024], BF16) for i in range(2)]
            sqb = sbP("sqb", [128, 1024], BF16)
            n2 = [sbP(f"n2{i}", [128, 32], F32) for i in range(4)]
            rr = [sbP(f"rr{i}", [128, 32], F32) for i in range(4)]
            qt = [sbP(f"qt{i}", [128, 1024], BF16) for i in range(4)]
            kt = [sbP(f"kt{i}", [128, 1024], BF16) for i in range(6)]
            vb = [sbP(f"vb{i}", [128, 1024], BF16) for i in range(6)]
            sm = [sbP(f"sm{i}", [128, 384], F32) for i in range(3)]
            bb = [sbP(f"bb{i}", [128, 4], F32) for i in range(3)]
            eE = [sbP(f"eE{i}", [128, 384], BF16) for i in range(3)]
            eN = [sbP(f"eN{i}", [128, 384], BF16) for i in range(3)]
            eT = [sbP(f"eT{i}", [128, 384], BF16) for i in range(3)]
            aot = [sbP(f"aot{i}", [128, 8 * 256], BF16) for i in range(2)]

            B_ = Buf
            b_wq, b_wo = B_("wq"), B_("wo")
            b_stage = [B_("st0")]
            b_tabs = {k: B_(k) for k in tabs}
            b_scl, b_sclb, b_qsb = B_("scl"), B_("sclb"), B_("qsb")
            b_xr = [B_(f"xr{i}") for i in range(2)]
            b_halo = B_("halo")
            b_xs = [B_(f"xs{i}") for i in range(2)]
            b_qraw = [B_(f"qraw{i}") for i in range(2)]
            b_kraw = [B_(f"kraw{i}") for i in range(2)]
            b_kn = [B_(f"kn{i}") for i in range(2)]
            b_rt1, b_rt2 = B_("rt1"), B_("rt2")
            b_qrot = [B_(f"qrot{i}") for i in range(2)]
            b_krot = [B_(f"krot{i}") for i in range(2)]
            b_sqb = B_("sqb")
            b_n2 = [B_(f"n2{i}") for i in range(4)]
            b_rr = [B_(f"rr{i}") for i in range(4)]
            b_qt = [B_(f"qt{i}") for i in range(4)]
            b_kt = [B_(f"kt{i}") for i in range(6)]
            b_vb = [B_(f"vb{i}") for i in range(6)]
            b_sm = [B_(f"sm{i}") for i in range(3)]
            b_bb = [B_(f"bb{i}") for i in range(3)]
            b_eE = [B_(f"eE{i}") for i in range(3)]
            b_eN = [B_(f"eN{i}") for i in range(3)]
            b_eT = [B_(f"eT{i}") for i in range(3)]
            b_aot = [B_(f"aot{i}") for i in range(2)]

            for dc in range(8):
                sg, bsg = stage[0], b_stage[0]
                p.emit("Y", lambda h, sg=sg, li=li, dc=dc: h.dma_start(
                    out=sg[:], in_=din["Wqkv"][li, ts(dc, 128), :]),
                    writes=[bsg])
                p.emit("V", lambda h, sg=sg, dc=dc: h.tensor_copy(
                    w_qkv[:, dc * 3072:(dc + 1) * 3072], sg[:]),
                    reads=[bsg], writes=[b_wq])
            for dc in range(8):
                sg, bsg = stage[0], b_stage[0]
                p.emit("Y", lambda h, sg=sg, li=li, dc=dc: h.dma_start(
                    out=sg[:, 0:1024], in_=din["Wo"][li, ts(dc, 128), :]),
                    writes=[bsg])
                p.emit("V", lambda h, sg=sg, dc=dc: h.tensor_copy(
                    w_o[:, dc * 1024:(dc + 1) * 1024], sg[:, 0:1024]),
                    reads=[bsg], writes=[b_wo])
            p.emit("Y", lambda h, li=li: h.dma_start(
                out=scl[:], in_=din["qscale"][li].rearrange("(a d) -> a d", a=1)),
                writes=[b_scl])
            p.emit("V", lambda h: h.tensor_copy(sclb[:], scl[:]),
                   reads=[b_scl], writes=[b_sclb])
            fns = [lambda h: h.matmul(pq[2][:, 0:64], c_ones_row[:], sclb[:],
                                      start=True, stop=False),
                   lambda h: h.matmul(pq[2][:, 64:96], c_ones_row[:],
                                      sclb[:, 32:64], start=False, stop=False),
                   lambda h: h.matmul(pq[2][:, 96:128], c_ones_row[:],
                                      sclb[:, 0:32], start=False, stop=True)]
            p.emit("T", fns, reads=[b_sclb, CONST], writes=[b_pq[2]])
            p.emit("V", lambda h: h.tensor_copy(qsb[:, 0:128], pq[2][:, 0:128]),
                   reads=[b_pq[2]], writes=[b_qsb])
            p.emit("Y", lambda h, li=li: h.dma_start(
                out=scl[:], in_=din["kscale"][li].rearrange("(a d) -> a d", a=1)),
                writes=[b_scl])
            p.emit("V", lambda h: h.tensor_copy(sclb[:], scl[:]),
                   reads=[b_scl], writes=[b_sclb])
            fns = [lambda h: h.matmul(pq[2][:, 0:64], c_ones_row[:], sclb[:],
                                      start=True, stop=False),
                   lambda h: h.matmul(pq[2][:, 64:96], c_ones_row[:],
                                      sclb[:, 32:64], start=False, stop=False),
                   lambda h: h.matmul(pq[2][:, 96:128], c_ones_row[:],
                                      sclb[:, 0:32], start=False, stop=True)]
            p.emit("T", fns, reads=[b_sclb, CONST], writes=[b_pq[2]])
            p.emit("V", lambda h: h.tensor_copy(qsb[:, 128:256], pq[2][:, 0:128]),
                   reads=[b_pq[2]], writes=[b_qsb])
            for nm, base, qoff in [("tqc", "tab_cq", 0), ("tqs", "tab_sq", 64),
                                   ("tkc", "tab_ck", 128), ("tks", "tab_sk", 192)]:
                sg, bsg = stage[0], b_stage[0]
                p.emit("Y", lambda h, sg=sg, base=base: h.dma_start(
                    out=sg[:].bitcast(BF16)[:, 0:(N // 128) * 64]
                    .rearrange("p (t d) -> p t d", t=N // 128),
                    in_=din[base].rearrange("(t p) d -> p t d", p=128)),
                    writes=[bsg])
                p.emit("V", lambda h, nm=nm, sg=sg, qoff=qoff: h.tensor_tensor(
                    tabs[nm][:].rearrange("p (t d) -> p t d", t=N // 128),
                    sg[:].bitcast(BF16)[:, 0:(N // 128) * 64]
                    .rearrange("p (t d) -> p t d", t=N // 128),
                    qsb[:, qoff:qoff + 64].unsqueeze(1)
                    .broadcast_to((128, N // 128, 64)),
                    ALU.mult), reads=[bsg, b_qsb], writes=[b_tabs[nm]])

            def a_s1_gen(b, w, wg):
                xrw, bxrw = xr[wg % 2], b_xr[wg % 2]
                xsw, bxsw = xs[wg % 2], b_xs[wg % 2]
                p.emit("Y", lambda h, b=b, w=w, xrw=xrw: h.dma_start(
                    out=xrw[:].rearrange("p (c n) -> p c n", c=8),
                    in_=h_resid[b].rearrange("(c p) n -> p c n", p=128)
                    [:, :, ts(w, 256)]),
                    reads=[bh_resid[b][2 * w], bh_resid[b][2 * w + 1]],
                    writes=[bxrw])
                p.emit("V", lambda h, xsw=xsw, xrw=xrw: h.tensor_copy(
                    xsw[:].rearrange("p (c n) -> p c n", c=8)[:, 0:4, :],
                    xrw[:].rearrange("p (c n) -> p c n", c=8)[:, 0:4, :]),
                    reads=[bxrw], writes=[bxsw])
                p.emit("V", lambda h, xsw=xsw, xrw=xrw: h.tensor_copy(
                    xsw[:].rearrange("p (c n) -> p c n", c=8)[:, 4:8, 1:256],
                    xrw[:].rearrange("p (c n) -> p c n", c=8)[:, 4:8, 0:255]),
                    reads=[bxrw], writes=[bxsw])
                if w == 0:
                    p.emit("V", lambda h, xsw=xsw: h.memset(
                        xsw[:].rearrange("p (c n) -> p c n", c=8)
                        [:, 4:8, 0:1], 0.0), writes=[bxsw])
                else:
                    p.emit("V", lambda h, xsw=xsw: h.tensor_copy(
                        xsw[:].rearrange("p (c n) -> p c n", c=8)
                        [:, 4:8, 0:1],
                        halo[:].rearrange("p (c n) -> p c n", c=4)),
                        reads=[b_halo], writes=[bxsw])
                p.emit("V", lambda h, xrw=xrw: h.tensor_copy(
                    halo[:].rearrange("p (c n) -> p c n", c=4),
                    xrw[:].rearrange("p (c n) -> p c n", c=8)
                    [:, 4:8, 255:256]),
                    reads=[bxrw], writes=[b_halo])
                yield
                for tile in range(2):
                    u2g = 2 * wg + tile
                    sl2 = u2g % 2
                    sl4 = u2g % 4
                    u2 = 2 * w + tile
                    qr_, kr_ = qraw[sl2], kraw[sl2]
                    for slab in range(6):
                        bank = slab % 2
                        fns = [(lambda h, xsw=xsw, dc=dc, slab=slab,
                                tile=tile, bank=bank: h.matmul(
                                    pq[bank][:],
                                    xsw[:, dc * 256 + tile * 128:
                                        dc * 256 + tile * 128 + 128],
                                    w_qkv[:, dc * 3072 + slab * 512:
                                          dc * 3072 + (slab + 1) * 512],
                                    start=(dc == 0), stop=(dc == 7)))
                               for dc in range(8)]
                        p.emit("T", fns, reads=[bxsw, b_wq],
                               writes=[b_pq[bank]])
                        if slab < 2:
                            p.emit("S", lambda h, qr_=qr_, slab=slab,
                                   bank=bank: h.copy(
                                       qr_[:, slab * 512:(slab + 1) * 512],
                                       pq[bank][:]),
                                   reads=[b_pq[bank]], writes=[b_qraw[sl2]])
                        elif slab < 4:
                            p.emit("S", lambda h, kr_=kr_, slab=slab,
                                   bank=bank: h.copy(
                                       kr_[:, (slab - 2) * 512:
                                           (slab - 1) * 512],
                                       pq[bank][:]),
                                   reads=[b_pq[bank]], writes=[b_kraw[sl2]])
                        else:
                            vslot = u2g % 6
                            p.emit("S", lambda h, vslot=vslot, slab=slab,
                                   bank=bank: h.copy(
                                       vb[vslot][:, (slab - 4) * 512:
                                                 (slab - 3) * 512],
                                       pq[bank][:]),
                                   reads=[b_pq[bank]], writes=[b_vb[vslot]])
                        yield
                    p.emit("V", lambda h, qr_=qr_: h.tensor_tensor(
                        sqb[:], qr_[:], qr_[:], ALU.mult),
                        reads=[b_qraw[sl2]], writes=[b_sqb])
                    p.emit("V", lambda h, sl4=sl4: h.tensor_reduce(
                        n2[sl4][:, 0:16],
                        sqb[:].rearrange("p (g d) -> p g d", g=16),
                        AX.X, ALU.add), reads=[b_sqb], writes=[b_n2[sl4]])
                    p.emit("V", lambda h, kr_=kr_: h.tensor_tensor(
                        sqb[:], kr_[:], kr_[:], ALU.mult),
                        reads=[b_kraw[sl2]], writes=[b_sqb])
                    p.emit("V", lambda h, sl4=sl4: h.tensor_reduce(
                        n2[sl4][:, 16:32],
                        sqb[:].rearrange("p (g d) -> p g d", g=16),
                        AX.X, ALU.add), reads=[b_sqb], writes=[b_n2[sl4]])
                    p.emit("V", lambda h, sl4=sl4: h.tensor_scalar(
                        n2[sl4][:], n2[sl4][:], 1e-24, None, ALU.max),
                        reads=[b_n2[sl4]], writes=[b_n2[sl4]])
                    p.emit("S", lambda h, sl4=sl4: h.activation(
                        n2[sl4][:], n2[sl4][:], AF.Ln),
                        reads=[b_n2[sl4]], writes=[b_n2[sl4]])
                    p.emit("S", lambda h, sl4=sl4: h.activation(
                        rr[sl4][:], n2[sl4][:], AF.Exp, scale=-0.5),
                        reads=[b_n2[sl4]], writes=[b_rr[sl4]])
                    yield
                    p.emit("V", lambda h, sl2=sl2, sl4=sl4, kr_=kr_:
                           h.tensor_tensor(
                               kn[sl2][:].rearrange(
                                   "p (g d) -> p g d", g=16),
                               kr_[:].rearrange("p (g d) -> p g d", g=16),
                               rr[sl4][:, 16:32].unsqueeze(2)
                               .broadcast_to((128, 16, 64)), ALU.mult),
                           reads=[b_kraw[sl2], b_rr[sl4]],
                           writes=[b_kn[sl2]])
                    yield
                    for (srcbuf, bsrc, dstbuf, bdst, tc, tsn) in [
                            (qr_, b_qraw[sl2], qrot[sl2], b_qrot[sl2],
                             "tqc", "tqs"),
                            (kn[sl2], b_kn[sl2], krot[sl2], b_krot[sl2],
                             "tkc", "tks")]:
                        p.emit("V", lambda h, srcbuf=srcbuf, tc=tc, u2=u2:
                               h.tensor_tensor(
                                   rt1[:].rearrange("p (g d) -> p g d", g=16),
                                   srcbuf[:].rearrange("p (g d) -> p g d", g=16),
                                   tabs[tc][:, ts(u2, 64)].unsqueeze(1)
                                   .broadcast_to((128, 16, 64)), ALU.mult),
                               reads=[bsrc, b_tabs[tc]], writes=[b_rt1])
                        p.emit("V", lambda h, srcbuf=srcbuf, tsn=tsn, u2=u2:
                               h.tensor_tensor(
                                   rt2[:].rearrange("p (g d) -> p g d", g=16)
                                   [:, :, 0:32],
                                   srcbuf[:].rearrange("p (g d) -> p g d", g=16)
                                   [:, :, 32:64],
                                   tabs[tsn][:, u2 * 64:u2 * 64 + 32]
                                   .unsqueeze(1).broadcast_to((128, 16, 32)),
                                   ALU.mult),
                               reads=[bsrc, b_tabs[tsn]], writes=[b_rt2])
                        p.emit("V", lambda h, srcbuf=srcbuf, tsn=tsn, u2=u2:
                               h.tensor_tensor(
                                   rt2[:].rearrange("p (g d) -> p g d", g=16)
                                   [:, :, 32:64],
                                   srcbuf[:].rearrange("p (g d) -> p g d", g=16)
                                   [:, :, 0:32],
                                   tabs[tsn][:, u2 * 64 + 32:u2 * 64 + 64]
                                   .unsqueeze(1).broadcast_to((128, 16, 32)),
                                   ALU.mult),
                               reads=[bsrc, b_tabs[tsn]], writes=[b_rt2])
                        p.emit("V", lambda h, dstbuf=dstbuf: h.tensor_tensor(
                            dstbuf[:], rt1[:], rt2[:], ALU.add),
                            reads=[b_rt1, b_rt2], writes=[bdst])
                        yield
                    for (srcb, bsrc, dstb, bdst, bankj) in [
                            (qrot[sl2], b_qrot[sl2], qt[sl4], b_qt[sl4], 2),
                            (krot[sl2], b_krot[sl2], kt[u2g % 6],
                             b_kt[u2g % 6], 3)]:
                        fns = [(lambda h, srcb=srcb, j=j, bankj=bankj:
                                h.matmul(
                                    pqb[bankj][:, ts(j, 128)],
                                    srcb[:, ts(j, 128)], c_idb[:],
                                    is_transpose=True, start=(j == 0),
                                    stop=(j == 7))) for j in range(8)]
                        p.emit("T", fns, reads=[bsrc, CONST],
                               writes=[b_pq[bankj]])
                        p.emit("V", lambda h, dstb=dstb, bankj=bankj:
                               h.tensor_copy(dstb[:], pqb[bankj][:]),
                               reads=[b_pq[bankj]], writes=[bdst])
                        yield

            def a_s2(b, w, wg, feeder):
                xrw, bxrw = xr[wg % 2], b_xr[wg % 2]
                aotw, baotw = aot[wg % 2], b_aot[wg % 2]
                for tile in range(2):
                    u2g = 2 * wg + tile
                    sl4 = u2g % 4
                    u2 = 2 * w + tile
                    s_idx = u2
                    nk = min(3, s_idx + 1)
                    if s_idx == 0:
                        msc, mec = 2, 3
                    elif s_idx == 1:
                        msc, mec = 4, 5
                    else:
                        msc, mec = 0, 1
                    slots = list(range(u2g - nk + 1, u2g + 1))

                    def unit_qk(hh, sl4=sl4, slots=slots):
                        sb_ = 4 + (hh % 2)
                        hrow = (hh % 2) * 64
                        hcol = hh // 2
                        ns_ = len(slots)
                        fns = [(lambda h, j=j, slot=slot, sb_=sb_,
                                hrow=hrow, hcol=hcol, sl4=sl4, ns_=ns_:
                                h.matmul(
                                    pq[sb_][:, ts(j, 128)],
                                    qt[sl4][hrow:hrow + 64,
                                            hcol * 128:(hcol + 1) * 128],
                                    kt[slot % 6][hrow:hrow + 64,
                                                 hcol * 128:(hcol + 1) * 128],
                                    start=(j == 0), stop=(j == ns_ - 1)))
                               for j, slot in enumerate(slots)]
                        kt_reads = [b_kt[slot % 6] for slot in slots]
                        p.emit("T", fns, reads=[b_qt[sl4]] + kt_reads,
                               writes=[b_pq[sb_]])

                    def st_A(hh):
                        un = hh % 3
                        sb_ = 4 + (hh % 2)
                        p.emit("V", lambda h, un=un, sb_=sb_, nk=nk,
                               msc=msc, mec=mec: h.tensor_mask_reduce(
                                   sm[un][:, 0:nk * 128],
                                   pq[sb_][:, 0:nk * 128],
                                   c_maskv[:, msc:msc + 1],
                                   c_maskv[:, mec:mec + 1], 1.0, -3.0e38,
                                   ALU.max, accum_out=bb[un][:, 0:1]),
                               reads=[b_pq[sb_], CONST],
                               writes=[b_sm[un], b_bb[un]])
                        p.emit("V", lambda h, un=un, sl4=sl4, hh=hh:
                               h.scalar_tensor_tensor(
                                   bb[un][:, 1:2], bb[un][:, 0:1], -1.0,
                                   rr[sl4][:, hh:hh + 1], ALU.mult,
                                   ALU.mult),
                               reads=[b_bb[un], b_rr[sl4]],
                               writes=[b_bb[un]])

                    def st_B(hh):
                        un = hh % 3
                        p.emit("S", lambda h, un=un, nk=nk, sl4=sl4, hh=hh:
                               h.activation(
                                   eE[un][:, 0:nk * 128],
                                   sm[un][:, 0:nk * 128], AF.Exp,
                                   bias=bb[un][:, 1:2],
                                   scale=rr[sl4][:, hh:hh + 1],
                                   accum_out=bb[un][:, 2:3]),
                               reads=[b_sm[un], b_bb[un], b_rr[sl4]],
                               writes=[b_eE[un], b_bb[un]])

                    def st_C(hh):
                        un = hh % 3
                        p.emit("V", lambda h, un=un: h.reciprocal(
                            bb[un][:, 3:4], bb[un][:, 2:3]),
                            reads=[b_bb[un]], writes=[b_bb[un]])
                        p.emit("V", lambda h, un=un, nk=nk: h.tensor_scalar(
                            eN[un][:, 0:nk * 128], eE[un][:, 0:nk * 128],
                            bb[un][:, 3:4], None, ALU.mult),
                            reads=[b_eE[un], b_bb[un]], writes=[b_eN[un]])

                    def st_D1(hh):
                        un = hh % 3
                        tb = 2 + (hh % 2)
                        fns = [(lambda h, j=j, un=un, tb=tb, nk=nk:
                                h.matmul(
                                    pqb[tb][:, ts(j, 128)],
                                    eN[un][:, ts(j, 128)], c_idb[:],
                                    is_transpose=True, start=(j == 0),
                                    stop=(j == nk - 1))) for j in range(nk)]
                        p.emit("T", fns, reads=[b_eN[un], CONST],
                               writes=[b_pq[tb]])

                    def st_EV(hh):
                        un = hh % 3
                        tb = 2 + (hh % 2)
                        p.emit("S", lambda h, un=un, nk=nk, tb=tb:
                               h.copy(eT[un][:, 0:nk * 128],
                                      pqb[tb][:, 0:nk * 128]),
                               reads=[b_pq[tb]], writes=[b_eT[un]])

                    def st_D2t(hh):
                        un = hh % 3
                        ob = 6 + ((hh // 2) % 2)
                        hrow = (hh % 2) * 64
                        ns_ = len(slots)
                        fns = [(lambda h, j=j, slot=slot, ob=ob, un=un,
                                hrow=hrow, hh=hh, ns_=ns_: h.matmul(
                                    pq[ob][hrow:hrow + 64, 0:128],
                                    vb[slot % 6][:, ts(hh, 64)],
                                    eT[un][:, ts(j, 128)],
                                    start=(j == 0),
                                    stop=(j == ns_ - 1)))
                               for j, slot in enumerate(slots)]
                        vb_reads = [b_vb[slot % 6] for slot in slots]
                        p.emit("T", fns, reads=[b_eT[un]] + vb_reads,
                               writes=[b_pq[ob]])

                    def st_D2v(hh):
                        ob = 6 + ((hh // 2) % 2)
                        hcol = hh // 2
                        if hh % 2 == 1:
                            p.emit("V", lambda h, ob=ob, hcol=hcol,
                                   tile=tile, aotw=aotw: h.tensor_copy(
                                       aotw[:, hcol * 256 + tile * 128:
                                            hcol * 256 + tile * 128 + 128],
                                       pq[ob][:, 0:128]),
                                   reads=[b_pq[ob]], writes=[baotw])

                    unit_qk(0)
                    unit_qk(1)
                    st_A(0)
                    st_B(0)
                    for u in range(20):
                        if 0 <= u - 4:
                            st_D2t(u - 4)
                        if u + 2 < 16:
                            unit_qk(u + 2)
                        if 0 <= u - 2 <= 15:
                            st_D1(u - 2)
                        if 0 <= u - 3 <= 15:
                            st_EV(u - 3)
                        if u + 1 < 16:
                            st_A(u + 1)
                            st_B(u + 1)
                        if u <= 15:
                            st_C(u)
                        if 0 <= u - 4:
                            st_D2v(u - 4)
                for mp in range(4):
                    if feeder is not None:
                        next(feeder, None)
                    bank = mp % 2
                    fns = []
                    for mi in range(2):
                        m = 2 * mp + mi
                        for kc in range(8):
                            fns.append((lambda h, m=m, kc=kc, mi=mi,
                                        bank=bank, aotw=aotw: h.matmul(
                                            pq[bank][:, mi * 256:
                                                     mi * 256 + 256],
                                            w_o[:, kc * 1024 + m * 128:
                                                kc * 1024 + (m + 1) * 128],
                                            aotw[:, ts(kc, 256)],
                                            start=(mi == 0 and kc == 0),
                                            stop=(mi == 1 and kc == 7))))
                    p.emit("T", fns, reads=[baotw, b_wo],
                           writes=[b_pq[bank]])
                    for mi in range(2):
                        m = 2 * mp + mi
                        p.emit("V", lambda h, m=m, mi=mi, bank=bank,
                               xrw=xrw: h.tensor_tensor(
                                   xrw[:, ts(m, 256)],
                                   xrw[:, ts(m, 256)],
                                   pq[bank][:, mi * 256:mi * 256 + 256],
                                   ALU.add),
                               reads=[bxrw, b_pq[bank]], writes=[bxrw])
                p.emit("G", lambda h, b=b, w=w, xrw=xrw: h.dma_start(
                    out=h_resid[b].rearrange("(c p) n -> p c n", p=128)
                    [:, :, ts(w, 256)],
                    in_=xrw[:].rearrange("p (c n) -> p c n", c=8)),
                    reads=[bxrw],
                    writes=[bh_resid[b][2 * w], bh_resid[b][2 * w + 1]])

            bw = [(b, w) for b in range(B) for w in range(NBLK)]
            g0 = a_s1_gen(*bw[0], 0)
            for _ in g0:
                pass
            for wg, (b, w) in enumerate(bw):
                if wg + 1 < len(bw):
                    for _ in a_s1_gen(*bw[wg + 1], wg + 1):
                        pass
                a_s2(b, w, wg, None)
            stP.close()
            p.barrier()

            stF = ExitStack()
            sbF = lambda nm, sh, dt: stF.enter_context(
                nc.sbuf_tensor(f"L{li}f_" + nm, sh, dt))
            w_1 = sbF("w1", [128, 8 * 5460], BF16)
            w_2 = sbF("w2", [128, 22 * 1024], BF16)
            b1p = sbF("b1p", [128, 44], F32)
            b2c = sbF("b2c", [128, 8], F32)
            stSet = ExitStack()
            stage2 = [stSet.enter_context(
                nc.sbuf_tensor(f"L{li}f_stage0", [128, 2730], F32))]
            gcol = stSet.enter_context(
                nc.sbuf_tensor(f"L{li}f_gcol", [128, 8], F32))
            bcolb = stSet.enter_context(
                nc.sbuf_tensor(f"L{li}f_bcolb", [128, 8], BF16))
            bcolf = stSet.enter_context(
                nc.sbuf_tensor(f"L{li}f_bcolf", [128, 8], F32))
            b1c = stSet.enter_context(
                nc.sbuf_tensor(f"L{li}f_b1c", [128, 44], F32))

            b_w1, b_w2 = Buf("w1"), Buf("w2")
            b_st2 = [Buf("st20")]
            b_gcol, b_bcolb, b_bcolf = Buf("gcol"), Buf("bcolb"), Buf("bcolf")
            b_b1c, b_b1p, b_b2c = Buf("b1c"), Buf("b1p"), Buf("b2c")
            b_xr2 = [Buf(f"xr2{i}") for i in range(2)]
            b_halo2 = Buf("halo2")
            b_xs2 = [Buf(f"xs2{i}") for i in range(2)]
            b_sqt = Buf("sqt")
            b_row = Buf("rowtmp")
            b_rowbf = Buf("rowbf")

            
            b_gel = [Buf(f"gel{i}") for i in range(2)]
            b_act = Buf("act")
                        
            p.emit("Y", lambda h, li=li: h.dma_start(
                out=gcol[:], in_=din["gcols"][li]), writes=[b_gcol])
            p.emit("Y", lambda h, li=li: h.dma_start(
                out=bcolf[:], in_=din["bcols"][li]), writes=[b_bcolf])
            p.emit("V", lambda h: h.tensor_copy(bcolb[:], bcolf[:]),
                   reads=[b_bcolf], writes=[b_bcolb])
            p.emit("Y", lambda h, li=li: h.dma_start(
                out=b1c[:], in_=din["b1cols"][li]), writes=[b_b1c])
            p.emit("Y", lambda h, li=li: h.dma_start(
                out=b2c[:], in_=din["b2cols"][li]), writes=[b_b2c])
            wtmp = stSet.enter_context(
                nc.sbuf_tensor(f"L{li}f_wtmp", [128, 2730], BF16))
            b_wtmp = Buf("wtmp")
            for dc in range(8):
                for hf in range(2):
                    sg2, bsg2 = stage2[0], b_st2[0]
                    p.emit("Y", lambda h, sg2=sg2, li=li, dc=dc, hf=hf:
                           h.dma_start(out=sg2[:],
                                       in_=din["W1"][li, ts(dc, 128),
                                                     hf * 2730:(hf + 1) * 2730]),
                           writes=[bsg2])
                    p.emit("V", lambda h, sg2=sg2, dc=dc, hf=hf:
                           h.tensor_scalar(
                               w_1[:, dc * 5460 + hf * 2730:
                                   dc * 5460 + (hf + 1) * 2730],
                               sg2[:], gcol[:, dc:dc + 1], None, ALU.mult),
                           reads=[bsg2, b_gcol], writes=[b_w1])
                    p.emit("V", lambda h, sg2=sg2: h.tensor_copy(
                        wtmp[:], sg2[:]), reads=[bsg2], writes=[b_wtmp])
                    fns = []
                    jorder = [0, 21] + list(range(1, 21))
                    for pos, j in enumerate(jorder):
                        wdt = 42 if j == 21 else 128
                        fns.append((lambda h, j=j, wdt=wdt, dc=dc, hf=hf,
                                    pos=pos: h.matmul(
                                        pq[2][0:wdt, hf * 22 + j:
                                              hf * 22 + j + 1],
                                        wtmp[:, j * 128:j * 128 + wdt],
                                        bcolb[:, dc:dc + 1],
                                        start=(dc == 0 and hf == 0 and
                                               pos == 0),
                                        stop=(dc == 7 and hf == 1 and
                                              pos == 21))))
                    p.emit("T", fns, reads=[b_wtmp, b_bcolb],
                           writes=[b_pq[2]])
            p.emit("V", lambda h: h.tensor_tensor(
                b1p[:], pq[2][:, 0:44], b1c[:], ALU.add),
                reads=[b_pq[2], b_b1c], writes=[b_b1p])
            p.emit("V", lambda h: h.memset(
                w_2[:, 21 * 1024:22 * 1024], 0.0), writes=[b_w2])
            for kc in range(22):
                rows = 42 if kc == 21 else 128
                sg2, bsg2 = stage2[0], b_st2[0]
                p.emit("Y", lambda h, sg2=sg2, li=li, kc=kc, rows=rows:
                       h.dma_start(out=sg2[0:rows, 0:1024],
                                   in_=din["W2"][li, kc * 128:kc * 128 + rows, :]),
                       writes=[bsg2])
                p.emit("V", lambda h, sg2=sg2, kc=kc, rows=rows: h.tensor_copy(
                    w_2[0:rows, kc * 1024:(kc + 1) * 1024],
                    sg2[0:rows, 0:1024]), reads=[bsg2], writes=[b_w2])

            stSet.close()
            p.barrier()
            xr2 = [sbF(f"xr2{i}", [128, 8 * 512], F32) for i in range(2)]
            halo2 = sbF("halo2", [128, 4], F32)
            xs2 = [sbF(f"xs2{i}", [128, 8 * 512], BF16) for i in range(2)]
            sqt = sbF("sqt", [128, 512], BF16)
            rowtmp = sbF("rowtmp", [1, 512], F32)
            rowbf = sbF("rowbf", [1, 1024], BF16)

            gel = [sbF(f"gel{i}", [128, 512], BF16) for i in range(2)]
            act = sbF("act", [128, 22 * 512], BF16)
            p.emit("V", lambda h: h.memset(act[:, 21 * 512:22 * 512], 0.0),
                   writes=[b_act])
            def f_s1_gen(b, tt, tg):
                xrw, bxrw = xr2[tg % 2], b_xr2[tg % 2]
                xsw, bxsw = xs2[tg % 2], b_xs2[tg % 2]
                rblk = [bh_resid[b][4 * tt + i] for i in range(4)]
                p.emit("Y", lambda h, b=b, tt=tt, xrw=xrw: h.dma_start(
                    out=xrw[:].rearrange("p (c n) -> p c n", c=8),
                    in_=h_resid[b].rearrange("(c p) n -> p c n", p=128)
                    [:, :, ts(tt, 512)]),
                    reads=rblk, writes=[bxrw])
                p.emit("G", lambda h, xsw=xsw, xrw=xrw: h.tensor_copy(
                    xsw[:].rearrange("p (c n) -> p c n", c=8)[:, 0:4, :],
                    xrw[:].rearrange("p (c n) -> p c n", c=8)[:, 0:4, :]),
                    reads=[bxrw], writes=[bxsw], gc=True)
                p.emit("G", lambda h, xsw=xsw, xrw=xrw: h.tensor_copy(
                    xsw[:].rearrange("p (c n) -> p c n", c=8)[:, 4:8, 1:512],
                    xrw[:].rearrange("p (c n) -> p c n", c=8)[:, 4:8, 0:511]),
                    reads=[bxrw], writes=[bxsw], gc=True)
                if tt == 0:
                    p.emit("V", lambda h, xsw=xsw: h.memset(
                        xsw[:].rearrange("p (c n) -> p c n", c=8)
                        [:, 4:8, 0:1], 0.0), writes=[bxsw])
                else:
                    p.emit("V", lambda h, xsw=xsw: h.tensor_copy(
                        xsw[:].rearrange("p (c n) -> p c n", c=8)
                        [:, 4:8, 0:1],
                        halo2[:].rearrange("p (c n) -> p c n", c=4)),
                        reads=[b_halo2], writes=[bxsw])
                p.emit("V", lambda h, xrw=xrw: h.tensor_copy(
                    halo2[:].rearrange("p (c n) -> p c n", c=4),
                    xrw[:].rearrange("p (c n) -> p c n", c=8)
                    [:, 4:8, 511:512]),
                    reads=[bxrw], writes=[b_halo2])
                yield
                fns = [(lambda h, xsw=xsw, dc=dc: h.matmul(
                    pq[0][0:1, :], c_ones_col[:],
                    xsw[:, ts(dc, 512)], start=(dc == 0), stop=(dc == 7)))
                       for dc in range(8)]
                p.emit("T", fns, reads=[bxsw, CONST], writes=[b_pq[0]])
                for dc in range(8):
                    p.emit("V", lambda h, xsw=xsw, dc=dc: h.tensor_tensor(
                        sqt[:], xsw[:, ts(dc, 512)], xsw[:, ts(dc, 512)],
                        ALU.mult), reads=[bxsw], writes=[b_sqt])
                    p.emit("T", lambda h, dc=dc: h.matmul(
                        pq[1][0:1, :], c_ones_col[:], sqt[:],
                        start=(dc == 0), stop=(dc == 7)),
                        reads=[b_sqt, CONST], writes=[b_pq[1]])
                    if dc % 2 == 1:
                        yield
                p.emit("V", lambda h: h.tensor_scalar(
                    pq[0][32:33, :], pq[0][0:1, :], 1.0 / D, None,
                    ALU.mult), reads=[b_pq[0]], writes=[b_pq[0]])
                p.emit("V", lambda h: h.tensor_scalar(
                    pq[1][32:33, :], pq[1][0:1, :], 1.0 / D, 1e-5,
                    ALU.mult, ALU.add), reads=[b_pq[1]], writes=[b_pq[1]])
                p.emit("S", lambda h: h.activation(
                    rowtmp[:], pq[0][32:33, :], AF.Square),
                    reads=[b_pq[0]], writes=[b_row])
                p.emit("V", lambda h: h.tensor_tensor(
                    pq[1][64:65, :], pq[1][32:33, :], rowtmp[:],
                    ALU.subtract), reads=[b_pq[1], b_row], writes=[b_pq[1]])
                p.emit("S", lambda h: h.activation(
                    pq[0][64:65, :], pq[1][64:65, :], AF.Sqrt),
                    reads=[b_pq[1]], writes=[b_pq[0]])
                p.emit("V", lambda h: h.reciprocal(
                    rowtmp[:], pq[0][64:65, :]),
                    reads=[b_pq[0]], writes=[b_row])
                p.emit("V", lambda h: h.tensor_copy(
                    rowbf[:, 0:512], rowtmp[:]),
                    reads=[b_row], writes=[b_rowbf])
                p.emit("V", lambda h: h.scalar_tensor_tensor(
                    rowbf[:, 512:1024], pq[0][32:33, :], -1.0,
                    rowtmp[:], ALU.mult, ALU.mult),
                    reads=[b_pq[0], b_row], writes=[b_rowbf])
                yield
                p.emit("T", lambda h: h.matmul(
                    pq[0][:], c_ones_row[:], rowbf[:, 0:512],
                    start=True, stop=True),
                    reads=[b_rowbf, CONST], writes=[b_pq[0]])
                p.emit("T", lambda h: h.matmul(
                    pq[1][:], c_ones_row[:], rowbf[:, 512:1024],
                    start=True, stop=True),
                    reads=[b_rowbf, CONST], writes=[b_pq[1]])
                for dc in range(8):
                    p.emit("V", lambda h, xsw=xsw, dc=dc: h.tensor_tensor(
                        sqt[:], xsw[:, ts(dc, 512)], pq[0][:], ALU.mult),
                        reads=[bxsw, b_pq[0]], writes=[b_sqt])
                    p.emit("V", lambda h, xsw=xsw, dc=dc: h.tensor_tensor(
                        xsw[:, ts(dc, 512)], sqt[:], pq[1][:], ALU.add),
                        reads=[b_sqt, b_pq[1]], writes=[bxsw])
                    yield

            def f_s2(b, tt, tg, feeder):
                xrw, bxrw = xr2[tg % 2], b_xr2[tg % 2]
                xsw, bxsw = xs2[tg % 2], b_xs2[tg % 2]
                rblk = [bh_resid[b][4 * tt + i] for i in range(4)]
                zz, b_zz = xsw, bxsw

                def fj_mm(j):
                    wdt = 42 if j == 21 else 128
                    pa = 2 + 2 * (j % 3)
                    pg = 3 + 2 * (j % 3)
                    fns = [(lambda h, dc=dc, j=j, wdt=wdt, pa=pa, zz=zz:
                            h.matmul(
                                pq[pa][0:wdt, :],
                                w_1[:, dc * 5460 + j * 128:
                                    dc * 5460 + j * 128 + wdt],
                                zz[:, ts(dc, 512)],
                                start=(dc == 0), stop=(dc == 7)))
                           for dc in range(8)]
                    p.emit("T", fns, reads=[b_zz, b_w1], writes=[b_pq[pa]])
                    fns = [(lambda h, dc=dc, j=j, wdt=wdt, pg=pg, zz=zz:
                            h.matmul(
                                pq[pg][0:wdt, :],
                                w_1[:, dc * 5460 + 2730 + j * 128:
                                    dc * 5460 + 2730 + j * 128 + wdt],
                                zz[:, ts(dc, 512)],
                                start=(dc == 0), stop=(dc == 7)))
                           for dc in range(8)]
                    p.emit("T", fns, reads=[b_zz, b_w1], writes=[b_pq[pg]])

                def fj_gelu(j):
                    wdt = 42 if j == 21 else 128
                    pg = 3 + 2 * (j % 3)
                    p.emit("S", lambda h, j=j, wdt=wdt, pg=pg: h.activation(
                        gel[j % 2][0:wdt, :],
                        pq[pg][0:wdt, :], AF.Gelu,
                        bias=b1p[0:wdt, 22 + j:23 + j]),
                        reads=[b_pq[pg], b_b1p], writes=[b_gel[j % 2]])

                def fj_stt(j):
                    wdt = 42 if j == 21 else 128
                    pa = 2 + 2 * (j % 3)
                    p.emit("V", lambda h, j=j, wdt=wdt, pa=pa:
                           h.scalar_tensor_tensor(
                               act[0:wdt, ts(j, 512)], pq[pa][0:wdt, :],
                               b1p[0:wdt, j:j + 1], gel[j % 2][0:wdt, :],
                               ALU.add, ALU.mult),
                           reads=[b_pq[pa], b_b1p, b_gel[j % 2]],
                           writes=[b_act])

                for j in range(24):
                    if feeder is not None:
                        next(feeder, None)
                    if j >= 2:
                        fj_stt(j - 2)
                    if 1 <= j <= 22:
                        fj_gelu(j - 1)
                    if j < 22:
                        fj_mm(j)
                for wave in range(2):
                    for mi in range(4):
                        m = wave * 4 + mi
                        bank = 4 + mi
                        fns = []
                        for kc in range(22):
                            wdt = 42 if kc == 21 else 128
                            fns.append((lambda h, m=m, kc=kc, wdt=wdt,
                                        bank=bank: h.matmul(
                                            pq[bank][:],
                                            w_2[0:wdt, kc * 1024 +
                                                m * 128:kc * 1024 +
                                                (m + 1) * 128],
                                            act[0:wdt, ts(kc, 512)],
                                            start=(kc == 0),
                                            stop=(kc == 21))))
                        p.emit("T", fns, reads=[b_act, b_w2],
                               writes=[b_pq[bank]])
                        p.emit("V", lambda h, m=m, bank=bank, xrw=xrw:
                               h.scalar_tensor_tensor(
                                   xrw[:, ts(m, 512)], pq[bank][:],
                                   b2c[:, m:m + 1], xrw[:, ts(m, 512)],
                                   ALU.add, ALU.add),
                               reads=[b_pq[bank], b_b2c, bxrw],
                               writes=[bxrw])
                if last_layer:
                    for m in range(8):
                        p.emit("S", lambda h, m=m, xrw=xrw: h.activation(
                            xrw[:, ts(m, 512)], xrw[:, ts(m, 512)],
                            AF.Silu), reads=[bxrw], writes=[bxrw])
                    p.emit("G", lambda h, b=b, tt=tt, xrw=xrw: h.dma_start(
                        out=h_g[b].rearrange("(c p) n -> p c n", p=128)
                        [:, :, ts(tt, 512)],
                        in_=xrw[:].rearrange("p (c n) -> p c n", c=8)),
                        reads=[bxrw],
                        writes=[bh_g[b][4 * tt + i] for i in range(4)])
                else:
                    for m in range(8):
                        p.emit("G", lambda h, b=b, tt=tt, m=m, xrw=xrw:
                               h.dma_start(
                                   out=h_resid[b, m * 128:(m + 1) * 128,
                                               ts(tt, 512)],
                                   in_=xrw[:, ts(m, 512)]),
                               reads=[bxrw], writes=rblk)

            btt = [(b, tt) for b in range(B) for tt in range(NFT)]
            g0 = f_s1_gen(*btt[0], 0)
            for _ in g0:
                pass
            for tg, (b, tt) in enumerate(btt):
                if tg + 1 < len(btt):
                    for _ in f_s1_gen(*btt[tg + 1], tg + 1):
                        pass
                f_s2(b, tt, tg, None)

            stF.close()
            p.barrier()

        stC = ExitStack()
        sbC = lambda nm, sh, dt: stC.enter_context(nc.sbuf_tensor(nm, sh, dt))
        c_xc = [sbC(f"c_xc{i}", [128, N], BF16) for i in range(8)]
        c_sq = [sbC("c_sq0", [128, N], BF16)]
        c_norm = sbC("c_norm", [16, N], F32)
        c_negc = sbC("c_negc", [16, N], BF16)
        c_maxn = sbC("c_maxn", [16, 1], F32)
        c_Q = [sbC(f"c_Q{i}", [65, N], BF16) for i in range(2)]
        c_K = [sbC(f"c_K{i}", [65, N], BF16) for i in range(2)]
        c_Vt = [sbC(f"c_Vt{i}", [64, N], BF16) for i in range(2)]
        c_Vb = [sbC(f"c_Vb{i}", [128, (N // 128) * 66], BF16) for i in range(2)]
        c_E = [sbC(f"c_E{i}", [128, 1024], BF16) for i in range(3)]
        c_rden = [sbC(f"c_rden{i}", [1, 512], F32) for i in range(2)]
        c_rdb = [sbC(f"c_rdb{i}", [1, 512], BF16) for i in range(2)]
        c_R = [sbC(f"c_R{i}", [64, 512], F32) for i in range(2)]
        c_sout = [sbC("c_sout0", [64, N], F32)]
        b_xc = [Buf(f"c_xc{i}") for i in range(8)]
        b_sq = [Buf("c_sq0")]
        b_norm = Buf("c_norm")
        b_negc = Buf("c_negc")
        b_maxn = Buf("c_maxn")
        b_Q = [Buf(f"c_Q{i}") for i in range(2)]
        b_K = [Buf(f"c_K{i}") for i in range(2)]
        b_Vt = [Buf(f"c_Vt{i}") for i in range(2)]
        b_Vb = [Buf(f"c_Vb{i}") for i in range(2)]
        b_E = [Buf(f"c_E{i}") for i in range(3)]
        b_rden = [Buf(f"c_rden{i}") for i in range(2)]
        b_rdb = [Buf(f"c_rdb{i}") for i in range(2)]
        b_R = [Buf(f"c_R{i}") for i in range(2)]
        b_sout = [Buf("c_sout0")]

        for i in range(2 if "C" in phases else 0):
            p.emit("V", lambda h, i=i: h.memset(c_K[i][64:65, :], 1.0),
                   writes=[b_K[i]])
            p.emit("V", lambda h, i=i: h.memset(
                c_Vb[i][:].rearrange("p (k o) -> p k o", k=N // 128)
                [:, :, 64:66], 1.0), writes=[b_Vb[i]])

        for b in range(B if "C" in phases else 0):
            NG = N // 512
            for c in range(8):
                p.emit("Y", lambda h, b=b, c=c: h.dma_start(
                    out=c_xc[c][:], in_=h_xnT[b, ts(c, 128), :]),
                    reads=bh_xnT[b], writes=[b_xc[c]])
                p.emit("V", lambda h, b=b, c=c: h.tensor_scalar(
                    c_xc[c][:], c_xc[c][:],
                    g_negmu[:, b * 8 + c:b * 8 + c + 1], None, ALU.add),
                    reads=[b_xc[c], b_negmu], writes=[b_xc[c]])
                sq, bsq = c_sq[0], b_sq[0]
                p.emit("V", lambda h, sq=sq, c=c: h.tensor_tensor(
                    sq[:], c_xc[c][:], c_xc[c][:], ALU.mult),
                    reads=[b_xc[c]], writes=[bsq])
                for g8 in range(NG):
                    p.emit("T", lambda h, sq=sq, c=c, g8=g8: h.matmul(
                        pq[g8][0:16, :], c_indic[:, ts(c, 16)],
                        sq[:, ts(g8, 512)], start=(c == 0), stop=(c == 7)),
                        reads=[bsq, CONST], writes=[b_pq[g8]])
            for g8 in range(NG):
                p.emit("S", lambda h, g8=g8: h.activation(
                    c_norm[:, ts(g8, 512)], pq[g8][0:16, :], AF.Sqrt),
                    reads=[b_pq[g8]], writes=[b_norm])
            p.emit("V", lambda h: h.reduce_max(c_maxn[:], c_norm[:], AX.X),
                   reads=[b_norm], writes=[b_maxn])
            p.emit("V", lambda h: h.tensor_scalar(
                c_maxn[:], c_maxn[:], -1.0, None, ALU.mult),
                reads=[b_maxn], writes=[b_maxn])
            p.emit("V", lambda h: h.tensor_scalar(
                c_negc[:], c_norm[:], c_maxn[:], None, ALU.mult),
                reads=[b_norm, b_maxn], writes=[b_negc])

            for hh in range(H):
                u = b * H + hh
                sl = u % 2
                Q, K_, Vt, Vb = c_Q[sl], c_K[sl], c_Vt[sl], c_Vb[sl]
                bQ, bK, bVt, bVb = b_Q[sl], b_K[sl], b_Vt[sl], b_Vb[sl]
                cchunk, half = hh // 2, hh % 2
                p.emit("V", lambda h, Q=Q, cc=cchunk, hf=half: h.tensor_copy(
                    Q[0:64, :], c_xc[cc][ts(hf, 64), :]),
                    reads=[b_xc[cchunk]], writes=[bQ])
                p.emit("G", lambda h, Q=Q, hh=hh: h.dma_start(
                    out=Q[64:65, :], in_=c_negc[hh:hh + 1, :]),
                    reads=[b_negc], writes=[bQ])
                p.emit("V", lambda h, K_=K_, cc=cchunk, hf=half: h.tensor_copy(
                    K_[0:64, :], c_xc[cc][ts(hf, 64), :]),
                    reads=[b_xc[cchunk]], writes=[bK])
                p.emit("Y", lambda h, Vt=Vt, b=b, hh=hh: h.dma_start(
                    out=Vt[:], in_=h_xnT[b, ts(hh, 64), :]),
                    reads=bh_xnT[b], writes=[bVt])
                NW = max(1, (N // 128) // 16)
                for wv in range(NW):
                    nj = min(16, N // 128)
                    fns = [(lambda h, Vt=Vt, j=j, kc=wv * 16 + j, nj=nj: h.matmul(
                        pqb[2][:, ts(j, 64)], Vt[:, ts(kc, 128)],
                        c_idb[0:64, 0:64], is_transpose=True,
                        start=(j == 0), stop=(j == nj - 1)))
                           for j in range(nj)]
                    p.emit("T", fns, reads=[bVt, CONST], writes=[b_pq[2]])
                    p.emit("V", lambda h, Vb=Vb, wv=wv, nj=nj: h.tensor_copy(
                        Vb[:].rearrange("p (k o) -> p k o", k=N // 128)
                        [:, wv * 16:wv * 16 + nj, 0:64],
                        pqb[2][:].rearrange("p (k o) -> p k o", k=16)
                        [:, 0:nj, :]),
                        reads=[b_pq[2]], writes=[bVb])
                so, bso = c_sout[0], b_sout[0]
                for qb in range(N // 512):
                    ob = 6 + (qb % 2)
                    NKC = N // 128
                    SPANS = [pq45, pq01, pq23]
                    SPB = [4, 0, 2]

                    def emit_qk(kc, Q=Q, K_=K_, qb=qb):
                        spq = SPB[(kc // 2) % 3] + (kc % 2)
                        p.emit("T", lambda h, Q=Q, K_=K_, qb=qb, kc=kc,
                               spq=spq: h.matmul(
                                   pq[spq][:], K_[:, ts(kc, 128)],
                                   Q[:, ts(qb, 512)], start=True, stop=True),
                               reads=[bK, bQ], writes=[b_pq[spq]])

                    def emit_exp2(kc0, NKC=NKC):
                        sp = SPB[(kc0 // 2) % 3]
                        span = SPANS[(kc0 // 2) % 3]
                        E, bE = c_E[(kc0 // 2) % 3], b_E[(kc0 // 2) % 3]
                        p.emit("S", lambda h, E=E, span=span: h.activation(
                            E[:], span, AF.Exp),
                            reads=[b_pq[sp], b_pq[sp + 1]], writes=[bE])

                    def emit_av(kc, Vb=Vb, bVb=bVb, ob=ob, NKC=NKC):
                        E, bE = c_E[(kc // 2) % 3], b_E[(kc // 2) % 3]
                        half = (kc % 2) * 512
                        p.emit("T", lambda h, E=E, Vb=Vb, kc=kc, ob=ob,
                               half=half: h.matmul(
                                   pq[ob][0:65, :],
                                   Vb[:, kc * 66:kc * 66 + 65],
                                   E[:, half:half + 512],
                                   start=(kc == 0),
                                   stop=(kc == NKC - 1)),
                               reads=[bE, bVb], writes=[b_pq[ob]])

                    for kc in range(4):
                        emit_qk(kc)
                    for kp in range(0, NKC, 2):
                        if kp + 4 < NKC:
                            emit_qk(kp + 4)
                        if kp + 5 < NKC:
                            emit_qk(kp + 5)
                        emit_exp2(kp)
                        emit_av(kp)
                        emit_av(kp + 1)
                    rs = qb % 2
                    p.emit("V", lambda h, rs=rs, ob=ob: h.reciprocal(
                        c_rden[rs][:], pq[ob][64:65, :]),
                        reads=[b_pq[ob]], writes=[b_rden[rs]])
                    p.emit("V", lambda h, rs=rs: h.tensor_copy(
                        c_rdb[rs][:], c_rden[rs][:]),
                        reads=[b_rden[rs]], writes=[b_rdb[rs]])
                    p.emit("T", lambda h, rs=rs, ob=ob: h.matmul(
                        pq[ob][64:128, :], c_ones_row[0:1, 0:64],
                        c_rdb[rs][:], start=True, stop=True),
                        reads=[b_rdb[rs], CONST], writes=[b_pq[ob]]),
                    p.emit("S", lambda h, rs=rs, ob=ob: h.copy(
                        c_R[rs][:], pq[ob][64:128, :]),
                        reads=[b_pq[ob]], writes=[b_R[rs]])
                    p.emit("V", lambda h, so=so, qb=qb, rs=rs, ob=ob:
                           h.tensor_tensor(so[:, ts(qb, 512)],
                                           pq[ob][0:64, :], c_R[rs][:],
                                           ALU.mult),
                           reads=[b_pq[ob], b_R[rs]], writes=[bso])
                p.emit("G", lambda h, so=so, b=b, hh=hh: h.dma_start(
                    out=h_s[b, ts(hh, 64), :], in_=so[:]),
                    reads=[bso], writes=[bh_s[b]])

        stC.close()
        p.barrier()

        stD = ExitStack()
        sbD = lambda nm, sh, dt: stD.enter_context(nc.sbuf_tensor(nm, sh, dt))
        d_x = [sbD(f"d_x{i}", [128, D], F32) for i in range(2)]
        d_s = [sbD(f"d_s{i}", [128, D], F32) for i in range(2)]
        d_g = [sbD(f"d_g{i}", [128, D], F32) for i in range(2)]
        d_f = [sbD(f"d_f{i}", [128, D], BF16) for i in range(2)]
        d_o = [sbD(f"d_o{i}", [128, D], F32) for i in range(2)]
        b_dx = [Buf(f"d_x{i}") for i in range(2)]
        b_ds = [Buf(f"d_s{i}") for i in range(2)]
        b_dg = [Buf(f"d_g{i}") for i in range(2)]
        b_df = [Buf(f"d_f{i}") for i in range(2)]
        b_do = [Buf(f"d_o{i}") for i in range(2)]

        for b in range(B if "D" in phases else 0):
            for t in range(NT):
                u = b * NT + t
                sl = u % 2
                p.emit("Y", lambda h, b=b, t=t, sl=sl: h.dma_start(
                    out=d_x[sl][:], in_=din["x"][b, ts(t, 128), :]),
                    writes=[b_dx[sl]])
                p.emit("Y", lambda h, b=b, t=t, sl=sl: h.dma_start(
                    out=d_s[sl][:].rearrange("p (c n) -> p c n", c=8),
                    in_=h_s[b].rearrange("(c p) n -> p c n", p=128)
                    [:, :, ts(t, 128)]),
                    reads=[bh_s[b]], writes=[b_ds[sl]])
                p.emit("Y", lambda h, b=b, t=t, sl=sl: h.dma_start(
                    out=d_g[sl][:].rearrange("p (c n) -> p c n", c=8),
                    in_=h_g[b].rearrange("(c p) n -> p c n", p=128)
                    [:, :, ts(t, 128)]),
                    reads=[bh_g[b][t]], writes=[b_dg[sl]])
                p.emit("G", lambda h, sl=sl: h.tensor_tensor(
                    d_f[sl][:], d_s[sl][:], d_g[sl][:], ALU.mult),
                    reads=[b_ds[sl], b_dg[sl]], writes=[b_df[sl]], gc=True)
                fns = [(lambda h, sl=sl, j=j: h.matmul(
                    pqb[1][:, ts(j, 128)], d_f[sl][:, ts(j, 128)], c_idb[:],
                    is_transpose=True, start=(j == 0), stop=(j == 7)))
                       for j in range(8)]
                p.emit("T", fns, reads=[b_df[sl], CONST], writes=[b_pq[1]])
                p.emit("V", lambda h, sl=sl: h.tensor_tensor(
                    d_o[sl][:], d_x[sl][:], pqb[1][:], ALU.add),
                    reads=[b_dx[sl], b_pq[1]], writes=[b_do[sl]])
                p.emit("G", lambda h, b=b, t=t, sl=sl: h.dma_start(
                    out=d_out[b, ts(t, 128), :], in_=d_o[sl][:]),
                    reads=[b_do[sl]])

        stD.close()
        global LAST_PROG
        LAST_PROG = p
        p.finalize(nc, st)
    return nc


def kernel(**inputs):
    consts = host_consts()
    lay = host_layouts(inputs)
    x = np.ascontiguousarray(np.asarray(inputs["L0_attn_x"], np.float32))
    wmap = dict(
        Wqkv=np.ascontiguousarray(np.asarray(inputs["L0_ffn_Wqkv"], np.float32)),
        Wo=np.ascontiguousarray(np.asarray(inputs["L0_ffn_Wo"], np.float32)),
        W1=np.ascontiguousarray(np.asarray(inputs["L0_ffn_W1"], np.float32)),
        W2=np.ascontiguousarray(np.asarray(inputs["L0_ffn_W2"], np.float32)),
        qscale=np.ascontiguousarray(np.asarray(inputs["L0_ffn_q_scale"], np.float32)),
        kscale=np.ascontiguousarray(np.asarray(inputs["L0_ffn_k_scale"], np.float32)),
        **lay,
    )
    nc = build_program()
    lower_extended_insts(nc)
    in_maps = []
    for cidx in range(NCORES):
        m = dict(x=np.ascontiguousarray(x[2 * cidx:2 * cidx + 2]))
        m.update(wmap)
        m.update(consts)
        in_maps.append(m)
    res = run_bass_kernel_spmd(nc, in_maps, core_ids=list(range(NCORES)))
    return np.concatenate([np.asarray(r["out"]) for r in res.results], axis=0)


if __name__ == "__main__":
    nc = build_program()
    n = len(nc.all_instructions() if hasattr(nc, "all_instructions") else [])
    print("built")


# version 18
# speedup vs baseline: 1.0619x; 1.0619x over previous
import numpy as np
import ml_dtypes
from contextlib import ExitStack

import concourse.bass as bass
import concourse.mybir as mybir
from concourse.bass_utils import run_bass_kernel_spmd
from concourse.library_overlay import lower_extended_insts

F32 = mybir.dt.float32
BF16 = mybir.dt.bfloat16
AF = mybir.ActivationFunctionType
ALU = mybir.AluOpType
AX = mybir.AxisListType
BFNP = ml_dtypes.bfloat16

B = 2
N = 4096
D = 1024
H = 16
DH = 64
WIN = 256
DI = 2730
NL = 6
NCORES = 8

N_LAYERS_BUILD = 6
LAST_PROG = None
DEBUG_TAPS = ()


def ts(i, sz):
    return slice(i * sz, (i + 1) * sz)


def host_consts():
    c = {}
    c["identf"] = np.eye(128, dtype=np.float32)
    c["identb"] = np.eye(128, dtype=BFNP)
    c["ones_col"] = np.ones((128, 1), dtype=BFNP)
    c["ones_row"] = np.ones((1, 128), dtype=BFNP)
    half = np.arange(0, DH, 2, dtype=np.float64)
    inv_freq = 1.0 / 10000.0 ** (half / DH)
    t = np.arange(N, dtype=np.float64)
    freqs = t[:, None] * inv_freq[None, :]
    freqs = np.concatenate([freqs, freqs], -1)
    base_scale = (half + 0.4 * DH) / (1.4 * DH)
    power = (t - N // 2) / (WIN // 2)
    scale = base_scale[None, :] ** power[:, None]
    scale = np.concatenate([scale, scale], -1)
    cos, sin = np.cos(freqs), np.sin(freqs)
    sign = np.concatenate([-np.ones(32), np.ones(32)])[None, :]
    c["tab_cq"] = (8.0 * cos * scale).astype(BFNP)
    c["tab_sq"] = (8.0 * sin * scale * sign).astype(BFNP)
    c["tab_ck"] = (cos / scale).astype(BFNP)
    c["tab_sk"] = (sin / scale * sign).astype(BFNP)
    i = np.arange(128, dtype=np.float32)
    mv = np.zeros((128, 6), dtype=np.float32)
    mv[:, 0] = i
    mv[:, 1] = i + 257
    mv[:, 2] = 0.0
    mv[:, 3] = i + 1
    mv[:, 4] = 0.0
    mv[:, 5] = i + 129
    c["maskv"] = mv
    ind = np.zeros((8, 128, 16), dtype=BFNP)
    for cc in range(8):
        for j in range(2):
            ind[cc, j * 64:(j + 1) * 64, 2 * cc + j] = 1.0
    c["indic"] = ind
    return c


def host_layouts(inputs):
    lay = {}
    b1 = np.asarray(inputs["L0_ffn_b1"], np.float32)
    cols = np.zeros((NL, 128, 44), dtype=np.float32)
    for j in range(22):
        for half in range(2):
            lo = half * 2730 + j * 128
            w = min(128, half * 2730 + 2730 - lo)
            cols[:, :w, half * 22 + j] = b1[:, lo:lo + w]
    lay["b1cols"] = np.ascontiguousarray(cols)
    for name, key in [("b2cols", "L0_ffn_b2"), ("gcols", "L0_ffn_ln_g"),
                      ("bcols", "L0_ffn_ln_b")]:
        v = np.asarray(inputs[key], np.float32)
        lay[name] = np.ascontiguousarray(v.reshape(NL, 8, 128).transpose(0, 2, 1))
    return lay


class Buf:
    __slots__ = ("name", "w", "r", "psum")

    def __init__(self, name="", psum=False):
        self.name = name
        self.w = None
        self.r = []
        self.psum = psum


class Op:
    __slots__ = ("eng", "fns", "deps", "need", "val", "idx", "skey", "dma")

    def __init__(self, eng, fns, deps, idx):
        self.eng = eng
        self.fns = fns
        self.deps = deps
        self.need = False
        self.val = None
        self.idx = idx
        self.dma = False


class Prog:
    ENGS = ("T", "V", "S", "Y", "G")

    NLANES = 8

    def __init__(self):
        self.ops = []
        self._bar = []
        self._bar_seen = set()
        self._lane_rr = {"Y": 0, "G": 0}
        self._lane_ctr = {}
        self._last_stream = {}

    def emit(self, eng, fns, reads=(), writes=(), deps=(), gc=False):
        if not isinstance(fns, (list, tuple)):
            fns = [fns]
        dset = []
        for b in reads:
            if b.w is not None:
                dset.append(b.w)
            if b.psum:
                dset.extend(o for o in b.r if o.eng != eng)
        for b in writes:
            dset.extend(b.r)
            if b.w is not None:
                dset.append(b.w)
        for d in deps:
            if d is not None:
                dset.append(d)
        if eng not in self._bar_seen:
            dset.extend(self._bar)
            self._bar_seen.add(eng)
        op = Op(eng, list(fns), dset, len(self.ops))
        if eng in ("Y", "G") and not gc:
            op.dma = True
            lane = self._lane_rr[eng]
            self._lane_rr[eng] = (lane + 1) % self.NLANES
            op.skey = (eng, lane)
            self._lane_ctr[op.skey] = self._lane_ctr.get(op.skey, 0) + 16
            op.val = self._lane_ctr[op.skey]
            op.need = True
            self._last_stream[op.skey] = op
        else:
            op.skey = None
            self._last_stream[("P" if eng == "G" else eng, 0)] = op
        self.ops.append(op)
        for b in reads:
            b.r.append(op)
        for b in writes:
            b.w = op
            b.r = []
        return op

    def barrier(self):
        self._bar = list(self._last_stream.values())
        self._bar_seen = set()

    def finalize(self, nc, st):
        for op in self.ops:
            for d in op.deps:
                d.need = True
        skey_ctr = dict(self._lane_ctr)
        for op in self.ops:
            if op.skey is not None:
                continue
            if op.need:
                op.skey = ("P" if op.eng == "G" else op.eng, 0)
                skey_ctr[op.skey] = skey_ctr.get(op.skey, 0) + 1
                op.val = skey_ctr[op.skey]
        sems = {k: st.enter_context(
            nc.semaphore(f"sem{k[0]}{k[1]}")) for k in skey_ctr}
        per_eng = {e: [o for o in self.ops if o.eng == e] for e in self.ENGS}
        total = dict(skey_ctr)
        block = st.enter_context(nc.Block())

        def run(eng_name, handle):
            waited = {k: 0 for k in sems}
            for op in per_eng[eng_name]:
                tgt = {}
                for d in op.deps:
                    if d.val is None:
                        continue
                    if d.val > tgt.get(d.skey, 0):
                        tgt[d.skey] = d.val
                if op.dma:
                    prev = op.val - 16
                    if prev > tgt.get(op.skey, 0):
                        tgt[op.skey] = prev
                for k, v in sorted(tgt.items()):
                    if v > waited[k]:
                        handle.wait_ge(sems[k], v)
                        waited[k] = v
                instr = None
                for fn in op.fns:
                    instr = fn(handle)
                if op.need or op.dma:
                    instr.then_inc(sems[op.skey], 16 if op.dma else 1)
            if eng_name == "G":
                for k in sorted(total):
                    if k[0] in ("Y", "G"):
                        handle.wait_ge(sems[k], total[k])

        @block.sync
        def _(h):
            run("Y", h)

        @block.vector
        def _(h):
            run("V", h)

        @block.scalar
        def _(h):
            run("S", h)

        @block.tensor
        def _(h):
            run("T", h)

        @block.gpsimd
        def _(h):
            run("G", h)

        return total


def build_program(n_layers=N_LAYERS_BUILD, taps=DEBUG_TAPS, phases="ACD"):
    import os
    SUBA = int(os.environ.get("SUBA", "4"))
    nc = bass.Bass("TRN2", target_bir_lowering=False)
    consts = host_consts()

    din = {}
    din["x"] = nc.dram_tensor("x", [B, N, D], F32, kind="ExternalInput").ap()
    for nm, sh in [("Wqkv", [NL, D, 3 * D]), ("Wo", [NL, D, D]),
                   ("W1", [NL, D, 5460]), ("W2", [NL, DI, D]),
                   ("qscale", [NL, DH]), ("kscale", [NL, DH]),
                   ("b1cols", [NL, 128, 44]), ("b2cols", [NL, 128, 8]),
                   ("gcols", [NL, 128, 8]), ("bcols", [NL, 128, 8])]:
        din[nm] = nc.dram_tensor(nm, sh, F32, kind="ExternalInput").ap()
    for nm in ["identf", "maskv"]:
        din[nm] = nc.dram_tensor(nm, list(consts[nm].shape), F32,
                                 kind="ExternalInput").ap()
    for nm in ["identb", "ones_col", "ones_row", "tab_cq", "tab_sq",
               "tab_ck", "tab_sk", "indic"]:
        din[nm] = nc.dram_tensor(nm, list(consts[nm].shape), BF16,
                                 kind="ExternalInput").ap()
    d_out = nc.dram_tensor("out", [B, N, D], F32, kind="ExternalOutput").ap()

    def dscratch(name, shape, dt):
        kind = "ExternalOutput" if name in taps else "Internal"
        return nc.dram_tensor(name, shape, dt, kind=kind).ap()

    h_resid = dscratch("resid", [B, D, N], F32)
    h_xnT = dscratch("xnT", [B, D, N], BF16)
    h_s = dscratch("s", [B, D, N], F32)
    h_g = dscratch("g", [B, D, N], F32)

    p = Prog()
    st = ExitStack()
    with st:
        sb = lambda nm, sh, dt: st.enter_context(nc.sbuf_tensor(nm, sh, dt))

        c_idb = sb("c_idb", [128, 128], BF16)
        c_ones_col = sb("c_ones_col", [128, 1], BF16)
        c_ones_row = sb("c_ones_row", [1, 128], BF16)
        c_maskv = sb("c_maskv", [128, 6], F32)
        c_indic = sb("c_indic", [128, 8 * 16], BF16)
        g_negmu = sb("g_negmu", [128, B * 8], F32)

        CONST = Buf("const")
        b_negmu = Buf("negmu")
        NTt = N // 128
        bh_resid = [[Buf(f"h_resid{b}_{t}") for t in range(NTt)]
                    for b in range(B)]
        bh_xnT = [[Buf(f"h_xnT{b}_{t}") for t in range(NTt)] for b in range(B)]
        bh_s = [Buf(f"h_s{b}") for b in range(B)]
        bh_g = [[Buf(f"h_g{b}_{t}") for t in range(NTt)] for b in range(B)]

        pq_t = ([st.enter_context(nc.psum_tensor("pq01", [128, 1024], F32))]
                + [st.enter_context(nc.psum_tensor("pq23", [128, 1024], F32))]
                + [st.enter_context(nc.psum_tensor("pq45", [128, 1024], F32))]
                + [st.enter_context(nc.psum_tensor(f"pq{i}", [128, 512], F32))
                   for i in range(6, 8)])
        pq01 = pq_t[0][:]
        pq23 = pq_t[1][:]
        pq45 = pq_t[2][:]
        pq = ([pq01[:, 0:512], pq01[:, 512:1024]]
              + [pq23[:, 0:512], pq23[:, 512:1024]]
              + [pq45[:, 0:512], pq45[:, 512:1024]]
              + [pq_t[3][:], pq_t[4][:]])
        pqb = [pq[i].bitcast(BF16) for i in range(8)]
        b_pq = [Buf(f"pq{i}", psum=True) for i in range(8)]

        def load_const(dst_ap, src_ap):
            p.emit("Y", lambda h, d=dst_ap, s=src_ap: h.dma_start(out=d, in_=s),
                   writes=[CONST])

        load_const(c_idb[:], din["identb"])
        load_const(c_ones_col[:], din["ones_col"])
        load_const(c_ones_row[:], din["ones_row"])
        load_const(c_maskv[:], din["maskv"])
        load_const(c_indic[:].rearrange("p (c j) -> p c j", c=8),
                   din["indic"].rearrange("c p j -> p c j"))

        stIdf = ExitStack()
        c_idf = stIdf.enter_context(nc.sbuf_tensor("c_idf", [128, 128], F32))
        load_const(c_idf[:], din["identf"])
        stA = ExitStack()
        sbA = lambda nm, sh, dt: stA.enter_context(nc.sbuf_tensor(nm, sh, dt))
        a_x = [sbA(f"a_x{i}", [128, D], F32) for i in range(3)]
        a_stat = [sbA(f"a_stat{i}", [128, 8], F32) for i in range(2)]
        a_junk = sbA("a_junk", [128, D], BF16)
        a_xnf = [sbA(f"a_xnf{i}", [128, D], F32) for i in range(2)]
        a_xnb = [sbA(f"a_xnb{i}", [128, D], BF16) for i in range(2)]
        a_xtf = [sbA(f"a_xtf{i}", [128, D], F32) for i in range(2)]
        a_xtb = [sbA(f"a_xtb{i}", [128, D], BF16) for i in range(2)]
        a_murow = sbA("a_murow", [1, D], F32)
        a_mub = sbA("a_mub", [1, D], BF16)
        b_ax = [Buf(f"a_x{i}") for i in range(3)]
        b_st = [Buf(f"a_stat{i}") for i in range(2)]
        b_junk = Buf("a_junk")
        b_xnf = [Buf(f"a_xnf{i}") for i in range(2)]
        b_xnb = [Buf(f"a_xnb{i}") for i in range(2)]
        b_xtf = [Buf(f"a_xtf{i}") for i in range(2)]
        b_xtb = [Buf(f"a_xtb{i}") for i in range(2)]
        b_mur = Buf("a_murow")

        NT = N // 128
        for b in range(B):
            for t in range(NT):
                u = b * NT + t
                xa, bxa = a_x[u % 3], b_ax[u % 3]
                stt, bst = a_stat[u % 2], b_st[u % 2]
                p.emit("Y", lambda h, xa=xa, b=b, t=t: h.dma_start(
                    out=xa[:], in_=din["x"][b, ts(t, 128), :]),
                    writes=[bxa])
                p.emit("V", lambda h, stt=stt, xa=xa: h.reduce_sum(
                    stt[:, 0:1], xa[:], AX.X, negate=True),
                    reads=[bxa], writes=[bst])
                p.emit("V", lambda h, stt=stt: h.tensor_scalar(
                    stt[:, 1:2], stt[:, 0:1], 1.0 / D, None, ALU.mult),
                    reads=[bst], writes=[bst])
                p.emit("S", lambda h, stt=stt, xa=xa: h.activation(
                    a_junk[:], xa[:], AF.Square, bias=stt[:, 1:2],
                    accum_out=stt[:, 2:3]),
                    reads=[bxa, bst], writes=[b_junk, bst])
                p.emit("V", lambda h, stt=stt: h.tensor_scalar(
                    stt[:, 3:4], stt[:, 2:3], 1.0 / D, 1e-5, ALU.mult, ALU.add),
                    reads=[bst], writes=[bst])
                p.emit("V", lambda h, stt=stt: h.reciprocal(
                    stt[:, 4:5], stt[:, 3:4]), reads=[bst], writes=[bst])
                p.emit("S", lambda h, stt=stt: h.activation(
                    stt[:, 5:6], stt[:, 4:5], AF.Sqrt),
                    reads=[bst], writes=[bst])
                xnf, bxnf = a_xnf[u % 2], b_xnf[u % 2]
                xnb, bxnb = a_xnb[u % 2], b_xnb[u % 2]
                p.emit("V", lambda h, xnf=xnf, xa=xa, stt=stt: h.tensor_scalar(
                    xnf[:], xa[:], stt[:, 1:2], stt[:, 5:6], ALU.add, ALU.mult),
                    reads=[bxa, bst], writes=[bxnf])
                p.emit("V", lambda h, xnb=xnb, xnf=xnf: h.tensor_copy(
                    xnb[:], xnf[:]), reads=[bxnf], writes=[bxnb])
                for half in range(2 if SUBA >= 2 else 0):
                    fns = [(lambda h, xnf=xnf, j=j, cc=half * 4 + j, hf=half:
                            h.matmul(pq[hf][:, ts(j, 128)], xnf[:, ts(cc, 128)],
                                     c_idf[:], is_transpose=True,
                                     start=(j == 0), stop=(j == 3)))
                           for j in range(4)]
                    p.emit("T", fns, reads=[bxnf, CONST], writes=[b_pq[half]])
                xtf, bxtf = a_xtf[u % 2], b_xtf[u % 2]
                xtb, bxtb = a_xtb[u % 2], b_xtb[u % 2]
                if SUBA < 2:
                    p.emit("G", lambda h, xnf=xnf, b=b, t=t: h.dma_start(
                        out=h_resid[b, ts(t, 128), 0:N], in_=xnf[:, 0:N]),
                        reads=[bxnf], writes=[bh_resid[b]])
                    continue
                for half in range(2):
                    p.emit("V", lambda h, xtf=xtf, hf=half: h.tensor_copy(
                        xtf[:].rearrange("p (c n) -> p c n", c=8)
                        [:, 4 * hf:4 * hf + 4, :],
                        pq[hf][:].rearrange("p (c n) -> p c n", c=4)),
                        reads=[b_pq[half]], writes=[bxtf])
                    if SUBA >= 3:
                        p.emit("S", lambda h, xtb=xtb, hf=half: h.copy(
                            xtb[:].rearrange("p (c n) -> p c n", c=8)
                            [:, 4 * hf:4 * hf + 4, :],
                            pq[hf][:].rearrange("p (c n) -> p c n", c=4)),
                            reads=[b_pq[half]], writes=[bxtb])
                if SUBA >= 4:
                    p.emit("T", lambda h, xnb=xnb, t=t: h.matmul(
                        pq[6][0:1, :], c_ones_col[:], xnb[:, 0:512],
                        start=(t == 0), stop=(t == NT - 1)),
                        reads=[bxnb, CONST], writes=[b_pq[6]])
                    p.emit("T", lambda h, xnb=xnb, t=t: h.matmul(
                        pq[7][0:1, :], c_ones_col[:], xnb[:, 512:1024],
                        start=(t == 0), stop=(t == NT - 1)),
                        reads=[bxnb, CONST], writes=[b_pq[7]])
                p.emit("G", lambda h, xtf=xtf, b=b, t=t: h.dma_start(
                    out=h_resid[b].rearrange("(c p) n -> p c n", p=128)
                    [:, :, ts(t, 128)],
                    in_=xtf[:].rearrange("p (c n) -> p c n", c=8)),
                    reads=[bxtf], writes=[bh_resid[b][t]])
                if n_layers == 0:
                    p.emit("G", lambda h, xtf=xtf, b=b, t=t: h.dma_start(
                        out=h_g[b].rearrange("(c p) n -> p c n", p=128)
                        [:, :, ts(t, 128)],
                        in_=xtf[:].rearrange("p (c n) -> p c n", c=8)),
                        reads=[bxtf], writes=[bh_g[b][t]])
                if SUBA >= 3 and SUBA != 5:
                    p.emit("G", lambda h, xtb=xtb, b=b, t=t: h.dma_start(
                        out=h_xnT[b].rearrange("(c p) n -> p c n", p=128)
                        [:, :, ts(t, 128)],
                        in_=xtb[:].rearrange("p (c n) -> p c n", c=8)),
                        reads=[bxtb], writes=[bh_xnT[b][t]])
            if SUBA < 4:
                continue
            p.emit("V", lambda h: h.tensor_scalar(
                a_murow[:, 0:512], pq[6][0:1, :], 1.0 / N, None, ALU.mult),
                reads=[b_pq[6]], writes=[b_mur])
            p.emit("V", lambda h: h.tensor_scalar(
                a_murow[:, 512:1024], pq[7][0:1, :], 1.0 / N, None, ALU.mult),
                reads=[b_pq[7]], writes=[b_mur])
            p.emit("V", lambda h: h.tensor_copy(a_mub[:], a_murow[:]),
                   reads=[b_mur], writes=[b_mur])
            fns = [(lambda h, j=j: h.matmul(
                pqb[0][:, 2 * j:2 * j + 1], a_mub[:, ts(j, 128)],
                c_idb[0:1, 0:1], is_transpose=True, start=(j == 0),
                stop=(j == 7)))
                   for j in range(8)]
            p.emit("T", fns, reads=[b_mur, CONST], writes=[b_pq[0]])
            p.emit("V", lambda h, b=b: h.tensor_scalar(
                g_negmu[:, b * 8:(b + 1) * 8],
                pqb[0][:, 0:16].rearrange("p (a t) -> p a t", t=2)[:, :, 0:1],
                -1.0, None, ALU.mult), reads=[b_pq[0]], writes=[b_negmu])

        stA.close()
        stIdf.close()
        p.barrier()


        NBLK = N // 256
        NT2 = N // 128
        NFT = N // 512
        for li in range(n_layers):
            last_layer = (li == n_layers - 1)
            stP = ExitStack()
            sbP = lambda nm, sh, dt: stP.enter_context(
                nc.sbuf_tensor(f"L{li}a_" + nm, sh, dt))
            w_qkv = sbP("wqkv", [128, 8 * 3072], BF16)
            w_o = sbP("wo", [128, 8 * 1024], BF16)
            stage = [sbP("stage0", [128, 3072], F32)]
            stageb = [stage[0][:].bitcast(BF16)]
            tabs = {nm: sbP(nm, [128, (N // 128) * 64], BF16)
                    for nm in ["tqc", "tqs", "tkc", "tks"]}
            scl = sbP("scl", [1, 64], F32)
            sclb = sbP("sclb", [1, 64], BF16)
            qsb = sbP("qsb", [128, 4 * 64], BF16)
            xr = [sbP(f"xr{i}", [128, 8 * 256], F32) for i in range(2)]
            halo = sbP("halo", [128, 4], F32)
            xs = [sbP(f"xs{i}", [128, 8 * 256], BF16) for i in range(2)]
            qraw = [sbP(f"qraw{i}", [128, 1024], BF16) for i in range(2)]
            kraw = [sbP(f"kraw{i}", [128, 1024], BF16) for i in range(2)]
            kn = [sbP(f"kn{i}", [128, 1024], BF16) for i in range(2)]
            rt1 = sbP("rt1", [128, 1024], BF16)
            rt2 = sbP("rt2", [128, 1024], BF16)
            qrot = [sbP(f"qrot{i}", [128, 1024], BF16) for i in range(2)]
            krot = [sbP(f"krot{i}", [128, 1024], BF16) for i in range(2)]
            sqb = sbP("sqb", [128, 1024], BF16)
            n2 = [sbP(f"n2{i}", [128, 32], F32) for i in range(4)]
            rr = [sbP(f"rr{i}", [128, 32], F32) for i in range(4)]
            qt = [sbP(f"qt{i}", [128, 1024], BF16) for i in range(4)]
            kt = [sbP(f"kt{i}", [128, 1024], BF16) for i in range(6)]
            vb = [sbP(f"vb{i}", [128, 1024], BF16) for i in range(6)]
            sm = [sbP(f"sm{i}", [128, 384], F32) for i in range(3)]
            bb = [sbP(f"bb{i}", [128, 4], F32) for i in range(3)]
            eE = [sbP(f"eE{i}", [128, 384], BF16) for i in range(3)]
            eN = [sbP(f"eN{i}", [128, 384], BF16) for i in range(3)]
            eT = [sbP(f"eT{i}", [128, 384], BF16) for i in range(3)]
            aot = [sbP(f"aot{i}", [128, 8 * 256], BF16) for i in range(2)]

            B_ = Buf
            b_wq, b_wo = B_("wq"), B_("wo")
            b_stage = [B_("st0")]
            b_tabs = {k: B_(k) for k in tabs}
            b_scl, b_sclb, b_qsb = B_("scl"), B_("sclb"), B_("qsb")
            b_xr = [B_(f"xr{i}") for i in range(2)]
            b_halo = B_("halo")
            b_xs = [B_(f"xs{i}") for i in range(2)]
            b_qraw = [B_(f"qraw{i}") for i in range(2)]
            b_kraw = [B_(f"kraw{i}") for i in range(2)]
            b_kn = [B_(f"kn{i}") for i in range(2)]
            b_rt1, b_rt2 = B_("rt1"), B_("rt2")
            b_qrot = [B_(f"qrot{i}") for i in range(2)]
            b_krot = [B_(f"krot{i}") for i in range(2)]
            b_sqb = B_("sqb")
            b_n2 = [B_(f"n2{i}") for i in range(4)]
            b_rr = [B_(f"rr{i}") for i in range(4)]
            b_qt = [B_(f"qt{i}") for i in range(4)]
            b_kt = [B_(f"kt{i}") for i in range(6)]
            b_vb = [B_(f"vb{i}") for i in range(6)]
            b_sm = [B_(f"sm{i}") for i in range(3)]
            b_bb = [B_(f"bb{i}") for i in range(3)]
            b_eE = [B_(f"eE{i}") for i in range(3)]
            b_eN = [B_(f"eN{i}") for i in range(3)]
            b_eT = [B_(f"eT{i}") for i in range(3)]
            b_aot = [B_(f"aot{i}") for i in range(2)]

            for dc in range(8):
                sg, bsg = stage[0], b_stage[0]
                p.emit("Y", lambda h, sg=sg, li=li, dc=dc: h.dma_start(
                    out=sg[:], in_=din["Wqkv"][li, ts(dc, 128), :]),
                    writes=[bsg])
                p.emit("V", lambda h, sg=sg, dc=dc: h.tensor_copy(
                    w_qkv[:, dc * 3072:(dc + 1) * 3072], sg[:]),
                    reads=[bsg], writes=[b_wq])
            for dc in range(8):
                sg, bsg = stage[0], b_stage[0]
                p.emit("Y", lambda h, sg=sg, li=li, dc=dc: h.dma_start(
                    out=sg[:, 0:1024], in_=din["Wo"][li, ts(dc, 128), :]),
                    writes=[bsg])
                p.emit("V", lambda h, sg=sg, dc=dc: h.tensor_copy(
                    w_o[:, dc * 1024:(dc + 1) * 1024], sg[:, 0:1024]),
                    reads=[bsg], writes=[b_wo])
            p.emit("Y", lambda h, li=li: h.dma_start(
                out=scl[:], in_=din["qscale"][li].rearrange("(a d) -> a d", a=1)),
                writes=[b_scl])
            p.emit("V", lambda h: h.tensor_copy(sclb[:], scl[:]),
                   reads=[b_scl], writes=[b_sclb])
            fns = [lambda h: h.matmul(pq[2][:, 0:64], c_ones_row[:], sclb[:],
                                      start=True, stop=False),
                   lambda h: h.matmul(pq[2][:, 64:96], c_ones_row[:],
                                      sclb[:, 32:64], start=False, stop=False),
                   lambda h: h.matmul(pq[2][:, 96:128], c_ones_row[:],
                                      sclb[:, 0:32], start=False, stop=True)]
            p.emit("T", fns, reads=[b_sclb, CONST], writes=[b_pq[2]])
            p.emit("V", lambda h: h.tensor_copy(qsb[:, 0:128], pq[2][:, 0:128]),
                   reads=[b_pq[2]], writes=[b_qsb])
            p.emit("Y", lambda h, li=li: h.dma_start(
                out=scl[:], in_=din["kscale"][li].rearrange("(a d) -> a d", a=1)),
                writes=[b_scl])
            p.emit("V", lambda h: h.tensor_copy(sclb[:], scl[:]),
                   reads=[b_scl], writes=[b_sclb])
            fns = [lambda h: h.matmul(pq[2][:, 0:64], c_ones_row[:], sclb[:],
                                      start=True, stop=False),
                   lambda h: h.matmul(pq[2][:, 64:96], c_ones_row[:],
                                      sclb[:, 32:64], start=False, stop=False),
                   lambda h: h.matmul(pq[2][:, 96:128], c_ones_row[:],
                                      sclb[:, 0:32], start=False, stop=True)]
            p.emit("T", fns, reads=[b_sclb, CONST], writes=[b_pq[2]])
            p.emit("V", lambda h: h.tensor_copy(qsb[:, 128:256], pq[2][:, 0:128]),
                   reads=[b_pq[2]], writes=[b_qsb])
            for nm, base, qoff in [("tqc", "tab_cq", 0), ("tqs", "tab_sq", 64),
                                   ("tkc", "tab_ck", 128), ("tks", "tab_sk", 192)]:
                sg, bsg = stage[0], b_stage[0]
                p.emit("Y", lambda h, sg=sg, base=base: h.dma_start(
                    out=sg[:].bitcast(BF16)[:, 0:(N // 128) * 64]
                    .rearrange("p (t d) -> p t d", t=N // 128),
                    in_=din[base].rearrange("(t p) d -> p t d", p=128)),
                    writes=[bsg])
                p.emit("V", lambda h, nm=nm, sg=sg, qoff=qoff: h.tensor_tensor(
                    tabs[nm][:].rearrange("p (t d) -> p t d", t=N // 128),
                    sg[:].bitcast(BF16)[:, 0:(N // 128) * 64]
                    .rearrange("p (t d) -> p t d", t=N // 128),
                    qsb[:, qoff:qoff + 64].unsqueeze(1)
                    .broadcast_to((128, N // 128, 64)),
                    ALU.mult), reads=[bsg, b_qsb], writes=[b_tabs[nm]])

            def a_s1_gen(b, w, wg):
                xrw, bxrw = xr[wg % 2], b_xr[wg % 2]
                xsw, bxsw = xs[wg % 2], b_xs[wg % 2]
                p.emit("Y", lambda h, b=b, w=w, xrw=xrw: h.dma_start(
                    out=xrw[:].rearrange("p (c n) -> p c n", c=8),
                    in_=h_resid[b].rearrange("(c p) n -> p c n", p=128)
                    [:, :, ts(w, 256)]),
                    reads=[bh_resid[b][2 * w], bh_resid[b][2 * w + 1]],
                    writes=[bxrw])
                p.emit("V", lambda h, xsw=xsw, xrw=xrw: h.tensor_copy(
                    xsw[:].rearrange("p (c n) -> p c n", c=8)[:, 0:4, :],
                    xrw[:].rearrange("p (c n) -> p c n", c=8)[:, 0:4, :]),
                    reads=[bxrw], writes=[bxsw])
                p.emit("V", lambda h, xsw=xsw, xrw=xrw: h.tensor_copy(
                    xsw[:].rearrange("p (c n) -> p c n", c=8)[:, 4:8, 1:256],
                    xrw[:].rearrange("p (c n) -> p c n", c=8)[:, 4:8, 0:255]),
                    reads=[bxrw], writes=[bxsw])
                if w == 0:
                    p.emit("V", lambda h, xsw=xsw: h.memset(
                        xsw[:].rearrange("p (c n) -> p c n", c=8)
                        [:, 4:8, 0:1], 0.0), writes=[bxsw])
                else:
                    p.emit("V", lambda h, xsw=xsw: h.tensor_copy(
                        xsw[:].rearrange("p (c n) -> p c n", c=8)
                        [:, 4:8, 0:1],
                        halo[:].rearrange("p (c n) -> p c n", c=4)),
                        reads=[b_halo], writes=[bxsw])
                p.emit("V", lambda h, xrw=xrw: h.tensor_copy(
                    halo[:].rearrange("p (c n) -> p c n", c=4),
                    xrw[:].rearrange("p (c n) -> p c n", c=8)
                    [:, 4:8, 255:256]),
                    reads=[bxrw], writes=[b_halo])
                yield
                for tile in range(2):
                    u2g = 2 * wg + tile
                    sl2 = u2g % 2
                    sl4 = u2g % 4
                    u2 = 2 * w + tile
                    qr_, kr_ = qraw[sl2], kraw[sl2]
                    for slab in range(6):
                        bank = slab % 2
                        fns = [(lambda h, xsw=xsw, dc=dc, slab=slab,
                                tile=tile, bank=bank: h.matmul(
                                    pq[bank][:],
                                    xsw[:, dc * 256 + tile * 128:
                                        dc * 256 + tile * 128 + 128],
                                    w_qkv[:, dc * 3072 + slab * 512:
                                          dc * 3072 + (slab + 1) * 512],
                                    start=(dc == 0), stop=(dc == 7)))
                               for dc in range(8)]
                        p.emit("T", fns, reads=[bxsw, b_wq],
                               writes=[b_pq[bank]])
                        if slab < 2:
                            p.emit("S", lambda h, qr_=qr_, slab=slab,
                                   bank=bank: h.copy(
                                       qr_[:, slab * 512:(slab + 1) * 512],
                                       pq[bank][:]),
                                   reads=[b_pq[bank]], writes=[b_qraw[sl2]])
                        elif slab < 4:
                            p.emit("S", lambda h, kr_=kr_, slab=slab,
                                   bank=bank: h.copy(
                                       kr_[:, (slab - 2) * 512:
                                           (slab - 1) * 512],
                                       pq[bank][:]),
                                   reads=[b_pq[bank]], writes=[b_kraw[sl2]])
                        else:
                            vslot = u2g % 6
                            p.emit("S", lambda h, vslot=vslot, slab=slab,
                                   bank=bank: h.copy(
                                       vb[vslot][:, (slab - 4) * 512:
                                                 (slab - 3) * 512],
                                       pq[bank][:]),
                                   reads=[b_pq[bank]], writes=[b_vb[vslot]])
                        yield
                    p.emit("V", lambda h, qr_=qr_: h.tensor_tensor(
                        sqb[:], qr_[:], qr_[:], ALU.mult),
                        reads=[b_qraw[sl2]], writes=[b_sqb])
                    p.emit("V", lambda h, sl4=sl4: h.tensor_reduce(
                        n2[sl4][:, 0:16],
                        sqb[:].rearrange("p (g d) -> p g d", g=16),
                        AX.X, ALU.add), reads=[b_sqb], writes=[b_n2[sl4]])
                    p.emit("V", lambda h, kr_=kr_: h.tensor_tensor(
                        sqb[:], kr_[:], kr_[:], ALU.mult),
                        reads=[b_kraw[sl2]], writes=[b_sqb])
                    p.emit("V", lambda h, sl4=sl4: h.tensor_reduce(
                        n2[sl4][:, 16:32],
                        sqb[:].rearrange("p (g d) -> p g d", g=16),
                        AX.X, ALU.add), reads=[b_sqb], writes=[b_n2[sl4]])
                    p.emit("V", lambda h, sl4=sl4: h.tensor_scalar(
                        n2[sl4][:], n2[sl4][:], 1e-24, None, ALU.max),
                        reads=[b_n2[sl4]], writes=[b_n2[sl4]])
                    p.emit("S", lambda h, sl4=sl4: h.activation(
                        n2[sl4][:], n2[sl4][:], AF.Ln),
                        reads=[b_n2[sl4]], writes=[b_n2[sl4]])
                    p.emit("S", lambda h, sl4=sl4: h.activation(
                        rr[sl4][:], n2[sl4][:], AF.Exp, scale=-0.5),
                        reads=[b_n2[sl4]], writes=[b_rr[sl4]])
                    yield
                    p.emit("V", lambda h, sl2=sl2, sl4=sl4, kr_=kr_:
                           h.tensor_tensor(
                               kn[sl2][:].rearrange(
                                   "p (g d) -> p g d", g=16),
                               kr_[:].rearrange("p (g d) -> p g d", g=16),
                               rr[sl4][:, 16:32].unsqueeze(2)
                               .broadcast_to((128, 16, 64)), ALU.mult),
                           reads=[b_kraw[sl2], b_rr[sl4]],
                           writes=[b_kn[sl2]])
                    yield
                    for (srcbuf, bsrc, dstbuf, bdst, tc, tsn) in [
                            (qr_, b_qraw[sl2], qrot[sl2], b_qrot[sl2],
                             "tqc", "tqs"),
                            (kn[sl2], b_kn[sl2], krot[sl2], b_krot[sl2],
                             "tkc", "tks")]:
                        p.emit("V", lambda h, srcbuf=srcbuf, tc=tc, u2=u2:
                               h.tensor_tensor(
                                   rt1[:].rearrange("p (g d) -> p g d", g=16),
                                   srcbuf[:].rearrange("p (g d) -> p g d", g=16),
                                   tabs[tc][:, ts(u2, 64)].unsqueeze(1)
                                   .broadcast_to((128, 16, 64)), ALU.mult),
                               reads=[bsrc, b_tabs[tc]], writes=[b_rt1])
                        p.emit("V", lambda h, srcbuf=srcbuf, tsn=tsn, u2=u2:
                               h.tensor_tensor(
                                   rt2[:].rearrange("p (g d) -> p g d", g=16)
                                   [:, :, 0:32],
                                   srcbuf[:].rearrange("p (g d) -> p g d", g=16)
                                   [:, :, 32:64],
                                   tabs[tsn][:, u2 * 64:u2 * 64 + 32]
                                   .unsqueeze(1).broadcast_to((128, 16, 32)),
                                   ALU.mult),
                               reads=[bsrc, b_tabs[tsn]], writes=[b_rt2])
                        p.emit("V", lambda h, srcbuf=srcbuf, tsn=tsn, u2=u2:
                               h.tensor_tensor(
                                   rt2[:].rearrange("p (g d) -> p g d", g=16)
                                   [:, :, 32:64],
                                   srcbuf[:].rearrange("p (g d) -> p g d", g=16)
                                   [:, :, 0:32],
                                   tabs[tsn][:, u2 * 64 + 32:u2 * 64 + 64]
                                   .unsqueeze(1).broadcast_to((128, 16, 32)),
                                   ALU.mult),
                               reads=[bsrc, b_tabs[tsn]], writes=[b_rt2])
                        p.emit("V", lambda h, dstbuf=dstbuf: h.tensor_tensor(
                            dstbuf[:], rt1[:], rt2[:], ALU.add),
                            reads=[b_rt1, b_rt2], writes=[bdst])
                        yield
                    for (srcb, bsrc, dstb, bdst, bankj) in [
                            (qrot[sl2], b_qrot[sl2], qt[sl4], b_qt[sl4], 2),
                            (krot[sl2], b_krot[sl2], kt[u2g % 6],
                             b_kt[u2g % 6], 3)]:
                        fns = [(lambda h, srcb=srcb, j=j, bankj=bankj:
                                h.matmul(
                                    pqb[bankj][:, ts(j, 128)],
                                    srcb[:, ts(j, 128)], c_idb[:],
                                    is_transpose=True, start=(j == 0),
                                    stop=(j == 7))) for j in range(8)]
                        p.emit("T", fns, reads=[bsrc, CONST],
                               writes=[b_pq[bankj]])
                        p.emit("V", lambda h, dstb=dstb, bankj=bankj:
                               h.tensor_copy(dstb[:], pqb[bankj][:]),
                               reads=[b_pq[bankj]], writes=[bdst])
                        yield

            def a_s2(b, w, wg, feeder):
                xrw, bxrw = xr[wg % 2], b_xr[wg % 2]
                aotw, baotw = aot[wg % 2], b_aot[wg % 2]
                for tile in range(2):
                    u2g = 2 * wg + tile
                    sl4 = u2g % 4
                    u2 = 2 * w + tile
                    s_idx = u2
                    nk = min(3, s_idx + 1)
                    if s_idx == 0:
                        msc, mec = 2, 3
                    elif s_idx == 1:
                        msc, mec = 4, 5
                    else:
                        msc, mec = 0, 1
                    slots = list(range(u2g - nk + 1, u2g + 1))

                    def unit_qk(hh, sl4=sl4, slots=slots):
                        sb_ = 4 + (hh % 2)
                        hrow = (hh % 2) * 64
                        hcol = hh // 2
                        ns_ = len(slots)
                        fns = [(lambda h, j=j, slot=slot, sb_=sb_,
                                hrow=hrow, hcol=hcol, sl4=sl4, ns_=ns_:
                                h.matmul(
                                    pq[sb_][:, ts(j, 128)],
                                    qt[sl4][hrow:hrow + 64,
                                            hcol * 128:(hcol + 1) * 128],
                                    kt[slot % 6][hrow:hrow + 64,
                                                 hcol * 128:(hcol + 1) * 128],
                                    start=(j == 0), stop=(j == ns_ - 1)))
                               for j, slot in enumerate(slots)]
                        kt_reads = [b_kt[slot % 6] for slot in slots]
                        p.emit("T", fns, reads=[b_qt[sl4]] + kt_reads,
                               writes=[b_pq[sb_]])

                    def st_A(hh):
                        un = hh % 3
                        sb_ = 4 + (hh % 2)
                        p.emit("V", lambda h, un=un, sb_=sb_, nk=nk,
                               msc=msc, mec=mec: h.tensor_mask_reduce(
                                   sm[un][:, 0:nk * 128],
                                   pq[sb_][:, 0:nk * 128],
                                   c_maskv[:, msc:msc + 1],
                                   c_maskv[:, mec:mec + 1], 1.0, -3.0e38,
                                   ALU.max, accum_out=bb[un][:, 0:1]),
                               reads=[b_pq[sb_], CONST],
                               writes=[b_sm[un], b_bb[un]])
                        p.emit("V", lambda h, un=un, sl4=sl4, hh=hh:
                               h.scalar_tensor_tensor(
                                   bb[un][:, 1:2], bb[un][:, 0:1], -1.0,
                                   rr[sl4][:, hh:hh + 1], ALU.mult,
                                   ALU.mult),
                               reads=[b_bb[un], b_rr[sl4]],
                               writes=[b_bb[un]])

                    def st_B(hh):
                        un = hh % 3
                        p.emit("S", lambda h, un=un, nk=nk, sl4=sl4, hh=hh:
                               h.activation(
                                   eE[un][:, 0:nk * 128],
                                   sm[un][:, 0:nk * 128], AF.Exp,
                                   bias=bb[un][:, 1:2],
                                   scale=rr[sl4][:, hh:hh + 1],
                                   accum_out=bb[un][:, 2:3]),
                               reads=[b_sm[un], b_bb[un], b_rr[sl4]],
                               writes=[b_eE[un], b_bb[un]])

                    def st_C(hh):
                        un = hh % 3
                        p.emit("V", lambda h, un=un: h.reciprocal(
                            bb[un][:, 3:4], bb[un][:, 2:3]),
                            reads=[b_bb[un]], writes=[b_bb[un]])
                        p.emit("V", lambda h, un=un, nk=nk: h.tensor_scalar(
                            eN[un][:, 0:nk * 128], eE[un][:, 0:nk * 128],
                            bb[un][:, 3:4], None, ALU.mult),
                            reads=[b_eE[un], b_bb[un]], writes=[b_eN[un]])

                    def st_D1(hh):
                        un = hh % 3
                        tb = 2 + (hh % 2)
                        fns = [(lambda h, j=j, un=un, tb=tb, nk=nk:
                                h.matmul(
                                    pqb[tb][:, ts(j, 128)],
                                    eN[un][:, ts(j, 128)], c_idb[:],
                                    is_transpose=True, start=(j == 0),
                                    stop=(j == nk - 1))) for j in range(nk)]
                        p.emit("T", fns, reads=[b_eN[un], CONST],
                               writes=[b_pq[tb]])

                    def st_EV(hh):
                        un = hh % 3
                        tb = 2 + (hh % 2)
                        p.emit("S", lambda h, un=un, nk=nk, tb=tb:
                               h.copy(eT[un][:, 0:nk * 128],
                                      pqb[tb][:, 0:nk * 128]),
                               reads=[b_pq[tb]], writes=[b_eT[un]])

                    def st_D2t(hh):
                        un = hh % 3
                        ob = 6 + ((hh // 2) % 2)
                        hrow = (hh % 2) * 64
                        ns_ = len(slots)
                        fns = [(lambda h, j=j, slot=slot, ob=ob, un=un,
                                hrow=hrow, hh=hh, ns_=ns_: h.matmul(
                                    pq[ob][hrow:hrow + 64, 0:128],
                                    vb[slot % 6][:, ts(hh, 64)],
                                    eT[un][:, ts(j, 128)],
                                    start=(j == 0),
                                    stop=(j == ns_ - 1)))
                               for j, slot in enumerate(slots)]
                        vb_reads = [b_vb[slot % 6] for slot in slots]
                        p.emit("T", fns, reads=[b_eT[un]] + vb_reads,
                               writes=[b_pq[ob]])

                    def st_D2v(hh):
                        ob = 6 + ((hh // 2) % 2)
                        hcol = hh // 2
                        if hh % 2 == 1:
                            p.emit("V", lambda h, ob=ob, hcol=hcol,
                                   tile=tile, aotw=aotw: h.tensor_copy(
                                       aotw[:, hcol * 256 + tile * 128:
                                            hcol * 256 + tile * 128 + 128],
                                       pq[ob][:, 0:128]),
                                   reads=[b_pq[ob]], writes=[baotw])

                    unit_qk(0)
                    unit_qk(1)
                    st_A(0)
                    st_B(0)
                    for u in range(20):
                        if 0 <= u - 4:
                            st_D2t(u - 4)
                        if u + 2 < 16:
                            unit_qk(u + 2)
                        if 0 <= u - 2 <= 15:
                            st_D1(u - 2)
                        if 0 <= u - 3 <= 15:
                            st_EV(u - 3)
                        if u + 1 < 16:
                            st_A(u + 1)
                            st_B(u + 1)
                        if u <= 15:
                            st_C(u)
                        if 0 <= u - 4:
                            st_D2v(u - 4)
                for mp in range(4):
                    if feeder is not None:
                        next(feeder, None)
                    bank = mp % 2
                    fns = []
                    for mi in range(2):
                        m = 2 * mp + mi
                        for kc in range(8):
                            fns.append((lambda h, m=m, kc=kc, mi=mi,
                                        bank=bank, aotw=aotw: h.matmul(
                                            pq[bank][:, mi * 256:
                                                     mi * 256 + 256],
                                            w_o[:, kc * 1024 + m * 128:
                                                kc * 1024 + (m + 1) * 128],
                                            aotw[:, ts(kc, 256)],
                                            start=(mi == 0 and kc == 0),
                                            stop=(mi == 1 and kc == 7))))
                    p.emit("T", fns, reads=[baotw, b_wo],
                           writes=[b_pq[bank]])
                    for mi in range(2):
                        m = 2 * mp + mi
                        p.emit("V", lambda h, m=m, mi=mi, bank=bank,
                               xrw=xrw: h.tensor_tensor(
                                   xrw[:, ts(m, 256)],
                                   xrw[:, ts(m, 256)],
                                   pq[bank][:, mi * 256:mi * 256 + 256],
                                   ALU.add),
                               reads=[bxrw, b_pq[bank]], writes=[bxrw])
                p.emit("G", lambda h, b=b, w=w, xrw=xrw: h.dma_start(
                    out=h_resid[b].rearrange("(c p) n -> p c n", p=128)
                    [:, :, ts(w, 256)],
                    in_=xrw[:].rearrange("p (c n) -> p c n", c=8)),
                    reads=[bxrw],
                    writes=[bh_resid[b][2 * w], bh_resid[b][2 * w + 1]])

            bw = [(b, w) for b in range(B) for w in range(NBLK)]
            g0 = a_s1_gen(*bw[0], 0)
            for _ in g0:
                pass
            for wg, (b, w) in enumerate(bw):
                if wg + 1 < len(bw):
                    for _ in a_s1_gen(*bw[wg + 1], wg + 1):
                        pass
                a_s2(b, w, wg, None)
            stP.close()
            p.barrier()

            stF = ExitStack()
            sbF = lambda nm, sh, dt: stF.enter_context(
                nc.sbuf_tensor(f"L{li}f_" + nm, sh, dt))
            w_1 = sbF("w1", [128, 8 * 5460], BF16)
            w_2 = sbF("w2", [128, 22 * 1024], BF16)
            b1p = sbF("b1p", [128, 44], F32)
            b2c = sbF("b2c", [128, 8], F32)
            stSet = ExitStack()
            stage2 = [stSet.enter_context(
                nc.sbuf_tensor(f"L{li}f_stage0", [128, 2730], F32))]
            gcol = stSet.enter_context(
                nc.sbuf_tensor(f"L{li}f_gcol", [128, 8], F32))
            bcolb = stSet.enter_context(
                nc.sbuf_tensor(f"L{li}f_bcolb", [128, 8], BF16))
            bcolf = stSet.enter_context(
                nc.sbuf_tensor(f"L{li}f_bcolf", [128, 8], F32))
            b1c = stSet.enter_context(
                nc.sbuf_tensor(f"L{li}f_b1c", [128, 44], F32))

            b_w1, b_w2 = Buf("w1"), Buf("w2")
            b_st2 = [Buf("st20")]
            b_gcol, b_bcolb, b_bcolf = Buf("gcol"), Buf("bcolb"), Buf("bcolf")
            b_b1c, b_b1p, b_b2c = Buf("b1c"), Buf("b1p"), Buf("b2c")
            b_xr2 = [Buf(f"xr2{i}") for i in range(2)]
            b_halo2 = Buf("halo2")
            b_xs2 = [Buf(f"xs2{i}") for i in range(2)]
            b_sqt = Buf("sqt")
            b_row = Buf("rowtmp")
            b_rowbf = Buf("rowbf")

            
            b_gel = [Buf(f"gel{i}") for i in range(2)]
            b_act = Buf("act")
                        
            p.emit("Y", lambda h, li=li: h.dma_start(
                out=gcol[:], in_=din["gcols"][li]), writes=[b_gcol])
            p.emit("Y", lambda h, li=li: h.dma_start(
                out=bcolf[:], in_=din["bcols"][li]), writes=[b_bcolf])
            p.emit("V", lambda h: h.tensor_copy(bcolb[:], bcolf[:]),
                   reads=[b_bcolf], writes=[b_bcolb])
            p.emit("Y", lambda h, li=li: h.dma_start(
                out=b1c[:], in_=din["b1cols"][li]), writes=[b_b1c])
            p.emit("Y", lambda h, li=li: h.dma_start(
                out=b2c[:], in_=din["b2cols"][li]), writes=[b_b2c])
            wtmp = stSet.enter_context(
                nc.sbuf_tensor(f"L{li}f_wtmp", [128, 2730], BF16))
            b_wtmp = Buf("wtmp")
            for dc in range(8):
                for hf in range(2):
                    sg2, bsg2 = stage2[0], b_st2[0]
                    p.emit("Y", lambda h, sg2=sg2, li=li, dc=dc, hf=hf:
                           h.dma_start(out=sg2[:],
                                       in_=din["W1"][li, ts(dc, 128),
                                                     hf * 2730:(hf + 1) * 2730]),
                           writes=[bsg2])
                    p.emit("V", lambda h, sg2=sg2, dc=dc, hf=hf:
                           h.tensor_scalar(
                               w_1[:, dc * 5460 + hf * 2730:
                                   dc * 5460 + (hf + 1) * 2730],
                               sg2[:], gcol[:, dc:dc + 1], None, ALU.mult),
                           reads=[bsg2, b_gcol], writes=[b_w1])
                    p.emit("V", lambda h, sg2=sg2: h.tensor_copy(
                        wtmp[:], sg2[:]), reads=[bsg2], writes=[b_wtmp])
                    fns = []
                    jorder = [0, 21] + list(range(1, 21))
                    for pos, j in enumerate(jorder):
                        wdt = 42 if j == 21 else 128
                        fns.append((lambda h, j=j, wdt=wdt, dc=dc, hf=hf,
                                    pos=pos: h.matmul(
                                        pq[2][0:wdt, hf * 22 + j:
                                              hf * 22 + j + 1],
                                        wtmp[:, j * 128:j * 128 + wdt],
                                        bcolb[:, dc:dc + 1],
                                        start=(dc == 0 and hf == 0 and
                                               pos == 0),
                                        stop=(dc == 7 and hf == 1 and
                                              pos == 21))))
                    p.emit("T", fns, reads=[b_wtmp, b_bcolb],
                           writes=[b_pq[2]])
            p.emit("V", lambda h: h.tensor_tensor(
                b1p[:], pq[2][:, 0:44], b1c[:], ALU.add),
                reads=[b_pq[2], b_b1c], writes=[b_b1p])
            p.emit("V", lambda h: h.memset(
                w_2[:, 21 * 1024:22 * 1024], 0.0), writes=[b_w2])
            for kc in range(22):
                rows = 42 if kc == 21 else 128
                sg2, bsg2 = stage2[0], b_st2[0]
                p.emit("Y", lambda h, sg2=sg2, li=li, kc=kc, rows=rows:
                       h.dma_start(out=sg2[0:rows, 0:1024],
                                   in_=din["W2"][li, kc * 128:kc * 128 + rows, :]),
                       writes=[bsg2])
                p.emit("V", lambda h, sg2=sg2, kc=kc, rows=rows: h.tensor_copy(
                    w_2[0:rows, kc * 1024:(kc + 1) * 1024],
                    sg2[0:rows, 0:1024]), reads=[bsg2], writes=[b_w2])

            stSet.close()
            p.barrier()
            xr2 = [sbF(f"xr2{i}", [128, 8 * 512], F32) for i in range(2)]
            halo2 = sbF("halo2", [128, 4], F32)
            xs2 = [sbF(f"xs2{i}", [128, 8 * 512], BF16) for i in range(2)]
            sqt = sbF("sqt", [128, 512], BF16)
            rowtmp = sbF("rowtmp", [1, 512], F32)
            rowbf = sbF("rowbf", [1, 1024], BF16)

            gel = [sbF(f"gel{i}", [128, 512], BF16) for i in range(2)]
            act = sbF("act", [128, 22 * 512], BF16)
            p.emit("V", lambda h: h.memset(act[:, 21 * 512:22 * 512], 0.0),
                   writes=[b_act])
            def f_s1_gen(b, tt, tg):
                xrw, bxrw = xr2[tg % 2], b_xr2[tg % 2]
                xsw, bxsw = xs2[tg % 2], b_xs2[tg % 2]
                rblk = [bh_resid[b][4 * tt + i] for i in range(4)]
                p.emit("Y", lambda h, b=b, tt=tt, xrw=xrw: h.dma_start(
                    out=xrw[:].rearrange("p (c n) -> p c n", c=8),
                    in_=h_resid[b].rearrange("(c p) n -> p c n", p=128)
                    [:, :, ts(tt, 512)]),
                    reads=rblk, writes=[bxrw])
                p.emit("V", lambda h, xsw=xsw, xrw=xrw: h.tensor_copy(
                    xsw[:].rearrange("p (c n) -> p c n", c=8)[:, 0:4, :],
                    xrw[:].rearrange("p (c n) -> p c n", c=8)[:, 0:4, :]),
                    reads=[bxrw], writes=[bxsw])
                p.emit("V", lambda h, xsw=xsw, xrw=xrw: h.tensor_copy(
                    xsw[:].rearrange("p (c n) -> p c n", c=8)[:, 4:8, 1:512],
                    xrw[:].rearrange("p (c n) -> p c n", c=8)[:, 4:8, 0:511]),
                    reads=[bxrw], writes=[bxsw])
                if tt == 0:
                    p.emit("V", lambda h, xsw=xsw: h.memset(
                        xsw[:].rearrange("p (c n) -> p c n", c=8)
                        [:, 4:8, 0:1], 0.0), writes=[bxsw])
                else:
                    p.emit("V", lambda h, xsw=xsw: h.tensor_copy(
                        xsw[:].rearrange("p (c n) -> p c n", c=8)
                        [:, 4:8, 0:1],
                        halo2[:].rearrange("p (c n) -> p c n", c=4)),
                        reads=[b_halo2], writes=[bxsw])
                p.emit("V", lambda h, xrw=xrw: h.tensor_copy(
                    halo2[:].rearrange("p (c n) -> p c n", c=4),
                    xrw[:].rearrange("p (c n) -> p c n", c=8)
                    [:, 4:8, 511:512]),
                    reads=[bxrw], writes=[b_halo2])
                yield
                fns = [(lambda h, xsw=xsw, dc=dc: h.matmul(
                    pq[0][0:1, :], c_ones_col[:],
                    xsw[:, ts(dc, 512)], start=(dc == 0), stop=(dc == 7)))
                       for dc in range(8)]
                p.emit("T", fns, reads=[bxsw, CONST], writes=[b_pq[0]])
                for dc in range(8):
                    p.emit("V", lambda h, xsw=xsw, dc=dc: h.tensor_tensor(
                        sqt[:], xsw[:, ts(dc, 512)], xsw[:, ts(dc, 512)],
                        ALU.mult), reads=[bxsw], writes=[b_sqt])
                    p.emit("T", lambda h, dc=dc: h.matmul(
                        pq[1][0:1, :], c_ones_col[:], sqt[:],
                        start=(dc == 0), stop=(dc == 7)),
                        reads=[b_sqt, CONST], writes=[b_pq[1]])
                    if dc % 2 == 1:
                        yield
                p.emit("V", lambda h: h.tensor_scalar(
                    pq[0][32:33, :], pq[0][0:1, :], 1.0 / D, None,
                    ALU.mult), reads=[b_pq[0]], writes=[b_pq[0]])
                p.emit("V", lambda h: h.tensor_scalar(
                    pq[1][32:33, :], pq[1][0:1, :], 1.0 / D, 1e-5,
                    ALU.mult, ALU.add), reads=[b_pq[1]], writes=[b_pq[1]])
                p.emit("S", lambda h: h.activation(
                    rowtmp[:], pq[0][32:33, :], AF.Square),
                    reads=[b_pq[0]], writes=[b_row])
                p.emit("V", lambda h: h.tensor_tensor(
                    pq[1][64:65, :], pq[1][32:33, :], rowtmp[:],
                    ALU.subtract), reads=[b_pq[1], b_row], writes=[b_pq[1]])
                p.emit("S", lambda h: h.activation(
                    pq[0][64:65, :], pq[1][64:65, :], AF.Sqrt),
                    reads=[b_pq[1]], writes=[b_pq[0]])
                p.emit("V", lambda h: h.reciprocal(
                    rowtmp[:], pq[0][64:65, :]),
                    reads=[b_pq[0]], writes=[b_row])
                p.emit("V", lambda h: h.tensor_copy(
                    rowbf[:, 0:512], rowtmp[:]),
                    reads=[b_row], writes=[b_rowbf])
                p.emit("V", lambda h: h.scalar_tensor_tensor(
                    rowbf[:, 512:1024], pq[0][32:33, :], -1.0,
                    rowtmp[:], ALU.mult, ALU.mult),
                    reads=[b_pq[0], b_row], writes=[b_rowbf])
                yield
                p.emit("T", lambda h: h.matmul(
                    pq[0][:], c_ones_row[:], rowbf[:, 0:512],
                    start=True, stop=True),
                    reads=[b_rowbf, CONST], writes=[b_pq[0]])
                p.emit("T", lambda h: h.matmul(
                    pq[1][:], c_ones_row[:], rowbf[:, 512:1024],
                    start=True, stop=True),
                    reads=[b_rowbf, CONST], writes=[b_pq[1]])
                for dc in range(8):
                    p.emit("V", lambda h, xsw=xsw, dc=dc: h.tensor_tensor(
                        sqt[:], xsw[:, ts(dc, 512)], pq[0][:], ALU.mult),
                        reads=[bxsw, b_pq[0]], writes=[b_sqt])
                    p.emit("V", lambda h, xsw=xsw, dc=dc: h.tensor_tensor(
                        xsw[:, ts(dc, 512)], sqt[:], pq[1][:], ALU.add),
                        reads=[b_sqt, b_pq[1]], writes=[bxsw])
                    yield

            def f_s2(b, tt, tg, feeder):
                xrw, bxrw = xr2[tg % 2], b_xr2[tg % 2]
                xsw, bxsw = xs2[tg % 2], b_xs2[tg % 2]
                rblk = [bh_resid[b][4 * tt + i] for i in range(4)]
                zz, b_zz = xsw, bxsw

                def fj_mm(j):
                    wdt = 42 if j == 21 else 128
                    pa = 2 + 2 * (j % 3)
                    pg = 3 + 2 * (j % 3)
                    fns = [(lambda h, dc=dc, j=j, wdt=wdt, pa=pa, zz=zz:
                            h.matmul(
                                pq[pa][0:wdt, :],
                                w_1[:, dc * 5460 + j * 128:
                                    dc * 5460 + j * 128 + wdt],
                                zz[:, ts(dc, 512)],
                                start=(dc == 0), stop=(dc == 7)))
                           for dc in range(8)]
                    p.emit("T", fns, reads=[b_zz, b_w1], writes=[b_pq[pa]])
                    fns = [(lambda h, dc=dc, j=j, wdt=wdt, pg=pg, zz=zz:
                            h.matmul(
                                pq[pg][0:wdt, :],
                                w_1[:, dc * 5460 + 2730 + j * 128:
                                    dc * 5460 + 2730 + j * 128 + wdt],
                                zz[:, ts(dc, 512)],
                                start=(dc == 0), stop=(dc == 7)))
                           for dc in range(8)]
                    p.emit("T", fns, reads=[b_zz, b_w1], writes=[b_pq[pg]])

                def fj_gelu(j):
                    wdt = 42 if j == 21 else 128
                    pg = 3 + 2 * (j % 3)
                    p.emit("S", lambda h, j=j, wdt=wdt, pg=pg: h.activation(
                        gel[j % 2][0:wdt, :],
                        pq[pg][0:wdt, :], AF.Gelu,
                        bias=b1p[0:wdt, 22 + j:23 + j]),
                        reads=[b_pq[pg], b_b1p], writes=[b_gel[j % 2]])

                def fj_stt(j):
                    wdt = 42 if j == 21 else 128
                    pa = 2 + 2 * (j % 3)
                    p.emit("V", lambda h, j=j, wdt=wdt, pa=pa:
                           h.scalar_tensor_tensor(
                               act[0:wdt, ts(j, 512)], pq[pa][0:wdt, :],
                               b1p[0:wdt, j:j + 1], gel[j % 2][0:wdt, :],
                               ALU.add, ALU.mult),
                           reads=[b_pq[pa], b_b1p, b_gel[j % 2]],
                           writes=[b_act])

                for j in range(24):
                    if feeder is not None:
                        next(feeder, None)
                    if j >= 2:
                        fj_stt(j - 2)
                    if 1 <= j <= 22:
                        fj_gelu(j - 1)
                    if j < 22:
                        fj_mm(j)
                for wave in range(2):
                    for mi in range(4):
                        m = wave * 4 + mi
                        bank = 4 + mi
                        fns = []
                        for kc in range(22):
                            wdt = 42 if kc == 21 else 128
                            fns.append((lambda h, m=m, kc=kc, wdt=wdt,
                                        bank=bank: h.matmul(
                                            pq[bank][:],
                                            w_2[0:wdt, kc * 1024 +
                                                m * 128:kc * 1024 +
                                                (m + 1) * 128],
                                            act[0:wdt, ts(kc, 512)],
                                            start=(kc == 0),
                                            stop=(kc == 21))))
                        p.emit("T", fns, reads=[b_act, b_w2],
                               writes=[b_pq[bank]])
                        p.emit("V", lambda h, m=m, bank=bank, xrw=xrw:
                               h.scalar_tensor_tensor(
                                   xrw[:, ts(m, 512)], pq[bank][:],
                                   b2c[:, m:m + 1], xrw[:, ts(m, 512)],
                                   ALU.add, ALU.add),
                               reads=[b_pq[bank], b_b2c, bxrw],
                               writes=[bxrw])
                if last_layer:
                    for m in range(8):
                        p.emit("S", lambda h, m=m, xrw=xrw: h.activation(
                            xrw[:, ts(m, 512)], xrw[:, ts(m, 512)],
                            AF.Silu), reads=[bxrw], writes=[bxrw])
                    p.emit("G", lambda h, b=b, tt=tt, xrw=xrw: h.dma_start(
                        out=h_g[b].rearrange("(c p) n -> p c n", p=128)
                        [:, :, ts(tt, 512)],
                        in_=xrw[:].rearrange("p (c n) -> p c n", c=8)),
                        reads=[bxrw],
                        writes=[bh_g[b][4 * tt + i] for i in range(4)])
                else:
                    for m in range(8):
                        p.emit("G", lambda h, b=b, tt=tt, m=m, xrw=xrw:
                               h.dma_start(
                                   out=h_resid[b, m * 128:(m + 1) * 128,
                                               ts(tt, 512)],
                                   in_=xrw[:, ts(m, 512)]),
                               reads=[bxrw], writes=rblk)

            btt = [(b, tt) for b in range(B) for tt in range(NFT)]
            g0 = f_s1_gen(*btt[0], 0)
            for _ in g0:
                pass
            for tg, (b, tt) in enumerate(btt):
                if tg + 1 < len(btt):
                    for _ in f_s1_gen(*btt[tg + 1], tg + 1):
                        pass
                f_s2(b, tt, tg, None)

            stF.close()
            p.barrier()

        stC = ExitStack()
        sbC = lambda nm, sh, dt: stC.enter_context(nc.sbuf_tensor(nm, sh, dt))
        c_xc = [sbC(f"c_xc{i}", [128, N], BF16) for i in range(8)]
        c_sq = [sbC("c_sq0", [128, N], BF16)]
        c_norm = sbC("c_norm", [16, N], F32)
        c_negc = sbC("c_negc", [16, N], BF16)
        c_maxn = sbC("c_maxn", [16, 1], F32)
        c_Q = [sbC(f"c_Q{i}", [65, N], BF16) for i in range(2)]
        c_K = [sbC(f"c_K{i}", [65, N], BF16) for i in range(2)]
        c_Vt = [sbC(f"c_Vt{i}", [64, N], BF16) for i in range(2)]
        c_Vb = [sbC(f"c_Vb{i}", [128, (N // 128) * 66], BF16) for i in range(2)]
        c_E = [sbC(f"c_E{i}", [128, 1024], BF16) for i in range(3)]
        c_rden = [sbC(f"c_rden{i}", [1, 512], F32) for i in range(2)]
        c_rdb = [sbC(f"c_rdb{i}", [1, 512], BF16) for i in range(2)]
        c_R = [sbC(f"c_R{i}", [64, 512], F32) for i in range(2)]
        c_sout = [sbC("c_sout0", [64, N], F32)]
        b_xc = [Buf(f"c_xc{i}") for i in range(8)]
        b_sq = [Buf("c_sq0")]
        b_norm = Buf("c_norm")
        b_negc = Buf("c_negc")
        b_maxn = Buf("c_maxn")
        b_Q = [Buf(f"c_Q{i}") for i in range(2)]
        b_K = [Buf(f"c_K{i}") for i in range(2)]
        b_Vt = [Buf(f"c_Vt{i}") for i in range(2)]
        b_Vb = [Buf(f"c_Vb{i}") for i in range(2)]
        b_E = [Buf(f"c_E{i}") for i in range(3)]
        b_rden = [Buf(f"c_rden{i}") for i in range(2)]
        b_rdb = [Buf(f"c_rdb{i}") for i in range(2)]
        b_R = [Buf(f"c_R{i}") for i in range(2)]
        b_sout = [Buf("c_sout0")]

        for i in range(2 if "C" in phases else 0):
            p.emit("V", lambda h, i=i: h.memset(c_K[i][64:65, :], 1.0),
                   writes=[b_K[i]])
            p.emit("V", lambda h, i=i: h.memset(
                c_Vb[i][:].rearrange("p (k o) -> p k o", k=N // 128)
                [:, :, 64:66], 1.0), writes=[b_Vb[i]])

        for b in range(B if "C" in phases else 0):
            NG = N // 512
            for c in range(8):
                p.emit("Y", lambda h, b=b, c=c: h.dma_start(
                    out=c_xc[c][:], in_=h_xnT[b, ts(c, 128), :]),
                    reads=bh_xnT[b], writes=[b_xc[c]])
                p.emit("V", lambda h, b=b, c=c: h.tensor_scalar(
                    c_xc[c][:], c_xc[c][:],
                    g_negmu[:, b * 8 + c:b * 8 + c + 1], None, ALU.add),
                    reads=[b_xc[c], b_negmu], writes=[b_xc[c]])
                sq, bsq = c_sq[0], b_sq[0]
                p.emit("V", lambda h, sq=sq, c=c: h.tensor_tensor(
                    sq[:], c_xc[c][:], c_xc[c][:], ALU.mult),
                    reads=[b_xc[c]], writes=[bsq])
                for g8 in range(NG):
                    p.emit("T", lambda h, sq=sq, c=c, g8=g8: h.matmul(
                        pq[g8][0:16, :], c_indic[:, ts(c, 16)],
                        sq[:, ts(g8, 512)], start=(c == 0), stop=(c == 7)),
                        reads=[bsq, CONST], writes=[b_pq[g8]])
            for g8 in range(NG):
                p.emit("S", lambda h, g8=g8: h.activation(
                    c_norm[:, ts(g8, 512)], pq[g8][0:16, :], AF.Sqrt),
                    reads=[b_pq[g8]], writes=[b_norm])
            p.emit("V", lambda h: h.reduce_max(c_maxn[:], c_norm[:], AX.X),
                   reads=[b_norm], writes=[b_maxn])
            p.emit("V", lambda h: h.tensor_scalar(
                c_maxn[:], c_maxn[:], -1.0, None, ALU.mult),
                reads=[b_maxn], writes=[b_maxn])
            p.emit("V", lambda h: h.tensor_scalar(
                c_negc[:], c_norm[:], c_maxn[:], None, ALU.mult),
                reads=[b_norm, b_maxn], writes=[b_negc])

            for hh in range(H):
                u = b * H + hh
                sl = u % 2
                Q, K_, Vt, Vb = c_Q[sl], c_K[sl], c_Vt[sl], c_Vb[sl]
                bQ, bK, bVt, bVb = b_Q[sl], b_K[sl], b_Vt[sl], b_Vb[sl]
                cchunk, half = hh // 2, hh % 2
                p.emit("V", lambda h, Q=Q, cc=cchunk, hf=half: h.tensor_copy(
                    Q[0:64, :], c_xc[cc][ts(hf, 64), :]),
                    reads=[b_xc[cchunk]], writes=[bQ])
                p.emit("G", lambda h, Q=Q, hh=hh: h.dma_start(
                    out=Q[64:65, :], in_=c_negc[hh:hh + 1, :]),
                    reads=[b_negc], writes=[bQ])
                p.emit("V", lambda h, K_=K_, cc=cchunk, hf=half: h.tensor_copy(
                    K_[0:64, :], c_xc[cc][ts(hf, 64), :]),
                    reads=[b_xc[cchunk]], writes=[bK])
                p.emit("Y", lambda h, Vt=Vt, b=b, hh=hh: h.dma_start(
                    out=Vt[:], in_=h_xnT[b, ts(hh, 64), :]),
                    reads=bh_xnT[b], writes=[bVt])
                NW = max(1, (N // 128) // 16)
                for wv in range(NW):
                    nj = min(16, N // 128)
                    fns = [(lambda h, Vt=Vt, j=j, kc=wv * 16 + j, nj=nj: h.matmul(
                        pqb[2][:, ts(j, 64)], Vt[:, ts(kc, 128)],
                        c_idb[0:64, 0:64], is_transpose=True,
                        start=(j == 0), stop=(j == nj - 1)))
                           for j in range(nj)]
                    p.emit("T", fns, reads=[bVt, CONST], writes=[b_pq[2]])
                    p.emit("V", lambda h, Vb=Vb, wv=wv, nj=nj: h.tensor_copy(
                        Vb[:].rearrange("p (k o) -> p k o", k=N // 128)
                        [:, wv * 16:wv * 16 + nj, 0:64],
                        pqb[2][:].rearrange("p (k o) -> p k o", k=16)
                        [:, 0:nj, :]),
                        reads=[b_pq[2]], writes=[bVb])
                so, bso = c_sout[0], b_sout[0]
                for qb in range(N // 512):
                    ob = 6 + (qb % 2)
                    NKC = N // 128
                    SPANS = [pq45, pq01, pq23]
                    SPB = [4, 0, 2]

                    def emit_qk(kc, Q=Q, K_=K_, qb=qb):
                        spq = SPB[(kc // 2) % 3] + (kc % 2)
                        p.emit("T", lambda h, Q=Q, K_=K_, qb=qb, kc=kc,
                               spq=spq: h.matmul(
                                   pq[spq][:], K_[:, ts(kc, 128)],
                                   Q[:, ts(qb, 512)], start=True, stop=True),
                               reads=[bK, bQ], writes=[b_pq[spq]])

                    def emit_exp2(kc0, NKC=NKC):
                        sp = SPB[(kc0 // 2) % 3]
                        span = SPANS[(kc0 // 2) % 3]
                        E, bE = c_E[(kc0 // 2) % 3], b_E[(kc0 // 2) % 3]
                        p.emit("S", lambda h, E=E, span=span: h.activation(
                            E[:], span, AF.Exp),
                            reads=[b_pq[sp], b_pq[sp + 1]], writes=[bE])

                    def emit_av(kc, Vb=Vb, bVb=bVb, ob=ob, NKC=NKC):
                        E, bE = c_E[(kc // 2) % 3], b_E[(kc // 2) % 3]
                        half = (kc % 2) * 512
                        p.emit("T", lambda h, E=E, Vb=Vb, kc=kc, ob=ob,
                               half=half: h.matmul(
                                   pq[ob][0:65, :],
                                   Vb[:, kc * 66:kc * 66 + 65],
                                   E[:, half:half + 512],
                                   start=(kc == 0),
                                   stop=(kc == NKC - 1)),
                               reads=[bE, bVb], writes=[b_pq[ob]])

                    for kc in range(4):
                        emit_qk(kc)
                    for kp in range(0, NKC, 2):
                        if kp + 4 < NKC:
                            emit_qk(kp + 4)
                        if kp + 5 < NKC:
                            emit_qk(kp + 5)
                        emit_exp2(kp)
                        emit_av(kp)
                        emit_av(kp + 1)
                    rs = qb % 2
                    p.emit("V", lambda h, rs=rs, ob=ob: h.reciprocal(
                        c_rden[rs][:], pq[ob][64:65, :]),
                        reads=[b_pq[ob]], writes=[b_rden[rs]])
                    p.emit("V", lambda h, rs=rs: h.tensor_copy(
                        c_rdb[rs][:], c_rden[rs][:]),
                        reads=[b_rden[rs]], writes=[b_rdb[rs]])
                    p.emit("T", lambda h, rs=rs, ob=ob: h.matmul(
                        pq[ob][64:128, :], c_ones_row[0:1, 0:64],
                        c_rdb[rs][:], start=True, stop=True),
                        reads=[b_rdb[rs], CONST], writes=[b_pq[ob]]),
                    p.emit("S", lambda h, rs=rs, ob=ob: h.copy(
                        c_R[rs][:], pq[ob][64:128, :]),
                        reads=[b_pq[ob]], writes=[b_R[rs]])
                    p.emit("V", lambda h, so=so, qb=qb, rs=rs, ob=ob:
                           h.tensor_tensor(so[:, ts(qb, 512)],
                                           pq[ob][0:64, :], c_R[rs][:],
                                           ALU.mult),
                           reads=[b_pq[ob], b_R[rs]], writes=[bso])
                p.emit("G", lambda h, so=so, b=b, hh=hh: h.dma_start(
                    out=h_s[b, ts(hh, 64), :], in_=so[:]),
                    reads=[bso], writes=[bh_s[b]])

        stC.close()
        p.barrier()

        stD = ExitStack()
        sbD = lambda nm, sh, dt: stD.enter_context(nc.sbuf_tensor(nm, sh, dt))
        d_x = [sbD(f"d_x{i}", [128, D], F32) for i in range(2)]
        d_s = [sbD(f"d_s{i}", [128, D], F32) for i in range(2)]
        d_g = [sbD(f"d_g{i}", [128, D], F32) for i in range(2)]
        d_f = [sbD(f"d_f{i}", [128, D], BF16) for i in range(2)]
        d_o = [sbD(f"d_o{i}", [128, D], F32) for i in range(2)]
        b_dx = [Buf(f"d_x{i}") for i in range(2)]
        b_ds = [Buf(f"d_s{i}") for i in range(2)]
        b_dg = [Buf(f"d_g{i}") for i in range(2)]
        b_df = [Buf(f"d_f{i}") for i in range(2)]
        b_do = [Buf(f"d_o{i}") for i in range(2)]

        for b in range(B if "D" in phases else 0):
            for t in range(NT):
                u = b * NT + t
                sl = u % 2
                p.emit("Y", lambda h, b=b, t=t, sl=sl: h.dma_start(
                    out=d_x[sl][:], in_=din["x"][b, ts(t, 128), :]),
                    writes=[b_dx[sl]])
                p.emit("Y", lambda h, b=b, t=t, sl=sl: h.dma_start(
                    out=d_s[sl][:].rearrange("p (c n) -> p c n", c=8),
                    in_=h_s[b].rearrange("(c p) n -> p c n", p=128)
                    [:, :, ts(t, 128)]),
                    reads=[bh_s[b]], writes=[b_ds[sl]])
                p.emit("Y", lambda h, b=b, t=t, sl=sl: h.dma_start(
                    out=d_g[sl][:].rearrange("p (c n) -> p c n", c=8),
                    in_=h_g[b].rearrange("(c p) n -> p c n", p=128)
                    [:, :, ts(t, 128)]),
                    reads=[bh_g[b][t]], writes=[b_dg[sl]])
                p.emit("G", lambda h, sl=sl: h.tensor_tensor(
                    d_f[sl][:], d_s[sl][:], d_g[sl][:], ALU.mult),
                    reads=[b_ds[sl], b_dg[sl]], writes=[b_df[sl]], gc=True)
                fns = [(lambda h, sl=sl, j=j: h.matmul(
                    pqb[1][:, ts(j, 128)], d_f[sl][:, ts(j, 128)], c_idb[:],
                    is_transpose=True, start=(j == 0), stop=(j == 7)))
                       for j in range(8)]
                p.emit("T", fns, reads=[b_df[sl], CONST], writes=[b_pq[1]])
                p.emit("V", lambda h, sl=sl: h.tensor_tensor(
                    d_o[sl][:], d_x[sl][:], pqb[1][:], ALU.add),
                    reads=[b_dx[sl], b_pq[1]], writes=[b_do[sl]])
                p.emit("G", lambda h, b=b, t=t, sl=sl: h.dma_start(
                    out=d_out[b, ts(t, 128), :], in_=d_o[sl][:]),
                    reads=[b_do[sl]])

        stD.close()
        global LAST_PROG
        LAST_PROG = p
        p.finalize(nc, st)
    return nc


def kernel(**inputs):
    consts = host_consts()
    lay = host_layouts(inputs)
    x = np.ascontiguousarray(np.asarray(inputs["L0_attn_x"], np.float32))
    wmap = dict(
        Wqkv=np.ascontiguousarray(np.asarray(inputs["L0_ffn_Wqkv"], np.float32)),
        Wo=np.ascontiguousarray(np.asarray(inputs["L0_ffn_Wo"], np.float32)),
        W1=np.ascontiguousarray(np.asarray(inputs["L0_ffn_W1"], np.float32)),
        W2=np.ascontiguousarray(np.asarray(inputs["L0_ffn_W2"], np.float32)),
        qscale=np.ascontiguousarray(np.asarray(inputs["L0_ffn_q_scale"], np.float32)),
        kscale=np.ascontiguousarray(np.asarray(inputs["L0_ffn_k_scale"], np.float32)),
        **lay,
    )
    nc = build_program()
    lower_extended_insts(nc)
    in_maps = []
    for cidx in range(NCORES):
        m = dict(x=np.ascontiguousarray(x[2 * cidx:2 * cidx + 2]))
        m.update(wmap)
        m.update(consts)
        in_maps.append(m)
    res = run_bass_kernel_spmd(nc, in_maps, core_ids=list(range(NCORES)))
    return np.concatenate([np.asarray(r["out"]) for r in res.results], axis=0)


if __name__ == "__main__":
    nc = build_program()
    n = len(nc.all_instructions() if hasattr(nc, "all_instructions") else [])
    print("built")


# version 19
# speedup vs baseline: 1.0630x; 1.0010x over previous
import numpy as np
import ml_dtypes
from contextlib import ExitStack

import concourse.bass as bass
import concourse.mybir as mybir
from concourse.bass_utils import run_bass_kernel_spmd
from concourse.library_overlay import lower_extended_insts

F32 = mybir.dt.float32
BF16 = mybir.dt.bfloat16
AF = mybir.ActivationFunctionType
ALU = mybir.AluOpType
AX = mybir.AxisListType
BFNP = ml_dtypes.bfloat16

B = 2
N = 4096
D = 1024
H = 16
DH = 64
WIN = 256
DI = 2730
NL = 6
NCORES = 8

N_LAYERS_BUILD = 6
LAST_PROG = None
DEBUG_TAPS = ()


def ts(i, sz):
    return slice(i * sz, (i + 1) * sz)


def host_consts():
    c = {}
    c["identf"] = np.eye(128, dtype=np.float32)
    c["identb"] = np.eye(128, dtype=BFNP)
    c["ones_col"] = np.ones((128, 1), dtype=BFNP)
    c["ones_row"] = np.ones((1, 128), dtype=BFNP)
    half = np.arange(0, DH, 2, dtype=np.float64)
    inv_freq = 1.0 / 10000.0 ** (half / DH)
    t = np.arange(N, dtype=np.float64)
    freqs = t[:, None] * inv_freq[None, :]
    freqs = np.concatenate([freqs, freqs], -1)
    base_scale = (half + 0.4 * DH) / (1.4 * DH)
    power = (t - N // 2) / (WIN // 2)
    scale = base_scale[None, :] ** power[:, None]
    scale = np.concatenate([scale, scale], -1)
    cos, sin = np.cos(freqs), np.sin(freqs)
    sign = np.concatenate([-np.ones(32), np.ones(32)])[None, :]
    c["tab_cq"] = (8.0 * cos * scale).astype(BFNP)
    c["tab_sq"] = (8.0 * sin * scale * sign).astype(BFNP)
    c["tab_ck"] = (cos / scale).astype(BFNP)
    c["tab_sk"] = (sin / scale * sign).astype(BFNP)
    i = np.arange(128, dtype=np.float32)
    mv = np.zeros((128, 6), dtype=np.float32)
    mv[:, 0] = i
    mv[:, 1] = i + 257
    mv[:, 2] = 0.0
    mv[:, 3] = i + 1
    mv[:, 4] = 0.0
    mv[:, 5] = i + 129
    c["maskv"] = mv
    ind = np.zeros((8, 128, 16), dtype=BFNP)
    for cc in range(8):
        for j in range(2):
            ind[cc, j * 64:(j + 1) * 64, 2 * cc + j] = 1.0
    c["indic"] = ind
    return c


def host_layouts(inputs):
    lay = {}
    b1 = np.asarray(inputs["L0_ffn_b1"], np.float32)
    cols = np.zeros((NL, 128, 44), dtype=np.float32)
    for j in range(22):
        for half in range(2):
            lo = half * 2730 + j * 128
            w = min(128, half * 2730 + 2730 - lo)
            cols[:, :w, half * 22 + j] = b1[:, lo:lo + w]
    lay["b1cols"] = np.ascontiguousarray(cols)
    for name, key in [("b2cols", "L0_ffn_b2"), ("gcols", "L0_ffn_ln_g"),
                      ("bcols", "L0_ffn_ln_b")]:
        v = np.asarray(inputs[key], np.float32)
        lay[name] = np.ascontiguousarray(v.reshape(NL, 8, 128).transpose(0, 2, 1))
    return lay


class Buf:
    __slots__ = ("name", "w", "r", "psum")

    def __init__(self, name="", psum=False):
        self.name = name
        self.w = None
        self.r = []
        self.psum = psum


class Op:
    __slots__ = ("eng", "fns", "deps", "need", "val", "idx", "skey", "dma")

    def __init__(self, eng, fns, deps, idx):
        self.eng = eng
        self.fns = fns
        self.deps = deps
        self.need = False
        self.val = None
        self.idx = idx
        self.dma = False


class Prog:
    ENGS = ("T", "V", "S", "Y", "G")

    NLANES = 8

    def __init__(self):
        self.ops = []
        self._bar = []
        self._bar_seen = set()
        self._lane_rr = {"Y": 0, "G": 0}
        self._lane_ctr = {}
        self._last_stream = {}

    def emit(self, eng, fns, reads=(), writes=(), deps=(), gc=False):
        if not isinstance(fns, (list, tuple)):
            fns = [fns]
        dset = []
        for b in reads:
            if b.w is not None:
                dset.append(b.w)
            if b.psum:
                dset.extend(o for o in b.r if o.eng != eng)
        for b in writes:
            dset.extend(b.r)
            if b.w is not None:
                dset.append(b.w)
        for d in deps:
            if d is not None:
                dset.append(d)
        if eng not in self._bar_seen:
            dset.extend(self._bar)
            self._bar_seen.add(eng)
        op = Op(eng, list(fns), dset, len(self.ops))
        if eng in ("Y", "G") and not gc:
            op.dma = True
            lane = self._lane_rr[eng]
            self._lane_rr[eng] = (lane + 1) % self.NLANES
            op.skey = (eng, lane)
            self._lane_ctr[op.skey] = self._lane_ctr.get(op.skey, 0) + 16
            op.val = self._lane_ctr[op.skey]
            op.need = True
            self._last_stream[op.skey] = op
        else:
            op.skey = None
            self._last_stream[("P" if eng == "G" else eng, 0)] = op
        self.ops.append(op)
        for b in reads:
            b.r.append(op)
        for b in writes:
            b.w = op
            b.r = []
        return op

    def barrier(self):
        self._bar = list(self._last_stream.values())
        self._bar_seen = set()

    def finalize(self, nc, st):
        for op in self.ops:
            for d in op.deps:
                d.need = True
        skey_ctr = dict(self._lane_ctr)
        for op in self.ops:
            if op.skey is not None:
                continue
            if op.need:
                op.skey = ("P" if op.eng == "G" else op.eng, 0)
                skey_ctr[op.skey] = skey_ctr.get(op.skey, 0) + 1
                op.val = skey_ctr[op.skey]
        sems = {k: st.enter_context(
            nc.semaphore(f"sem{k[0]}{k[1]}")) for k in skey_ctr}
        per_eng = {e: [o for o in self.ops if o.eng == e] for e in self.ENGS}
        total = dict(skey_ctr)
        block = st.enter_context(nc.Block())

        def run(eng_name, handle):
            waited = {k: 0 for k in sems}
            for op in per_eng[eng_name]:
                tgt = {}
                for d in op.deps:
                    if d.val is None:
                        continue
                    if d.val > tgt.get(d.skey, 0):
                        tgt[d.skey] = d.val
                if op.dma:
                    prev = op.val - 16
                    if prev > tgt.get(op.skey, 0):
                        tgt[op.skey] = prev
                for k, v in sorted(tgt.items()):
                    if v > waited[k]:
                        handle.wait_ge(sems[k], v)
                        waited[k] = v
                instr = None
                for fn in op.fns:
                    instr = fn(handle)
                if op.need or op.dma:
                    instr.then_inc(sems[op.skey], 16 if op.dma else 1)
            if eng_name == "G":
                for k in sorted(total):
                    if k[0] in ("Y", "G"):
                        handle.wait_ge(sems[k], total[k])

        @block.sync
        def _(h):
            run("Y", h)

        @block.vector
        def _(h):
            run("V", h)

        @block.scalar
        def _(h):
            run("S", h)

        @block.tensor
        def _(h):
            run("T", h)

        @block.gpsimd
        def _(h):
            run("G", h)

        return total


def build_program(n_layers=N_LAYERS_BUILD, taps=DEBUG_TAPS, phases="ACD"):
    import os
    SUBA = int(os.environ.get("SUBA", "4"))
    nc = bass.Bass("TRN2", target_bir_lowering=False)
    consts = host_consts()

    din = {}
    din["x"] = nc.dram_tensor("x", [B, N, D], F32, kind="ExternalInput").ap()
    for nm, sh in [("Wqkv", [NL, D, 3 * D]), ("Wo", [NL, D, D]),
                   ("W1", [NL, D, 5460]), ("W2", [NL, DI, D]),
                   ("qscale", [NL, DH]), ("kscale", [NL, DH]),
                   ("b1cols", [NL, 128, 44]), ("b2cols", [NL, 128, 8]),
                   ("gcols", [NL, 128, 8]), ("bcols", [NL, 128, 8])]:
        din[nm] = nc.dram_tensor(nm, sh, F32, kind="ExternalInput").ap()
    for nm in ["identf", "maskv"]:
        din[nm] = nc.dram_tensor(nm, list(consts[nm].shape), F32,
                                 kind="ExternalInput").ap()
    for nm in ["identb", "ones_col", "ones_row", "tab_cq", "tab_sq",
               "tab_ck", "tab_sk", "indic"]:
        din[nm] = nc.dram_tensor(nm, list(consts[nm].shape), BF16,
                                 kind="ExternalInput").ap()
    d_out = nc.dram_tensor("out", [B, N, D], F32, kind="ExternalOutput").ap()

    def dscratch(name, shape, dt):
        kind = "ExternalOutput" if name in taps else "Internal"
        return nc.dram_tensor(name, shape, dt, kind=kind).ap()

    h_resid = dscratch("resid", [B, D, N], F32)
    h_xnT = dscratch("xnT", [B, D, N], BF16)
    h_s = dscratch("s", [B, D, N], F32)
    h_g = dscratch("g", [B, D, N], F32)

    p = Prog()
    st = ExitStack()
    with st:
        sb = lambda nm, sh, dt: st.enter_context(nc.sbuf_tensor(nm, sh, dt))

        c_idb = sb("c_idb", [128, 128], BF16)
        c_ones_col = sb("c_ones_col", [128, 1], BF16)
        c_ones_row = sb("c_ones_row", [1, 128], BF16)
        c_maskv = sb("c_maskv", [128, 6], F32)
        c_indic = sb("c_indic", [128, 8 * 16], BF16)
        g_negmu = sb("g_negmu", [128, B * 8], F32)

        CONST = Buf("const")
        b_negmu = Buf("negmu")
        NTt = N // 128
        bh_resid = [[Buf(f"h_resid{b}_{t}") for t in range(NTt)]
                    for b in range(B)]
        bh_xnT = [[Buf(f"h_xnT{b}_{t}") for t in range(NTt)] for b in range(B)]
        bh_s = [Buf(f"h_s{b}") for b in range(B)]
        bh_g = [[Buf(f"h_g{b}_{t}") for t in range(NTt)] for b in range(B)]

        pq_t = ([st.enter_context(nc.psum_tensor("pq01", [128, 1024], F32))]
                + [st.enter_context(nc.psum_tensor("pq23", [128, 1024], F32))]
                + [st.enter_context(nc.psum_tensor("pq45", [128, 1024], F32))]
                + [st.enter_context(nc.psum_tensor(f"pq{i}", [128, 512], F32))
                   for i in range(6, 8)])
        pq01 = pq_t[0][:]
        pq23 = pq_t[1][:]
        pq45 = pq_t[2][:]
        pq = ([pq01[:, 0:512], pq01[:, 512:1024]]
              + [pq23[:, 0:512], pq23[:, 512:1024]]
              + [pq45[:, 0:512], pq45[:, 512:1024]]
              + [pq_t[3][:], pq_t[4][:]])
        pqb = [pq[i].bitcast(BF16) for i in range(8)]
        b_pq = [Buf(f"pq{i}", psum=True) for i in range(8)]

        def load_const(dst_ap, src_ap):
            p.emit("Y", lambda h, d=dst_ap, s=src_ap: h.dma_start(out=d, in_=s),
                   writes=[CONST])

        load_const(c_idb[:], din["identb"])
        load_const(c_ones_col[:], din["ones_col"])
        load_const(c_ones_row[:], din["ones_row"])
        load_const(c_maskv[:], din["maskv"])
        load_const(c_indic[:].rearrange("p (c j) -> p c j", c=8),
                   din["indic"].rearrange("c p j -> p c j"))

        stIdf = ExitStack()
        c_idf = stIdf.enter_context(nc.sbuf_tensor("c_idf", [128, 128], F32))
        load_const(c_idf[:], din["identf"])
        stA = ExitStack()
        sbA = lambda nm, sh, dt: stA.enter_context(nc.sbuf_tensor(nm, sh, dt))
        a_x = [sbA(f"a_x{i}", [128, D], F32) for i in range(3)]
        a_stat = [sbA(f"a_stat{i}", [128, 8], F32) for i in range(2)]
        a_junk = sbA("a_junk", [128, D], BF16)
        a_xnf = [sbA(f"a_xnf{i}", [128, D], F32) for i in range(2)]
        a_xnb = [sbA(f"a_xnb{i}", [128, D], BF16) for i in range(2)]
        a_xtf = [sbA(f"a_xtf{i}", [128, D], F32) for i in range(2)]
        a_xtb = [sbA(f"a_xtb{i}", [128, D], BF16) for i in range(2)]
        a_murow = sbA("a_murow", [1, D], F32)
        a_mub = sbA("a_mub", [1, D], BF16)
        b_ax = [Buf(f"a_x{i}") for i in range(3)]
        b_st = [Buf(f"a_stat{i}") for i in range(2)]
        b_junk = Buf("a_junk")
        b_xnf = [Buf(f"a_xnf{i}") for i in range(2)]
        b_xnb = [Buf(f"a_xnb{i}") for i in range(2)]
        b_xtf = [Buf(f"a_xtf{i}") for i in range(2)]
        b_xtb = [Buf(f"a_xtb{i}") for i in range(2)]
        b_mur = Buf("a_murow")

        NT = N // 128
        for b in range(B):
            for t in range(NT):
                u = b * NT + t
                xa, bxa = a_x[u % 3], b_ax[u % 3]
                stt, bst = a_stat[u % 2], b_st[u % 2]
                p.emit("Y", lambda h, xa=xa, b=b, t=t: h.dma_start(
                    out=xa[:], in_=din["x"][b, ts(t, 128), :]),
                    writes=[bxa])
                p.emit("V", lambda h, stt=stt, xa=xa: h.reduce_sum(
                    stt[:, 0:1], xa[:], AX.X, negate=True),
                    reads=[bxa], writes=[bst])
                p.emit("V", lambda h, stt=stt: h.tensor_scalar(
                    stt[:, 1:2], stt[:, 0:1], 1.0 / D, None, ALU.mult),
                    reads=[bst], writes=[bst])
                p.emit("S", lambda h, stt=stt, xa=xa: h.activation(
                    a_junk[:], xa[:], AF.Square, bias=stt[:, 1:2],
                    accum_out=stt[:, 2:3]),
                    reads=[bxa, bst], writes=[b_junk, bst])
                p.emit("V", lambda h, stt=stt: h.tensor_scalar(
                    stt[:, 3:4], stt[:, 2:3], 1.0 / D, 1e-5, ALU.mult, ALU.add),
                    reads=[bst], writes=[bst])
                p.emit("V", lambda h, stt=stt: h.reciprocal(
                    stt[:, 4:5], stt[:, 3:4]), reads=[bst], writes=[bst])
                p.emit("S", lambda h, stt=stt: h.activation(
                    stt[:, 5:6], stt[:, 4:5], AF.Sqrt),
                    reads=[bst], writes=[bst])
                xnf, bxnf = a_xnf[u % 2], b_xnf[u % 2]
                xnb, bxnb = a_xnb[u % 2], b_xnb[u % 2]
                p.emit("V", lambda h, xnf=xnf, xa=xa, stt=stt: h.tensor_scalar(
                    xnf[:], xa[:], stt[:, 1:2], stt[:, 5:6], ALU.add, ALU.mult),
                    reads=[bxa, bst], writes=[bxnf])
                p.emit("V", lambda h, xnb=xnb, xnf=xnf: h.tensor_copy(
                    xnb[:], xnf[:]), reads=[bxnf], writes=[bxnb])
                for half in range(2 if SUBA >= 2 else 0):
                    fns = [(lambda h, xnf=xnf, j=j, cc=half * 4 + j, hf=half:
                            h.matmul(pq[hf][:, ts(j, 128)], xnf[:, ts(cc, 128)],
                                     c_idf[:], is_transpose=True,
                                     start=(j == 0), stop=(j == 3)))
                           for j in range(4)]
                    p.emit("T", fns, reads=[bxnf, CONST], writes=[b_pq[half]])
                xtf, bxtf = a_xtf[u % 2], b_xtf[u % 2]
                xtb, bxtb = a_xtb[u % 2], b_xtb[u % 2]
                if SUBA < 2:
                    p.emit("G", lambda h, xnf=xnf, b=b, t=t: h.dma_start(
                        out=h_resid[b, ts(t, 128), 0:N], in_=xnf[:, 0:N]),
                        reads=[bxnf], writes=[bh_resid[b]])
                    continue
                for half in range(2):
                    p.emit("V", lambda h, xtf=xtf, hf=half: h.tensor_copy(
                        xtf[:].rearrange("p (c n) -> p c n", c=8)
                        [:, 4 * hf:4 * hf + 4, :],
                        pq[hf][:].rearrange("p (c n) -> p c n", c=4)),
                        reads=[b_pq[half]], writes=[bxtf])
                    if SUBA >= 3:
                        p.emit("S", lambda h, xtb=xtb, hf=half: h.copy(
                            xtb[:].rearrange("p (c n) -> p c n", c=8)
                            [:, 4 * hf:4 * hf + 4, :],
                            pq[hf][:].rearrange("p (c n) -> p c n", c=4)),
                            reads=[b_pq[half]], writes=[bxtb])
                if SUBA >= 4:
                    p.emit("T", lambda h, xnb=xnb, t=t: h.matmul(
                        pq[6][0:1, :], c_ones_col[:], xnb[:, 0:512],
                        start=(t == 0), stop=(t == NT - 1)),
                        reads=[bxnb, CONST], writes=[b_pq[6]])
                    p.emit("T", lambda h, xnb=xnb, t=t: h.matmul(
                        pq[7][0:1, :], c_ones_col[:], xnb[:, 512:1024],
                        start=(t == 0), stop=(t == NT - 1)),
                        reads=[bxnb, CONST], writes=[b_pq[7]])
                p.emit("G", lambda h, xtf=xtf, b=b, t=t: h.dma_start(
                    out=h_resid[b].rearrange("(c p) n -> p c n", p=128)
                    [:, :, ts(t, 128)],
                    in_=xtf[:].rearrange("p (c n) -> p c n", c=8)),
                    reads=[bxtf], writes=[bh_resid[b][t]])
                if n_layers == 0:
                    p.emit("G", lambda h, xtf=xtf, b=b, t=t: h.dma_start(
                        out=h_g[b].rearrange("(c p) n -> p c n", p=128)
                        [:, :, ts(t, 128)],
                        in_=xtf[:].rearrange("p (c n) -> p c n", c=8)),
                        reads=[bxtf], writes=[bh_g[b][t]])
                if SUBA >= 3 and SUBA != 5:
                    p.emit("G", lambda h, xtb=xtb, b=b, t=t: h.dma_start(
                        out=h_xnT[b].rearrange("(c p) n -> p c n", p=128)
                        [:, :, ts(t, 128)],
                        in_=xtb[:].rearrange("p (c n) -> p c n", c=8)),
                        reads=[bxtb], writes=[bh_xnT[b][t]])
            if SUBA < 4:
                continue
            p.emit("V", lambda h: h.tensor_scalar(
                a_murow[:, 0:512], pq[6][0:1, :], 1.0 / N, None, ALU.mult),
                reads=[b_pq[6]], writes=[b_mur])
            p.emit("V", lambda h: h.tensor_scalar(
                a_murow[:, 512:1024], pq[7][0:1, :], 1.0 / N, None, ALU.mult),
                reads=[b_pq[7]], writes=[b_mur])
            p.emit("V", lambda h: h.tensor_copy(a_mub[:], a_murow[:]),
                   reads=[b_mur], writes=[b_mur])
            fns = [(lambda h, j=j: h.matmul(
                pqb[0][:, 2 * j:2 * j + 1], a_mub[:, ts(j, 128)],
                c_idb[0:1, 0:1], is_transpose=True, start=(j == 0),
                stop=(j == 7)))
                   for j in range(8)]
            p.emit("T", fns, reads=[b_mur, CONST], writes=[b_pq[0]])
            p.emit("V", lambda h, b=b: h.tensor_scalar(
                g_negmu[:, b * 8:(b + 1) * 8],
                pqb[0][:, 0:16].rearrange("p (a t) -> p a t", t=2)[:, :, 0:1],
                -1.0, None, ALU.mult), reads=[b_pq[0]], writes=[b_negmu])

        stA.close()
        stIdf.close()
        p.barrier()


        NBLK = N // 256
        NT2 = N // 128
        NFT = N // 512
        for li in range(n_layers):
            last_layer = (li == n_layers - 1)
            stP = ExitStack()
            sbP = lambda nm, sh, dt: stP.enter_context(
                nc.sbuf_tensor(f"L{li}a_" + nm, sh, dt))
            w_qkv = sbP("wqkv", [128, 8 * 3072], BF16)
            w_o = sbP("wo", [128, 8 * 1024], BF16)
            stage = [sbP("stage0", [128, 3072], F32)]
            stageb = [stage[0][:].bitcast(BF16)]
            tabs = {nm: sbP(nm, [128, (N // 128) * 64], BF16)
                    for nm in ["tqc", "tqs", "tkc", "tks"]}
            scl = sbP("scl", [1, 64], F32)
            sclb = sbP("sclb", [1, 64], BF16)
            qsb = sbP("qsb", [128, 4 * 64], BF16)
            xr = [sbP(f"xr{i}", [128, 8 * 256], F32) for i in range(2)]
            halo = sbP("halo", [128, 4], F32)
            xs = [sbP(f"xs{i}", [128, 8 * 256], BF16) for i in range(2)]
            qraw = [sbP(f"qraw{i}", [128, 1024], BF16) for i in range(2)]
            kraw = [sbP(f"kraw{i}", [128, 1024], BF16) for i in range(2)]
            kn = [sbP(f"kn{i}", [128, 1024], BF16) for i in range(2)]
            rt1 = sbP("rt1", [128, 1024], BF16)
            rt2 = sbP("rt2", [128, 1024], BF16)
            qrot = [sbP(f"qrot{i}", [128, 1024], BF16) for i in range(2)]
            krot = [sbP(f"krot{i}", [128, 1024], BF16) for i in range(2)]
            sqb = sbP("sqb", [128, 1024], BF16)
            n2 = [sbP(f"n2{i}", [128, 32], F32) for i in range(4)]
            rr = [sbP(f"rr{i}", [128, 32], F32) for i in range(4)]
            qt = [sbP(f"qt{i}", [128, 1024], BF16) for i in range(4)]
            kt = [sbP(f"kt{i}", [128, 1024], BF16) for i in range(6)]
            vb = [sbP(f"vb{i}", [128, 1024], BF16) for i in range(6)]
            sm = [sbP(f"sm{i}", [128, 384], F32) for i in range(3)]
            bb = [sbP(f"bb{i}", [128, 4], F32) for i in range(3)]
            eE = [sbP(f"eE{i}", [128, 384], BF16) for i in range(3)]
            eN = [sbP(f"eN{i}", [128, 384], BF16) for i in range(3)]
            eT = [sbP(f"eT{i}", [128, 384], BF16) for i in range(3)]
            aot = [sbP(f"aot{i}", [128, 8 * 256], BF16) for i in range(2)]

            B_ = Buf
            b_wq, b_wo = B_("wq"), B_("wo")
            b_stage = [B_("st0")]
            b_tabs = {k: B_(k) for k in tabs}
            b_scl, b_sclb, b_qsb = B_("scl"), B_("sclb"), B_("qsb")
            b_xr = [B_(f"xr{i}") for i in range(2)]
            b_halo = B_("halo")
            b_xs = [B_(f"xs{i}") for i in range(2)]
            b_qraw = [B_(f"qraw{i}") for i in range(2)]
            b_kraw = [B_(f"kraw{i}") for i in range(2)]
            b_kn = [B_(f"kn{i}") for i in range(2)]
            b_rt1, b_rt2 = B_("rt1"), B_("rt2")
            b_qrot = [B_(f"qrot{i}") for i in range(2)]
            b_krot = [B_(f"krot{i}") for i in range(2)]
            b_sqb = B_("sqb")
            b_n2 = [B_(f"n2{i}") for i in range(4)]
            b_rr = [B_(f"rr{i}") for i in range(4)]
            b_qt = [B_(f"qt{i}") for i in range(4)]
            b_kt = [B_(f"kt{i}") for i in range(6)]
            b_vb = [B_(f"vb{i}") for i in range(6)]
            b_sm = [B_(f"sm{i}") for i in range(3)]
            b_bb = [B_(f"bb{i}") for i in range(3)]
            b_eE = [B_(f"eE{i}") for i in range(3)]
            b_eN = [B_(f"eN{i}") for i in range(3)]
            b_eT = [B_(f"eT{i}") for i in range(3)]
            b_aot = [B_(f"aot{i}") for i in range(2)]

            for dc in range(8):
                sg, bsg = stage[0], b_stage[0]
                p.emit("Y", lambda h, sg=sg, li=li, dc=dc: h.dma_start(
                    out=sg[:], in_=din["Wqkv"][li, ts(dc, 128), :]),
                    writes=[bsg])
                p.emit("V", lambda h, sg=sg, dc=dc: h.tensor_copy(
                    w_qkv[:, dc * 3072:(dc + 1) * 3072], sg[:]),
                    reads=[bsg], writes=[b_wq])
            for dc in range(8):
                sg, bsg = stage[0], b_stage[0]
                p.emit("Y", lambda h, sg=sg, li=li, dc=dc: h.dma_start(
                    out=sg[:, 0:1024], in_=din["Wo"][li, ts(dc, 128), :]),
                    writes=[bsg])
                p.emit("V", lambda h, sg=sg, dc=dc: h.tensor_copy(
                    w_o[:, dc * 1024:(dc + 1) * 1024], sg[:, 0:1024]),
                    reads=[bsg], writes=[b_wo])
            p.emit("Y", lambda h, li=li: h.dma_start(
                out=scl[:], in_=din["qscale"][li].rearrange("(a d) -> a d", a=1)),
                writes=[b_scl])
            p.emit("V", lambda h: h.tensor_copy(sclb[:], scl[:]),
                   reads=[b_scl], writes=[b_sclb])
            fns = [lambda h: h.matmul(pq[2][:, 0:64], c_ones_row[:], sclb[:],
                                      start=True, stop=False),
                   lambda h: h.matmul(pq[2][:, 64:96], c_ones_row[:],
                                      sclb[:, 32:64], start=False, stop=False),
                   lambda h: h.matmul(pq[2][:, 96:128], c_ones_row[:],
                                      sclb[:, 0:32], start=False, stop=True)]
            p.emit("T", fns, reads=[b_sclb, CONST], writes=[b_pq[2]])
            p.emit("V", lambda h: h.tensor_copy(qsb[:, 0:128], pq[2][:, 0:128]),
                   reads=[b_pq[2]], writes=[b_qsb])
            p.emit("Y", lambda h, li=li: h.dma_start(
                out=scl[:], in_=din["kscale"][li].rearrange("(a d) -> a d", a=1)),
                writes=[b_scl])
            p.emit("V", lambda h: h.tensor_copy(sclb[:], scl[:]),
                   reads=[b_scl], writes=[b_sclb])
            fns = [lambda h: h.matmul(pq[2][:, 0:64], c_ones_row[:], sclb[:],
                                      start=True, stop=False),
                   lambda h: h.matmul(pq[2][:, 64:96], c_ones_row[:],
                                      sclb[:, 32:64], start=False, stop=False),
                   lambda h: h.matmul(pq[2][:, 96:128], c_ones_row[:],
                                      sclb[:, 0:32], start=False, stop=True)]
            p.emit("T", fns, reads=[b_sclb, CONST], writes=[b_pq[2]])
            p.emit("V", lambda h: h.tensor_copy(qsb[:, 128:256], pq[2][:, 0:128]),
                   reads=[b_pq[2]], writes=[b_qsb])
            for nm, base, qoff in [("tqc", "tab_cq", 0), ("tqs", "tab_sq", 64),
                                   ("tkc", "tab_ck", 128), ("tks", "tab_sk", 192)]:
                sg, bsg = stage[0], b_stage[0]
                p.emit("Y", lambda h, sg=sg, base=base: h.dma_start(
                    out=sg[:].bitcast(BF16)[:, 0:(N // 128) * 64]
                    .rearrange("p (t d) -> p t d", t=N // 128),
                    in_=din[base].rearrange("(t p) d -> p t d", p=128)),
                    writes=[bsg])
                p.emit("V", lambda h, nm=nm, sg=sg, qoff=qoff: h.tensor_tensor(
                    tabs[nm][:].rearrange("p (t d) -> p t d", t=N // 128),
                    sg[:].bitcast(BF16)[:, 0:(N // 128) * 64]
                    .rearrange("p (t d) -> p t d", t=N // 128),
                    qsb[:, qoff:qoff + 64].unsqueeze(1)
                    .broadcast_to((128, N // 128, 64)),
                    ALU.mult), reads=[bsg, b_qsb], writes=[b_tabs[nm]])

            def a_s1_gen(b, w, wg):
                xrw, bxrw = xr[wg % 2], b_xr[wg % 2]
                xsw, bxsw = xs[wg % 2], b_xs[wg % 2]
                p.emit("Y", lambda h, b=b, w=w, xrw=xrw: h.dma_start(
                    out=xrw[:].rearrange("p (c n) -> p c n", c=8),
                    in_=h_resid[b].rearrange("(c p) n -> p c n", p=128)
                    [:, :, ts(w, 256)]),
                    reads=[bh_resid[b][2 * w], bh_resid[b][2 * w + 1]],
                    writes=[bxrw])
                p.emit("V", lambda h, xsw=xsw, xrw=xrw: h.tensor_copy(
                    xsw[:].rearrange("p (c n) -> p c n", c=8)[:, 0:4, :],
                    xrw[:].rearrange("p (c n) -> p c n", c=8)[:, 0:4, :]),
                    reads=[bxrw], writes=[bxsw])
                p.emit("V", lambda h, xsw=xsw, xrw=xrw: h.tensor_copy(
                    xsw[:].rearrange("p (c n) -> p c n", c=8)[:, 4:8, 1:256],
                    xrw[:].rearrange("p (c n) -> p c n", c=8)[:, 4:8, 0:255]),
                    reads=[bxrw], writes=[bxsw])
                if w == 0:
                    p.emit("V", lambda h, xsw=xsw: h.memset(
                        xsw[:].rearrange("p (c n) -> p c n", c=8)
                        [:, 4:8, 0:1], 0.0), writes=[bxsw])
                else:
                    p.emit("V", lambda h, xsw=xsw: h.tensor_copy(
                        xsw[:].rearrange("p (c n) -> p c n", c=8)
                        [:, 4:8, 0:1],
                        halo[:].rearrange("p (c n) -> p c n", c=4)),
                        reads=[b_halo], writes=[bxsw])
                p.emit("V", lambda h, xrw=xrw: h.tensor_copy(
                    halo[:].rearrange("p (c n) -> p c n", c=4),
                    xrw[:].rearrange("p (c n) -> p c n", c=8)
                    [:, 4:8, 255:256]),
                    reads=[bxrw], writes=[b_halo])
                yield
                for tile in range(2):
                    u2g = 2 * wg + tile
                    sl2 = u2g % 2
                    sl4 = u2g % 4
                    u2 = 2 * w + tile
                    qr_, kr_ = qraw[sl2], kraw[sl2]
                    for slab in range(6):
                        bank = slab % 2
                        fns = [(lambda h, xsw=xsw, dc=dc, slab=slab,
                                tile=tile, bank=bank: h.matmul(
                                    pq[bank][:],
                                    xsw[:, dc * 256 + tile * 128:
                                        dc * 256 + tile * 128 + 128],
                                    w_qkv[:, dc * 3072 + slab * 512:
                                          dc * 3072 + (slab + 1) * 512],
                                    start=(dc == 0), stop=(dc == 7)))
                               for dc in range(8)]
                        p.emit("T", fns, reads=[bxsw, b_wq],
                               writes=[b_pq[bank]])
                        if slab < 2:
                            p.emit("S", lambda h, qr_=qr_, slab=slab,
                                   bank=bank: h.copy(
                                       qr_[:, slab * 512:(slab + 1) * 512],
                                       pq[bank][:]),
                                   reads=[b_pq[bank]], writes=[b_qraw[sl2]])
                        elif slab < 4:
                            p.emit("S", lambda h, kr_=kr_, slab=slab,
                                   bank=bank: h.copy(
                                       kr_[:, (slab - 2) * 512:
                                           (slab - 1) * 512],
                                       pq[bank][:]),
                                   reads=[b_pq[bank]], writes=[b_kraw[sl2]])
                        else:
                            vslot = u2g % 6
                            p.emit("S", lambda h, vslot=vslot, slab=slab,
                                   bank=bank: h.copy(
                                       vb[vslot][:, (slab - 4) * 512:
                                                 (slab - 3) * 512],
                                       pq[bank][:]),
                                   reads=[b_pq[bank]], writes=[b_vb[vslot]])
                        yield
                    p.emit("V", lambda h, qr_=qr_: h.tensor_tensor(
                        sqb[:], qr_[:], qr_[:], ALU.mult),
                        reads=[b_qraw[sl2]], writes=[b_sqb])
                    p.emit("V", lambda h, sl4=sl4: h.tensor_reduce(
                        n2[sl4][:, 0:16],
                        sqb[:].rearrange("p (g d) -> p g d", g=16),
                        AX.X, ALU.add), reads=[b_sqb], writes=[b_n2[sl4]])
                    p.emit("V", lambda h, kr_=kr_: h.tensor_tensor(
                        sqb[:], kr_[:], kr_[:], ALU.mult),
                        reads=[b_kraw[sl2]], writes=[b_sqb])
                    p.emit("V", lambda h, sl4=sl4: h.tensor_reduce(
                        n2[sl4][:, 16:32],
                        sqb[:].rearrange("p (g d) -> p g d", g=16),
                        AX.X, ALU.add), reads=[b_sqb], writes=[b_n2[sl4]])
                    p.emit("V", lambda h, sl4=sl4: h.tensor_scalar(
                        n2[sl4][:], n2[sl4][:], 1e-24, None, ALU.max),
                        reads=[b_n2[sl4]], writes=[b_n2[sl4]])
                    p.emit("S", lambda h, sl4=sl4: h.activation(
                        n2[sl4][:], n2[sl4][:], AF.Ln),
                        reads=[b_n2[sl4]], writes=[b_n2[sl4]])
                    p.emit("S", lambda h, sl4=sl4: h.activation(
                        rr[sl4][:], n2[sl4][:], AF.Exp, scale=-0.5),
                        reads=[b_n2[sl4]], writes=[b_rr[sl4]])
                    yield
                    p.emit("V", lambda h, sl2=sl2, sl4=sl4, kr_=kr_:
                           h.tensor_tensor(
                               kn[sl2][:].rearrange(
                                   "p (g d) -> p g d", g=16),
                               kr_[:].rearrange("p (g d) -> p g d", g=16),
                               rr[sl4][:, 16:32].unsqueeze(2)
                               .broadcast_to((128, 16, 64)), ALU.mult),
                           reads=[b_kraw[sl2], b_rr[sl4]],
                           writes=[b_kn[sl2]])
                    yield
                    for (srcbuf, bsrc, dstbuf, bdst, tc, tsn) in [
                            (qr_, b_qraw[sl2], qrot[sl2], b_qrot[sl2],
                             "tqc", "tqs"),
                            (kn[sl2], b_kn[sl2], krot[sl2], b_krot[sl2],
                             "tkc", "tks")]:
                        p.emit("V", lambda h, srcbuf=srcbuf, tc=tc, u2=u2:
                               h.tensor_tensor(
                                   rt1[:].rearrange("p (g d) -> p g d", g=16),
                                   srcbuf[:].rearrange("p (g d) -> p g d", g=16),
                                   tabs[tc][:, ts(u2, 64)].unsqueeze(1)
                                   .broadcast_to((128, 16, 64)), ALU.mult),
                               reads=[bsrc, b_tabs[tc]], writes=[b_rt1])
                        p.emit("V", lambda h, srcbuf=srcbuf, tsn=tsn, u2=u2:
                               h.tensor_tensor(
                                   rt2[:].rearrange("p (g d) -> p g d", g=16)
                                   [:, :, 0:32],
                                   srcbuf[:].rearrange("p (g d) -> p g d", g=16)
                                   [:, :, 32:64],
                                   tabs[tsn][:, u2 * 64:u2 * 64 + 32]
                                   .unsqueeze(1).broadcast_to((128, 16, 32)),
                                   ALU.mult),
                               reads=[bsrc, b_tabs[tsn]], writes=[b_rt2])
                        p.emit("V", lambda h, srcbuf=srcbuf, tsn=tsn, u2=u2:
                               h.tensor_tensor(
                                   rt2[:].rearrange("p (g d) -> p g d", g=16)
                                   [:, :, 32:64],
                                   srcbuf[:].rearrange("p (g d) -> p g d", g=16)
                                   [:, :, 0:32],
                                   tabs[tsn][:, u2 * 64 + 32:u2 * 64 + 64]
                                   .unsqueeze(1).broadcast_to((128, 16, 32)),
                                   ALU.mult),
                               reads=[bsrc, b_tabs[tsn]], writes=[b_rt2])
                        p.emit("V", lambda h, dstbuf=dstbuf: h.tensor_tensor(
                            dstbuf[:], rt1[:], rt2[:], ALU.add),
                            reads=[b_rt1, b_rt2], writes=[bdst])
                        yield
                    for (srcb, bsrc, dstb, bdst, bankj) in [
                            (qrot[sl2], b_qrot[sl2], qt[sl4], b_qt[sl4], 2),
                            (krot[sl2], b_krot[sl2], kt[u2g % 6],
                             b_kt[u2g % 6], 3)]:
                        fns = [(lambda h, srcb=srcb, j=j, bankj=bankj:
                                h.matmul(
                                    pqb[bankj][:, ts(j, 128)],
                                    srcb[:, ts(j, 128)], c_idb[:],
                                    is_transpose=True, start=(j == 0),
                                    stop=(j == 7))) for j in range(8)]
                        p.emit("T", fns, reads=[bsrc, CONST],
                               writes=[b_pq[bankj]])
                        p.emit("V", lambda h, dstb=dstb, bankj=bankj:
                               h.tensor_copy(dstb[:], pqb[bankj][:]),
                               reads=[b_pq[bankj]], writes=[bdst])
                        yield

            def a_s2(b, w, wg, feeder):
                xrw, bxrw = xr[wg % 2], b_xr[wg % 2]
                aotw, baotw = aot[wg % 2], b_aot[wg % 2]
                for tile in range(2):
                    u2g = 2 * wg + tile
                    sl4 = u2g % 4
                    u2 = 2 * w + tile
                    s_idx = u2
                    nk = min(3, s_idx + 1)
                    if s_idx == 0:
                        msc, mec = 2, 3
                    elif s_idx == 1:
                        msc, mec = 4, 5
                    else:
                        msc, mec = 0, 1
                    slots = list(range(u2g - nk + 1, u2g + 1))

                    def unit_qk(hh, sl4=sl4, slots=slots):
                        sb_ = 4 + (hh % 2)
                        hrow = (hh % 2) * 64
                        hcol = hh // 2
                        ns_ = len(slots)
                        fns = [(lambda h, j=j, slot=slot, sb_=sb_,
                                hrow=hrow, hcol=hcol, sl4=sl4, ns_=ns_:
                                h.matmul(
                                    pq[sb_][:, ts(j, 128)],
                                    qt[sl4][hrow:hrow + 64,
                                            hcol * 128:(hcol + 1) * 128],
                                    kt[slot % 6][hrow:hrow + 64,
                                                 hcol * 128:(hcol + 1) * 128],
                                    start=(j == 0), stop=(j == ns_ - 1)))
                               for j, slot in enumerate(slots)]
                        kt_reads = [b_kt[slot % 6] for slot in slots]
                        p.emit("T", fns, reads=[b_qt[sl4]] + kt_reads,
                               writes=[b_pq[sb_]])

                    def st_A(hh):
                        un = hh % 3
                        sb_ = 4 + (hh % 2)
                        p.emit("V", lambda h, un=un, sb_=sb_, nk=nk,
                               msc=msc, mec=mec: h.tensor_mask_reduce(
                                   sm[un][:, 0:nk * 128],
                                   pq[sb_][:, 0:nk * 128],
                                   c_maskv[:, msc:msc + 1],
                                   c_maskv[:, mec:mec + 1], 1.0, -3.0e38,
                                   ALU.max, accum_out=bb[un][:, 0:1]),
                               reads=[b_pq[sb_], CONST],
                               writes=[b_sm[un], b_bb[un]])
                        p.emit("V", lambda h, un=un, sl4=sl4, hh=hh:
                               h.scalar_tensor_tensor(
                                   bb[un][:, 1:2], bb[un][:, 0:1], -1.0,
                                   rr[sl4][:, hh:hh + 1], ALU.mult,
                                   ALU.mult),
                               reads=[b_bb[un], b_rr[sl4]],
                               writes=[b_bb[un]])

                    def st_B(hh):
                        un = hh % 3
                        p.emit("S", lambda h, un=un, nk=nk, sl4=sl4, hh=hh:
                               h.activation(
                                   eE[un][:, 0:nk * 128],
                                   sm[un][:, 0:nk * 128], AF.Exp,
                                   bias=bb[un][:, 1:2],
                                   scale=rr[sl4][:, hh:hh + 1],
                                   accum_out=bb[un][:, 2:3]),
                               reads=[b_sm[un], b_bb[un], b_rr[sl4]],
                               writes=[b_eE[un], b_bb[un]])

                    def st_C(hh):
                        un = hh % 3
                        p.emit("V", lambda h, un=un: h.reciprocal(
                            bb[un][:, 3:4], bb[un][:, 2:3]),
                            reads=[b_bb[un]], writes=[b_bb[un]])
                        p.emit("V", lambda h, un=un, nk=nk: h.tensor_scalar(
                            eN[un][:, 0:nk * 128], eE[un][:, 0:nk * 128],
                            bb[un][:, 3:4], None, ALU.mult),
                            reads=[b_eE[un], b_bb[un]], writes=[b_eN[un]])

                    def st_D1(hh):
                        un = hh % 3
                        tb = 2 + (hh % 2)
                        fns = [(lambda h, j=j, un=un, tb=tb, nk=nk:
                                h.matmul(
                                    pqb[tb][:, ts(j, 128)],
                                    eN[un][:, ts(j, 128)], c_idb[:],
                                    is_transpose=True, start=(j == 0),
                                    stop=(j == nk - 1))) for j in range(nk)]
                        p.emit("T", fns, reads=[b_eN[un], CONST],
                               writes=[b_pq[tb]])

                    def st_EV(hh):
                        un = hh % 3
                        tb = 2 + (hh % 2)
                        p.emit("S", lambda h, un=un, nk=nk, tb=tb:
                               h.copy(eT[un][:, 0:nk * 128],
                                      pqb[tb][:, 0:nk * 128]),
                               reads=[b_pq[tb]], writes=[b_eT[un]])

                    def st_D2t(hh):
                        un = hh % 3
                        ob = 6 + ((hh // 2) % 2)
                        hrow = (hh % 2) * 64
                        ns_ = len(slots)
                        fns = [(lambda h, j=j, slot=slot, ob=ob, un=un,
                                hrow=hrow, hh=hh, ns_=ns_: h.matmul(
                                    pq[ob][hrow:hrow + 64, 0:128],
                                    vb[slot % 6][:, ts(hh, 64)],
                                    eT[un][:, ts(j, 128)],
                                    start=(j == 0),
                                    stop=(j == ns_ - 1)))
                               for j, slot in enumerate(slots)]
                        vb_reads = [b_vb[slot % 6] for slot in slots]
                        p.emit("T", fns, reads=[b_eT[un]] + vb_reads,
                               writes=[b_pq[ob]])

                    def st_D2v(hh):
                        ob = 6 + ((hh // 2) % 2)
                        hcol = hh // 2
                        if hh % 2 == 1:
                            p.emit("V", lambda h, ob=ob, hcol=hcol,
                                   tile=tile, aotw=aotw: h.tensor_copy(
                                       aotw[:, hcol * 256 + tile * 128:
                                            hcol * 256 + tile * 128 + 128],
                                       pq[ob][:, 0:128]),
                                   reads=[b_pq[ob]], writes=[baotw])

                    unit_qk(0)
                    unit_qk(1)
                    st_A(0)
                    st_B(0)
                    for u in range(20):
                        if 0 <= u - 4:
                            st_D2t(u - 4)
                        if u + 2 < 16:
                            unit_qk(u + 2)
                        if 0 <= u - 2 <= 15:
                            st_D1(u - 2)
                        if 0 <= u - 3 <= 15:
                            st_EV(u - 3)
                        if u + 1 < 16:
                            st_A(u + 1)
                            st_B(u + 1)
                        if u <= 15:
                            st_C(u)
                        if 0 <= u - 4:
                            st_D2v(u - 4)
                for mp in range(4):
                    if feeder is not None:
                        next(feeder, None)
                    bank = mp % 2
                    fns = []
                    for mi in range(2):
                        m = 2 * mp + mi
                        for kc in range(8):
                            fns.append((lambda h, m=m, kc=kc, mi=mi,
                                        bank=bank, aotw=aotw: h.matmul(
                                            pq[bank][:, mi * 256:
                                                     mi * 256 + 256],
                                            w_o[:, kc * 1024 + m * 128:
                                                kc * 1024 + (m + 1) * 128],
                                            aotw[:, ts(kc, 256)],
                                            start=(mi == 0 and kc == 0),
                                            stop=(mi == 1 and kc == 7))))
                    p.emit("T", fns, reads=[baotw, b_wo],
                           writes=[b_pq[bank]])
                    for mi in range(2):
                        m = 2 * mp + mi
                        p.emit("V", lambda h, m=m, mi=mi, bank=bank,
                               xrw=xrw: h.tensor_tensor(
                                   xrw[:, ts(m, 256)],
                                   xrw[:, ts(m, 256)],
                                   pq[bank][:, mi * 256:mi * 256 + 256],
                                   ALU.add),
                               reads=[bxrw, b_pq[bank]], writes=[bxrw])
                p.emit("G", lambda h, b=b, w=w, xrw=xrw: h.dma_start(
                    out=h_resid[b].rearrange("(c p) n -> p c n", p=128)
                    [:, :, ts(w, 256)],
                    in_=xrw[:].rearrange("p (c n) -> p c n", c=8)),
                    reads=[bxrw],
                    writes=[bh_resid[b][2 * w], bh_resid[b][2 * w + 1]])

            bw = [(b, w) for b in range(B) for w in range(NBLK)]
            g0 = a_s1_gen(*bw[0], 0)
            for _ in g0:
                pass
            for wg, (b, w) in enumerate(bw):
                if wg + 1 < len(bw):
                    for _ in a_s1_gen(*bw[wg + 1], wg + 1):
                        pass
                a_s2(b, w, wg, None)
            stP.close()
            p.barrier()

            stF = ExitStack()
            sbF = lambda nm, sh, dt: stF.enter_context(
                nc.sbuf_tensor(f"L{li}f_" + nm, sh, dt))
            w_1 = sbF("w1", [128, 8 * 5460], BF16)
            w_2 = sbF("w2", [128, 22 * 1024], BF16)
            b1p = sbF("b1p", [128, 44], F32)
            b2c = sbF("b2c", [128, 8], F32)
            stSet = ExitStack()
            stage2 = [stSet.enter_context(
                nc.sbuf_tensor(f"L{li}f_stage0", [128, 2730], F32))]
            gcol = stSet.enter_context(
                nc.sbuf_tensor(f"L{li}f_gcol", [128, 8], F32))
            bcolb = stSet.enter_context(
                nc.sbuf_tensor(f"L{li}f_bcolb", [128, 8], BF16))
            bcolf = stSet.enter_context(
                nc.sbuf_tensor(f"L{li}f_bcolf", [128, 8], F32))
            b1c = stSet.enter_context(
                nc.sbuf_tensor(f"L{li}f_b1c", [128, 44], F32))

            b_w1, b_w2 = Buf("w1"), Buf("w2")
            b_st2 = [Buf("st20")]
            b_gcol, b_bcolb, b_bcolf = Buf("gcol"), Buf("bcolb"), Buf("bcolf")
            b_b1c, b_b1p, b_b2c = Buf("b1c"), Buf("b1p"), Buf("b2c")
            b_xr2 = [Buf(f"xr2{i}") for i in range(2)]
            b_halo2 = Buf("halo2")
            b_xs2 = [Buf(f"xs2{i}") for i in range(2)]
            b_sqt = Buf("sqt")
            b_row = Buf("rowtmp")
            b_rowbf = Buf("rowbf")

            
            b_gel = [Buf(f"gel{i}") for i in range(2)]
            b_act = Buf("act")
                        
            p.emit("Y", lambda h, li=li: h.dma_start(
                out=gcol[:], in_=din["gcols"][li]), writes=[b_gcol])
            p.emit("Y", lambda h, li=li: h.dma_start(
                out=bcolf[:], in_=din["bcols"][li]), writes=[b_bcolf])
            p.emit("V", lambda h: h.tensor_copy(bcolb[:], bcolf[:]),
                   reads=[b_bcolf], writes=[b_bcolb])
            p.emit("Y", lambda h, li=li: h.dma_start(
                out=b1c[:], in_=din["b1cols"][li]), writes=[b_b1c])
            p.emit("Y", lambda h, li=li: h.dma_start(
                out=b2c[:], in_=din["b2cols"][li]), writes=[b_b2c])
            wtmp = stSet.enter_context(
                nc.sbuf_tensor(f"L{li}f_wtmp", [128, 2730], BF16))
            b_wtmp = Buf("wtmp")
            for dc in range(8):
                for hf in range(2):
                    sg2, bsg2 = stage2[0], b_st2[0]
                    p.emit("Y", lambda h, sg2=sg2, li=li, dc=dc, hf=hf:
                           h.dma_start(out=sg2[:],
                                       in_=din["W1"][li, ts(dc, 128),
                                                     hf * 2730:(hf + 1) * 2730]),
                           writes=[bsg2])
                    p.emit("V", lambda h, sg2=sg2, dc=dc, hf=hf:
                           h.tensor_scalar(
                               w_1[:, dc * 5460 + hf * 2730:
                                   dc * 5460 + (hf + 1) * 2730],
                               sg2[:], gcol[:, dc:dc + 1], None, ALU.mult),
                           reads=[bsg2, b_gcol], writes=[b_w1])
                    p.emit("V", lambda h, sg2=sg2: h.tensor_copy(
                        wtmp[:], sg2[:]), reads=[bsg2], writes=[b_wtmp])
                    fns = []
                    jorder = [0, 21] + list(range(1, 21))
                    for pos, j in enumerate(jorder):
                        wdt = 42 if j == 21 else 128
                        fns.append((lambda h, j=j, wdt=wdt, dc=dc, hf=hf,
                                    pos=pos: h.matmul(
                                        pq[2][0:wdt, hf * 22 + j:
                                              hf * 22 + j + 1],
                                        wtmp[:, j * 128:j * 128 + wdt],
                                        bcolb[:, dc:dc + 1],
                                        start=(dc == 0 and hf == 0 and
                                               pos == 0),
                                        stop=(dc == 7 and hf == 1 and
                                              pos == 21))))
                    p.emit("T", fns, reads=[b_wtmp, b_bcolb],
                           writes=[b_pq[2]])
            p.emit("V", lambda h: h.tensor_tensor(
                b1p[:], pq[2][:, 0:44], b1c[:], ALU.add),
                reads=[b_pq[2], b_b1c], writes=[b_b1p])
            p.emit("V", lambda h: h.memset(
                w_2[:, 21 * 1024:22 * 1024], 0.0), writes=[b_w2])
            for kc in range(22):
                rows = 42 if kc == 21 else 128
                sg2, bsg2 = stage2[0], b_st2[0]
                p.emit("Y", lambda h, sg2=sg2, li=li, kc=kc, rows=rows:
                       h.dma_start(out=sg2[0:rows, 0:1024],
                                   in_=din["W2"][li, kc * 128:kc * 128 + rows, :]),
                       writes=[bsg2])
                p.emit("V", lambda h, sg2=sg2, kc=kc, rows=rows: h.tensor_copy(
                    w_2[0:rows, kc * 1024:(kc + 1) * 1024],
                    sg2[0:rows, 0:1024]), reads=[bsg2], writes=[b_w2])

            stSet.close()
            p.barrier()
            xr2 = [sbF(f"xr2{i}", [128, 8 * 512], F32) for i in range(2)]
            halo2 = sbF("halo2", [128, 4], F32)
            xs2 = [sbF(f"xs2{i}", [128, 8 * 512], BF16) for i in range(2)]
            sqt = sbF("sqt", [128, 512], BF16)
            rowtmp = sbF("rowtmp", [1, 512], F32)
            rowbf = sbF("rowbf", [1, 1024], BF16)

            gel = [sbF(f"gel{i}", [128, 512], BF16) for i in range(2)]
            act = sbF("act", [128, 22 * 512], BF16)
            p.emit("V", lambda h: h.memset(act[:, 21 * 512:22 * 512], 0.0),
                   writes=[b_act])
            def f_s1_gen(b, tt, tg):
                xrw, bxrw = xr2[tg % 2], b_xr2[tg % 2]
                xsw, bxsw = xs2[tg % 2], b_xs2[tg % 2]
                rblk = [bh_resid[b][4 * tt + i] for i in range(4)]
                p.emit("Y", lambda h, b=b, tt=tt, xrw=xrw: h.dma_start(
                    out=xrw[:].rearrange("p (c n) -> p c n", c=8),
                    in_=h_resid[b].rearrange("(c p) n -> p c n", p=128)
                    [:, :, ts(tt, 512)]),
                    reads=rblk, writes=[bxrw])
                p.emit("V", lambda h, xsw=xsw, xrw=xrw: h.tensor_copy(
                    xsw[:].rearrange("p (c n) -> p c n", c=8)[:, 0:4, :],
                    xrw[:].rearrange("p (c n) -> p c n", c=8)[:, 0:4, :]),
                    reads=[bxrw], writes=[bxsw])
                p.emit("V", lambda h, xsw=xsw, xrw=xrw: h.tensor_copy(
                    xsw[:].rearrange("p (c n) -> p c n", c=8)[:, 4:8, 1:512],
                    xrw[:].rearrange("p (c n) -> p c n", c=8)[:, 4:8, 0:511]),
                    reads=[bxrw], writes=[bxsw])
                if tt == 0:
                    p.emit("V", lambda h, xsw=xsw: h.memset(
                        xsw[:].rearrange("p (c n) -> p c n", c=8)
                        [:, 4:8, 0:1], 0.0), writes=[bxsw])
                else:
                    p.emit("V", lambda h, xsw=xsw: h.tensor_copy(
                        xsw[:].rearrange("p (c n) -> p c n", c=8)
                        [:, 4:8, 0:1],
                        halo2[:].rearrange("p (c n) -> p c n", c=4)),
                        reads=[b_halo2], writes=[bxsw])
                p.emit("V", lambda h, xrw=xrw: h.tensor_copy(
                    halo2[:].rearrange("p (c n) -> p c n", c=4),
                    xrw[:].rearrange("p (c n) -> p c n", c=8)
                    [:, 4:8, 511:512]),
                    reads=[bxrw], writes=[b_halo2])
                yield
                fns = [(lambda h, xsw=xsw, dc=dc: h.matmul(
                    pq[0][0:1, :], c_ones_col[:],
                    xsw[:, ts(dc, 512)], start=(dc == 0), stop=(dc == 7)))
                       for dc in range(8)]
                p.emit("T", fns, reads=[bxsw, CONST], writes=[b_pq[0]])
                for dc in range(8):
                    p.emit("V", lambda h, xsw=xsw, dc=dc: h.tensor_tensor(
                        sqt[:], xsw[:, ts(dc, 512)], xsw[:, ts(dc, 512)],
                        ALU.mult), reads=[bxsw], writes=[b_sqt])
                    p.emit("T", lambda h, dc=dc: h.matmul(
                        pq[1][0:1, :], c_ones_col[:], sqt[:],
                        start=(dc == 0), stop=(dc == 7)),
                        reads=[b_sqt, CONST], writes=[b_pq[1]])
                    if dc % 2 == 1:
                        yield
                p.emit("V", lambda h: h.tensor_scalar(
                    pq[0][32:33, :], pq[0][0:1, :], 1.0 / D, None,
                    ALU.mult), reads=[b_pq[0]], writes=[b_pq[0]])
                p.emit("V", lambda h: h.tensor_scalar(
                    pq[1][32:33, :], pq[1][0:1, :], 1.0 / D, 1e-5,
                    ALU.mult, ALU.add), reads=[b_pq[1]], writes=[b_pq[1]])
                p.emit("S", lambda h: h.activation(
                    rowtmp[:], pq[0][32:33, :], AF.Square),
                    reads=[b_pq[0]], writes=[b_row])
                p.emit("V", lambda h: h.tensor_tensor(
                    pq[1][64:65, :], pq[1][32:33, :], rowtmp[:],
                    ALU.subtract), reads=[b_pq[1], b_row], writes=[b_pq[1]])
                p.emit("S", lambda h: h.activation(
                    pq[0][64:65, :], pq[1][64:65, :], AF.Sqrt),
                    reads=[b_pq[1]], writes=[b_pq[0]])
                p.emit("V", lambda h: h.reciprocal(
                    rowtmp[:], pq[0][64:65, :]),
                    reads=[b_pq[0]], writes=[b_row])
                p.emit("V", lambda h: h.tensor_copy(
                    rowbf[:, 0:512], rowtmp[:]),
                    reads=[b_row], writes=[b_rowbf])
                p.emit("V", lambda h: h.scalar_tensor_tensor(
                    rowbf[:, 512:1024], pq[0][32:33, :], -1.0,
                    rowtmp[:], ALU.mult, ALU.mult),
                    reads=[b_pq[0], b_row], writes=[b_rowbf])
                yield
                p.emit("T", lambda h: h.matmul(
                    pq[0][:], c_ones_row[:], rowbf[:, 0:512],
                    start=True, stop=True),
                    reads=[b_rowbf, CONST], writes=[b_pq[0]])
                p.emit("T", lambda h: h.matmul(
                    pq[1][:], c_ones_row[:], rowbf[:, 512:1024],
                    start=True, stop=True),
                    reads=[b_rowbf, CONST], writes=[b_pq[1]])
                for dc in range(8):
                    p.emit("V", lambda h, xsw=xsw, dc=dc: h.tensor_tensor(
                        sqt[:], xsw[:, ts(dc, 512)], pq[0][:], ALU.mult),
                        reads=[bxsw, b_pq[0]], writes=[b_sqt])
                    p.emit("V", lambda h, xsw=xsw, dc=dc: h.tensor_tensor(
                        xsw[:, ts(dc, 512)], sqt[:], pq[1][:], ALU.add),
                        reads=[b_sqt, b_pq[1]], writes=[bxsw])
                    yield

            def f_s2(b, tt, tg, feeder):
                xrw, bxrw = xr2[tg % 2], b_xr2[tg % 2]
                xsw, bxsw = xs2[tg % 2], b_xs2[tg % 2]
                rblk = [bh_resid[b][4 * tt + i] for i in range(4)]
                zz, b_zz = xsw, bxsw

                def fj_mm(j):
                    wdt = 42 if j == 21 else 128
                    pa = 2 + 2 * (j % 3)
                    pg = 3 + 2 * (j % 3)
                    fns = [(lambda h, dc=dc, j=j, wdt=wdt, pa=pa, zz=zz:
                            h.matmul(
                                pq[pa][0:wdt, :],
                                w_1[:, dc * 5460 + j * 128:
                                    dc * 5460 + j * 128 + wdt],
                                zz[:, ts(dc, 512)],
                                start=(dc == 0), stop=(dc == 7)))
                           for dc in range(8)]
                    p.emit("T", fns, reads=[b_zz, b_w1], writes=[b_pq[pa]])
                    fns = [(lambda h, dc=dc, j=j, wdt=wdt, pg=pg, zz=zz:
                            h.matmul(
                                pq[pg][0:wdt, :],
                                w_1[:, dc * 5460 + 2730 + j * 128:
                                    dc * 5460 + 2730 + j * 128 + wdt],
                                zz[:, ts(dc, 512)],
                                start=(dc == 0), stop=(dc == 7)))
                           for dc in range(8)]
                    p.emit("T", fns, reads=[b_zz, b_w1], writes=[b_pq[pg]])

                def fj_gelu(j):
                    wdt = 42 if j == 21 else 128
                    pg = 3 + 2 * (j % 3)
                    p.emit("S", lambda h, j=j, wdt=wdt, pg=pg: h.activation(
                        gel[j % 2][0:wdt, :],
                        pq[pg][0:wdt, :], AF.Gelu,
                        bias=b1p[0:wdt, 22 + j:23 + j]),
                        reads=[b_pq[pg], b_b1p], writes=[b_gel[j % 2]])

                def fj_stt(j):
                    wdt = 42 if j == 21 else 128
                    pa = 2 + 2 * (j % 3)
                    p.emit("V", lambda h, j=j, wdt=wdt, pa=pa:
                           h.scalar_tensor_tensor(
                               act[0:wdt, ts(j, 512)], pq[pa][0:wdt, :],
                               b1p[0:wdt, j:j + 1], gel[j % 2][0:wdt, :],
                               ALU.add, ALU.mult),
                           reads=[b_pq[pa], b_b1p, b_gel[j % 2]],
                           writes=[b_act])

                for j in range(24):
                    if feeder is not None:
                        next(feeder, None)
                    if j >= 2:
                        fj_stt(j - 2)
                    if 1 <= j <= 22:
                        fj_gelu(j - 1)
                    if j < 22:
                        fj_mm(j)
                for wave in range(2):
                    for mi in range(4):
                        m = wave * 4 + mi
                        bank = 4 + mi
                        fns = []
                        for kc in range(22):
                            wdt = 42 if kc == 21 else 128
                            fns.append((lambda h, m=m, kc=kc, wdt=wdt,
                                        bank=bank: h.matmul(
                                            pq[bank][:],
                                            w_2[0:wdt, kc * 1024 +
                                                m * 128:kc * 1024 +
                                                (m + 1) * 128],
                                            act[0:wdt, ts(kc, 512)],
                                            start=(kc == 0),
                                            stop=(kc == 21))))
                        p.emit("T", fns, reads=[b_act, b_w2],
                               writes=[b_pq[bank]])
                        p.emit("V", lambda h, m=m, bank=bank, xrw=xrw:
                               h.scalar_tensor_tensor(
                                   xrw[:, ts(m, 512)], pq[bank][:],
                                   b2c[:, m:m + 1], xrw[:, ts(m, 512)],
                                   ALU.add, ALU.add),
                               reads=[b_pq[bank], b_b2c, bxrw],
                               writes=[bxrw])
                if last_layer:
                    for m in range(8):
                        p.emit("S", lambda h, m=m, xrw=xrw: h.activation(
                            xrw[:, ts(m, 512)], xrw[:, ts(m, 512)],
                            AF.Silu), reads=[bxrw], writes=[bxrw])
                    p.emit("G", lambda h, b=b, tt=tt, xrw=xrw: h.dma_start(
                        out=h_g[b].rearrange("(c p) n -> p c n", p=128)
                        [:, :, ts(tt, 512)],
                        in_=xrw[:].rearrange("p (c n) -> p c n", c=8)),
                        reads=[bxrw],
                        writes=[bh_g[b][4 * tt + i] for i in range(4)])
                else:
                    for m in range(8):
                        p.emit("G", lambda h, b=b, tt=tt, m=m, xrw=xrw:
                               h.dma_start(
                                   out=h_resid[b, m * 128:(m + 1) * 128,
                                               ts(tt, 512)],
                                   in_=xrw[:, ts(m, 512)]),
                               reads=[bxrw], writes=rblk)

            btt = [(b, tt) for b in range(B) for tt in range(NFT)]
            g0 = f_s1_gen(*btt[0], 0)
            for _ in g0:
                pass
            for tg, (b, tt) in enumerate(btt):
                if tg + 1 < len(btt):
                    for _ in f_s1_gen(*btt[tg + 1], tg + 1):
                        pass
                f_s2(b, tt, tg, None)

            stF.close()
            p.barrier()

        stC = ExitStack()
        sbC = lambda nm, sh, dt: stC.enter_context(nc.sbuf_tensor(nm, sh, dt))
        c_xc = [sbC(f"c_xc{i}", [128, N], BF16) for i in range(8)]
        c_sq = [sbC("c_sq0", [128, N], BF16)]
        c_norm = sbC("c_norm", [16, N], F32)
        c_negc = sbC("c_negc", [16, N], BF16)
        c_maxn = sbC("c_maxn", [16, 1], F32)
        c_Q = [sbC(f"c_Q{i}", [65, N], BF16) for i in range(2)]
        c_K = [sbC(f"c_K{i}", [65, N], BF16) for i in range(2)]
        c_Vt = [sbC(f"c_Vt{i}", [64, N], BF16) for i in range(2)]
        c_Vb = [sbC(f"c_Vb{i}", [128, (N // 128) * 66], BF16) for i in range(2)]
        c_E = [sbC(f"c_E{i}", [128, 1024], BF16) for i in range(3)]
        c_rden = [sbC(f"c_rden{i}", [1, 512], F32) for i in range(2)]
        c_rdb = [sbC(f"c_rdb{i}", [1, 512], BF16) for i in range(2)]
        c_R = [sbC(f"c_R{i}", [64, 512], F32) for i in range(2)]
        c_sout = [sbC("c_sout0", [64, N], F32)]
        b_xc = [Buf(f"c_xc{i}") for i in range(8)]
        b_sq = [Buf("c_sq0")]
        b_norm = Buf("c_norm")
        b_negc = Buf("c_negc")
        b_maxn = Buf("c_maxn")
        b_Q = [Buf(f"c_Q{i}") for i in range(2)]
        b_K = [Buf(f"c_K{i}") for i in range(2)]
        b_Vt = [Buf(f"c_Vt{i}") for i in range(2)]
        b_Vb = [Buf(f"c_Vb{i}") for i in range(2)]
        b_E = [Buf(f"c_E{i}") for i in range(3)]
        b_rden = [Buf(f"c_rden{i}") for i in range(2)]
        b_rdb = [Buf(f"c_rdb{i}") for i in range(2)]
        b_R = [Buf(f"c_R{i}") for i in range(2)]
        b_sout = [Buf("c_sout0")]

        for i in range(2 if "C" in phases else 0):
            p.emit("V", lambda h, i=i: h.memset(c_K[i][64:65, :], 1.0),
                   writes=[b_K[i]])
            p.emit("V", lambda h, i=i: h.memset(
                c_Vb[i][:].rearrange("p (k o) -> p k o", k=N // 128)
                [:, :, 64:66], 1.0), writes=[b_Vb[i]])

        for b in range(B if "C" in phases else 0):
            NG = N // 512
            for c in range(8):
                p.emit("Y", lambda h, b=b, c=c: h.dma_start(
                    out=c_xc[c][:], in_=h_xnT[b, ts(c, 128), :]),
                    reads=bh_xnT[b], writes=[b_xc[c]])
                p.emit("V", lambda h, b=b, c=c: h.tensor_scalar(
                    c_xc[c][:], c_xc[c][:],
                    g_negmu[:, b * 8 + c:b * 8 + c + 1], None, ALU.add),
                    reads=[b_xc[c], b_negmu], writes=[b_xc[c]])
                sq, bsq = c_sq[0], b_sq[0]
                p.emit("V", lambda h, sq=sq, c=c: h.tensor_tensor(
                    sq[:], c_xc[c][:], c_xc[c][:], ALU.mult),
                    reads=[b_xc[c]], writes=[bsq])
                for g8 in range(NG):
                    p.emit("T", lambda h, sq=sq, c=c, g8=g8: h.matmul(
                        pq[g8][0:16, :], c_indic[:, ts(c, 16)],
                        sq[:, ts(g8, 512)], start=(c == 0), stop=(c == 7)),
                        reads=[bsq, CONST], writes=[b_pq[g8]])
            for g8 in range(NG):
                p.emit("S", lambda h, g8=g8: h.activation(
                    c_norm[:, ts(g8, 512)], pq[g8][0:16, :], AF.Sqrt),
                    reads=[b_pq[g8]], writes=[b_norm])
            p.emit("V", lambda h: h.reduce_max(c_maxn[:], c_norm[:], AX.X),
                   reads=[b_norm], writes=[b_maxn])
            p.emit("V", lambda h: h.tensor_scalar(
                c_maxn[:], c_maxn[:], -1.0, None, ALU.mult),
                reads=[b_maxn], writes=[b_maxn])
            p.emit("V", lambda h: h.tensor_scalar(
                c_negc[:], c_norm[:], c_maxn[:], None, ALU.mult),
                reads=[b_norm, b_maxn], writes=[b_negc])

            for hh in range(H):
                u = b * H + hh
                sl = u % 2
                Q, K_, Vt, Vb = c_Q[sl], c_K[sl], c_Vt[sl], c_Vb[sl]
                bQ, bK, bVt, bVb = b_Q[sl], b_K[sl], b_Vt[sl], b_Vb[sl]
                cchunk, half = hh // 2, hh % 2
                p.emit("V", lambda h, Q=Q, cc=cchunk, hf=half: h.tensor_copy(
                    Q[0:64, :], c_xc[cc][ts(hf, 64), :]),
                    reads=[b_xc[cchunk]], writes=[bQ])
                p.emit("G", lambda h, Q=Q, hh=hh: h.dma_start(
                    out=Q[64:65, :], in_=c_negc[hh:hh + 1, :]),
                    reads=[b_negc], writes=[bQ])
                p.emit("V", lambda h, K_=K_, cc=cchunk, hf=half: h.tensor_copy(
                    K_[0:64, :], c_xc[cc][ts(hf, 64), :]),
                    reads=[b_xc[cchunk]], writes=[bK])
                p.emit("Y", lambda h, Vt=Vt, b=b, hh=hh: h.dma_start(
                    out=Vt[:], in_=h_xnT[b, ts(hh, 64), :]),
                    reads=bh_xnT[b], writes=[bVt])
                NW = max(1, (N // 128) // 16)
                for wv in range(NW):
                    nj = min(16, N // 128)
                    fns = [(lambda h, Vt=Vt, j=j, kc=wv * 16 + j, nj=nj: h.matmul(
                        pqb[2][:, ts(j, 64)], Vt[:, ts(kc, 128)],
                        c_idb[0:64, 0:64], is_transpose=True,
                        start=(j == 0), stop=(j == nj - 1)))
                           for j in range(nj)]
                    p.emit("T", fns, reads=[bVt, CONST], writes=[b_pq[2]])
                    p.emit("V", lambda h, Vb=Vb, wv=wv, nj=nj: h.tensor_copy(
                        Vb[:].rearrange("p (k o) -> p k o", k=N // 128)
                        [:, wv * 16:wv * 16 + nj, 0:64],
                        pqb[2][:].rearrange("p (k o) -> p k o", k=16)
                        [:, 0:nj, :]),
                        reads=[b_pq[2]], writes=[bVb])
                so, bso = c_sout[0], b_sout[0]
                pend = []

                def drain_norm():
                    while pend:
                        pend.pop(0)()

                for qb in range(N // 512):
                    ob = 6 + (qb % 2)
                    NKC = N // 128
                    SPANS = [pq45, pq01, pq23]
                    SPB = [4, 0, 2]

                    def emit_qk(kc, Q=Q, K_=K_, qb=qb):
                        spq = SPB[(kc // 2) % 3] + (kc % 2)
                        p.emit("T", lambda h, Q=Q, K_=K_, qb=qb, kc=kc,
                               spq=spq: h.matmul(
                                   pq[spq][:], K_[:, ts(kc, 128)],
                                   Q[:, ts(qb, 512)], start=True, stop=True),
                               reads=[bK, bQ], writes=[b_pq[spq]])

                    def emit_exp2(kc0, NKC=NKC):
                        sp = SPB[(kc0 // 2) % 3]
                        span = SPANS[(kc0 // 2) % 3]
                        E, bE = c_E[(kc0 // 2) % 3], b_E[(kc0 // 2) % 3]
                        p.emit("S", lambda h, E=E, span=span: h.activation(
                            E[:], span, AF.Exp),
                            reads=[b_pq[sp], b_pq[sp + 1]], writes=[bE])

                    def emit_av(kc, Vb=Vb, bVb=bVb, ob=ob, NKC=NKC):
                        E, bE = c_E[(kc // 2) % 3], b_E[(kc // 2) % 3]
                        half = (kc % 2) * 512
                        p.emit("T", lambda h, E=E, Vb=Vb, kc=kc, ob=ob,
                               half=half: h.matmul(
                                   pq[ob][0:65, :],
                                   Vb[:, kc * 66:kc * 66 + 65],
                                   E[:, half:half + 512],
                                   start=(kc == 0),
                                   stop=(kc == NKC - 1)),
                               reads=[bE, bVb], writes=[b_pq[ob]])

                    for kc in range(4):
                        emit_qk(kc)
                    drain_norm()
                    for kp in range(0, NKC, 2):
                        if kp + 4 < NKC:
                            emit_qk(kp + 4)
                        if kp + 5 < NKC:
                            emit_qk(kp + 5)
                        emit_exp2(kp)
                        if kp >= 2:
                            emit_av(kp - 2)
                            emit_av(kp - 1)
                    emit_av(NKC - 2)
                    emit_av(NKC - 1)

                    def norm_qb(qb=qb, ob=ob):
                        rs = qb % 2
                        p.emit("V", lambda h, rs=rs, ob=ob: h.reciprocal(
                            c_rden[rs][:], pq[ob][64:65, :]),
                            reads=[b_pq[ob]], writes=[b_rden[rs]])
                        p.emit("V", lambda h, rs=rs: h.tensor_copy(
                            c_rdb[rs][:], c_rden[rs][:]),
                            reads=[b_rden[rs]], writes=[b_rdb[rs]])
                        p.emit("T", lambda h, rs=rs, ob=ob: h.matmul(
                            pq[ob][64:128, :], c_ones_row[0:1, 0:64],
                            c_rdb[rs][:], start=True, stop=True),
                            reads=[b_rdb[rs], CONST], writes=[b_pq[ob]])
                        p.emit("S", lambda h, rs=rs, ob=ob: h.copy(
                            c_R[rs][:], pq[ob][64:128, :]),
                            reads=[b_pq[ob]], writes=[b_R[rs]])
                        p.emit("V", lambda h, so=so, qb=qb, rs=rs, ob=ob:
                               h.tensor_tensor(so[:, ts(qb, 512)],
                                               pq[ob][0:64, :], c_R[rs][:],
                                               ALU.mult),
                               reads=[b_pq[ob], b_R[rs]], writes=[bso])

                    pend.append(norm_qb)
                drain_norm()
                p.emit("G", lambda h, so=so, b=b, hh=hh: h.dma_start(
                    out=h_s[b, ts(hh, 64), :], in_=so[:]),
                    reads=[bso], writes=[bh_s[b]])

        stC.close()
        p.barrier()

        stD = ExitStack()
        sbD = lambda nm, sh, dt: stD.enter_context(nc.sbuf_tensor(nm, sh, dt))
        d_x = [sbD(f"d_x{i}", [128, D], F32) for i in range(2)]
        d_s = [sbD(f"d_s{i}", [128, D], F32) for i in range(2)]
        d_g = [sbD(f"d_g{i}", [128, D], F32) for i in range(2)]
        d_f = [sbD(f"d_f{i}", [128, D], BF16) for i in range(2)]
        d_o = [sbD(f"d_o{i}", [128, D], F32) for i in range(2)]
        b_dx = [Buf(f"d_x{i}") for i in range(2)]
        b_ds = [Buf(f"d_s{i}") for i in range(2)]
        b_dg = [Buf(f"d_g{i}") for i in range(2)]
        b_df = [Buf(f"d_f{i}") for i in range(2)]
        b_do = [Buf(f"d_o{i}") for i in range(2)]

        for b in range(B if "D" in phases else 0):
            for t in range(NT):
                u = b * NT + t
                sl = u % 2
                p.emit("Y", lambda h, b=b, t=t, sl=sl: h.dma_start(
                    out=d_x[sl][:], in_=din["x"][b, ts(t, 128), :]),
                    writes=[b_dx[sl]])
                p.emit("Y", lambda h, b=b, t=t, sl=sl: h.dma_start(
                    out=d_s[sl][:].rearrange("p (c n) -> p c n", c=8),
                    in_=h_s[b].rearrange("(c p) n -> p c n", p=128)
                    [:, :, ts(t, 128)]),
                    reads=[bh_s[b]], writes=[b_ds[sl]])
                p.emit("Y", lambda h, b=b, t=t, sl=sl: h.dma_start(
                    out=d_g[sl][:].rearrange("p (c n) -> p c n", c=8),
                    in_=h_g[b].rearrange("(c p) n -> p c n", p=128)
                    [:, :, ts(t, 128)]),
                    reads=[bh_g[b][t]], writes=[b_dg[sl]])
                p.emit("V", lambda h, sl=sl: h.tensor_tensor(
                    d_f[sl][:], d_s[sl][:], d_g[sl][:], ALU.mult),
                    reads=[b_ds[sl], b_dg[sl]], writes=[b_df[sl]])
                fns = [(lambda h, sl=sl, j=j: h.matmul(
                    pqb[1][:, ts(j, 128)], d_f[sl][:, ts(j, 128)], c_idb[:],
                    is_transpose=True, start=(j == 0), stop=(j == 7)))
                       for j in range(8)]
                p.emit("T", fns, reads=[b_df[sl], CONST], writes=[b_pq[1]])
                p.emit("V", lambda h, sl=sl: h.tensor_tensor(
                    d_o[sl][:], d_x[sl][:], pqb[1][:], ALU.add),
                    reads=[b_dx[sl], b_pq[1]], writes=[b_do[sl]])
                p.emit("G", lambda h, b=b, t=t, sl=sl: h.dma_start(
                    out=d_out[b, ts(t, 128), :], in_=d_o[sl][:]),
                    reads=[b_do[sl]])

        stD.close()
        global LAST_PROG
        LAST_PROG = p
        p.finalize(nc, st)
    return nc


def kernel(**inputs):
    consts = host_consts()
    lay = host_layouts(inputs)
    x = np.ascontiguousarray(np.asarray(inputs["L0_attn_x"], np.float32))
    wmap = dict(
        Wqkv=np.ascontiguousarray(np.asarray(inputs["L0_ffn_Wqkv"], np.float32)),
        Wo=np.ascontiguousarray(np.asarray(inputs["L0_ffn_Wo"], np.float32)),
        W1=np.ascontiguousarray(np.asarray(inputs["L0_ffn_W1"], np.float32)),
        W2=np.ascontiguousarray(np.asarray(inputs["L0_ffn_W2"], np.float32)),
        qscale=np.ascontiguousarray(np.asarray(inputs["L0_ffn_q_scale"], np.float32)),
        kscale=np.ascontiguousarray(np.asarray(inputs["L0_ffn_k_scale"], np.float32)),
        **lay,
    )
    nc = build_program()
    lower_extended_insts(nc)
    in_maps = []
    for cidx in range(NCORES):
        m = dict(x=np.ascontiguousarray(x[2 * cidx:2 * cidx + 2]))
        m.update(wmap)
        m.update(consts)
        in_maps.append(m)
    res = run_bass_kernel_spmd(nc, in_maps, core_ids=list(range(NCORES)))
    return np.concatenate([np.asarray(r["out"]) for r in res.results], axis=0)


if __name__ == "__main__":
    nc = build_program()
    n = len(nc.all_instructions() if hasattr(nc, "all_instructions") else [])
    print("built")
